# version 08
# baseline (speedup 1.0000x reference)
import numpy as np
import ml_dtypes
from contextlib import ExitStack
import concourse.bass as bass
import concourse.mybir as mybir
from concourse.bass_utils import run_bass_kernel_spmd

F32 = mybir.dt.float32
BF16 = mybir.dt.bfloat16
F16 = mybir.dt.float16
I32 = mybir.dt.int32
U8 = mybir.dt.uint8
U16 = mybir.dt.uint16
AF = mybir.ActivationFunctionType
ALU = mybir.AluOpType
BF = ml_dtypes.bfloat16

L = 2048
DM = 2048
NK = 16
NS = 8
NLH = 2
LHALF = L // NLH
HID = 5632
NT = HID // 128
LB = 1024
NB = L // LB


class Ctx:
    pass


def _barrier(nc, name):
    ph = nc._stk.enter_context(nc.semaphore(name))
    with nc.Block() as blk:
        def mk(e):
            e.drain().then_inc(ph, 1)
            e.wait_ge(ph, 5)
        blk.sync(mk)
        blk.scalar(mk)
        blk.vector(mk)
        blk.tensor(mk)
        blk.gpsimd(mk)
    return ph


def emit_p1(nc, cx):
    wx = nc.dram_tensor("wx", [128, NK * 272], F32, kind="ExternalInput").ap()
    wdt = nc.dram_tensor("wdt", [128, 2 * DM], F32, kind="ExternalInput").ap()
    bdt_d = nc.dram_tensor("bdt", [128, NK], F32, kind="ExternalInput").ap()
    alog_d = nc.dram_tensor("alog", [128, NK * NS], F32, kind="ExternalInput").ap()
    dpar_d = nc.dram_tensor("dpar", [128, NK], F32, kind="ExternalInput").ap()
    sel_d = nc.dram_tensor("sel", [16, 2048], BF16, kind="ExternalInput").ap()
    xt = cx.xt
    onesc = cx.onesc
    onesr = cx.onesr
    cvals = cx.cvals
    with ExitStack() as stk:
        def sb(name, shape, dtype):
            return stk.enter_context(nc.sbuf_tensor(name, shape, dtype))

        def sem(name):
            return stk.enter_context(nc.semaphore(name))

        wx_s = sb("wx_s", [128, NK * 272], BF16)
        wdt_s = sb("wdt_s", [128, 2 * DM], BF16)
        bdt_s = sb("bdt_s", [128, NK], F32)
        alog_s = sb("alog_s", [128, NK * NS], F32)
        negA = sb("negA", [128, NK * NS], F32)
        dpar_s = sb("dpar_s", [128, NK], F32)
        dp1 = sb("dp1", [128, NK], F32)
        sel_s = sb("sel_s", [16, 2048], BF16)
        dtr = sb("dtr", [128, 2 * L], BF16)
        bc16 = sb("bc16", [16, L], BF16)
        bcb = sb("bcb", [128, NS * LHALF], BF16)
        ccb = sb("ccb", [128, NS * LHALF], BF16)
        zz = sb("zz", [128, LHALF], F32)
        sp1 = sb("sp1", [128, LHALF], F32)
        sp2 = sb("sp2", [128, LHALF], F32)
        dlt = sb("dlt", [128, LHALF], F32)
        uu = sb("uu", [128, LHALF], BF16)
        dbu = sb("dbu", [128, LHALF], BF16)
        hh = sb("hh", [128, LHALF], BF16)
        tmp = sb("tmp", [128, LHALF], BF16)
        yy = sb("yy", [128, LHALF], BF16)
        da0 = sb("da0", [128, LHALF], BF16)
        da1 = sb("da1", [128, LHALF], BF16)
        state = sb("state", [128, 128], F32)
        sq = sb("sq", [128, L], BF16)
        meanr = sb("meanr", [1, L], F32)
        varr = sb("varr", [1, L], F32)
        tmpr = sb("tmpr", [1, L], F32)
        psA = stk.enter_context(nc.psum_tensor("psA", [128, 2048], F32))
        psB = stk.enter_context(nc.psum_tensor("psB", [128, 2048], F32))
        ld = sem("p1ld")
        ldg = sem("p1ldg")
        mmA = sem("p1mmA")
        evA = sem("p1evA")
        evBC = sem("p1evBC")
        zk = sem("p1zk")
        sp = sem("p1sp")
        da = sem("p1da")
        hc = sem("p1hc")
        dl = sem("p1dl")
        res = sem("p1res")
        sqs = sem("p1sqs")
        lnk = sem("p1lnk")
        st_ = sem("p1st")
        rst = sem("p1rst")
        app = sem("p1app")
        da_bufs = [da0, da1]

        with nc.Block() as blk:
            @blk.sync
            def _(e):
                e.dma_start(out=bdt_s[:], in_=bdt_d).then_inc(ld, 16)
                e.dma_start(out=alog_s[:], in_=alog_d).then_inc(ld, 16)
                e.dma_start(out=dpar_s[:], in_=dpar_d).then_inc(ld, 16)
                e.dma_start(out=sel_s[:], in_=sel_d).then_inc(ld, 16)

            @blk.gpsimd
            def _(e):
                e.dma_start(out=wx_s[:], in_=wx).then_inc(ldg, 16)
                e.dma_start(out=wdt_s[:], in_=wdt).then_inc(ldg, 16)

            @blk.tensor
            def _(e):
                e.wait_ge(ld, 64)
                e.wait_ge(ldg, 32)
                for c in range(3):
                    mwid = 128 if c < 2 else 16
                    if c >= 1:
                        e.wait_ge(evA, 4 * c)
                    for k in range(NK):
                        for lc in range(4):
                            ins = e.matmul(
                                psA[0:mwid, lc * 512:(lc + 1) * 512],
                                wx_s[:, k * 272 + c * 128: k * 272 + c * 128 + mwid],
                                xt[:, k * L + lc * 512: k * L + (lc + 1) * 512],
                                start=(k == 0), stop=(k == NK - 1),
                            )
                            if k == NK - 1:
                                ins.then_inc(mmA, 1)
                for lh in range(NLH):
                    for n in range(16):
                        idx = lh * 16 + n
                        if idx - 1 >= 1:
                            e.wait_ge(evBC, idx - 1)
                        if idx == 0:
                            e.wait_ge(evA, 12)
                        if n == 0 and lh >= 1:
                            e.wait_ge(zk, lh * NK)
                        ps = psA if (n % 2 == 0) else psB
                        for lc in range(2):
                            ins = e.matmul(
                                ps[:, lc * 512:(lc + 1) * 512],
                                sel_s[:, n * 128:(n + 1) * 128],
                                bc16[:, lh * LHALF + lc * 512: lh * LHALF + (lc + 1) * 512],
                                start=True, stop=True,
                            )
                            if lc == 1:
                                ins.then_inc(mmA, 1)
                    for k in range(NK):
                        gidx = lh * NK + k
                        if gidx == 0:
                            e.wait_ge(evBC, 16)
                        if gidx >= 1:
                            e.wait_ge(zk, gidx)
                        if gidx == NK:
                            e.wait_ge(evBC, 32)
                        for r in range(2):
                            for lc in range(2):
                                ins = e.matmul(
                                    psA[:, lc * 512:(lc + 1) * 512],
                                    wdt_s[:, r * DM + k * 128: r * DM + (k + 1) * 128],
                                    dtr[:, r * L + lh * LHALF + lc * 512: r * L + lh * LHALF + (lc + 1) * 512],
                                    start=(r == 0), stop=(r == 1),
                                )
                                if lc == 1 and r == 1:
                                    ins.then_inc(mmA, 1)
                e.wait_ge(zk, 32)
                for k in range(NK):
                    e.wait_ge(res, NK + k + 1)
                    e.wait_ge(sqs, k + 1)
                    for lc in range(4):
                        ins1 = e.matmul(psA[0:1, lc * 512:(lc + 1) * 512], onesc[:],
                                        xt[:, k * L + lc * 512: k * L + (lc + 1) * 512],
                                        start=(k == 0), stop=(k == NK - 1))
                        ins2 = e.matmul(psB[0:1, lc * 512:(lc + 1) * 512], onesc[:],
                                        sq[:, lc * 512:(lc + 1) * 512],
                                        start=(k == 0), stop=(k == NK - 1))
                        if lc == 3:
                            ins2.then_inc(lnk, 1)
                        if k == NK - 1 and lc == 3:
                            ins1.then_inc(mmA, 1)
                e.wait_ge(rst, 1)
                for lc in range(4):
                    e.matmul(psA[:, lc * 512:(lc + 1) * 512], onesr[:],
                             meanr[:, lc * 512:(lc + 1) * 512], start=True, stop=True)
                    ins = e.matmul(psB[:, lc * 512:(lc + 1) * 512], onesr[:],
                                   varr[:, lc * 512:(lc + 1) * 512], start=True, stop=True)
                    if lc == 3:
                        ins.then_inc(mmA, 1)

            @blk.vector
            def _(e):
                for c in range(3):
                    for lc in range(4):
                        i = c * 4 + lc
                        e.wait_ge(mmA, i + 1)
                        if c < 2:
                            e.tensor_copy(dtr[:, c * L + lc * 512: c * L + (lc + 1) * 512],
                                          psA[:, lc * 512:(lc + 1) * 512]).then_inc(evA, 1)
                        else:
                            e.tensor_copy(bc16[:, lc * 512:(lc + 1) * 512],
                                          psA[0:16, lc * 512:(lc + 1) * 512]).then_inc(evA, 1)
                for lh in range(NLH):
                    for n in range(16):
                        e.wait_ge(mmA, 12 + lh * 32 + n + 1)
                        ps = psA if (n % 2 == 0) else psB
                        dstt = bcb if n < 8 else ccb
                        nn = n % 8
                        e.tensor_copy(dstt[:, nn * LHALF:(nn + 1) * LHALF],
                                      ps[:, 0:1024]).then_inc(evBC, 1)
                    for k in range(NK):
                        gidx = lh * NK + k
                        e.wait_ge(mmA, 12 + lh * 32 + 16 + k + 1)
                        e.tensor_scalar(zz[:], psA[:, 0:1024], bdt_s[:, k:k + 1], None,
                                        ALU.add).then_inc(zk, 1)
                        e.wait_ge(sp, gidx + 1)
                        e.scalar_tensor_tensor(dlt[:], zz[:], 0.0, sp1[:],
                                               ALU.max, ALU.add).then_inc(dl, 1)
                        e.tensor_tensor(uu[:], dlt[:],
                                        xt[:, k * L + lh * LHALF: k * L + lh * LHALF + LHALF],
                                        ALU.mult)
                        for n in range(NS):
                            g = gidx * NS + n
                            e.tensor_tensor(dbu[:], uu[:],
                                            bcb[:, n * LHALF:(n + 1) * LHALF], ALU.mult)
                            e.wait_ge(da, g + 1)
                            init = 0.0 if lh == 0 else state[:, (k * NS + n):(k * NS + n + 1)]
                            e.tensor_tensor_scan(hh[:], da_bufs[n % 2][:], dbu[:], init,
                                                 ALU.mult, ALU.add).then_inc(hc, 1)
                            if n == 0:
                                e.tensor_tensor(yy[:], hh[:],
                                                ccb[:, n * LHALF:(n + 1) * LHALF], ALU.mult)
                            else:
                                e.tensor_tensor(tmp[:], hh[:],
                                                ccb[:, n * LHALF:(n + 1) * LHALF], ALU.mult)
                                e.tensor_tensor(yy[:], yy[:], tmp[:], ALU.add)
                            if lh == 0:
                                e.tensor_copy(state[:, (k * NS + n):(k * NS + n + 1)],
                                              hh[:, LHALF - 1:LHALF])
                        xsl = xt[:, k * L + lh * LHALF: k * L + lh * LHALF + LHALF]
                        e.scalar_tensor_tensor(xsl, xsl, dp1[:, k:k + 1], yy[:],
                                               ALU.mult, ALU.add).then_inc(res, 1)
                e.wait_ge(mmA, 77)
                e.wait_ge(lnk, 16)
                e.tensor_scalar(meanr[:], psA[0:1, :], 1.0 / DM, None, ALU.mult)
                e.tensor_scalar(varr[:], psB[0:1, :], 1.0 / DM, None, ALU.mult)
                e.tensor_tensor(tmpr[:], meanr[:], meanr[:], ALU.mult)
                e.tensor_tensor(varr[:], varr[:], tmpr[:], ALU.subtract).then_inc(st_, 1)
                e.wait_ge(mmA, 78)
                for k in range(NK):
                    xsl = xt[:, k * L:(k + 1) * L]
                    e.tensor_tensor(sq[:], xsl, psA[:, :], ALU.subtract)
                    e.tensor_tensor(xsl, sq[:], psB[:, :], ALU.mult).then_inc(app, 1)

            @blk.scalar
            def _(e):
                e.wait_ge(ld, 64)
                e.wait_ge(ldg, 32)
                e.activation(out=negA[:], in_=alog_s[:], func=AF.Exp)
                e.mul(negA[:], negA[:], -1.0)
                e.add(dp1[:], dpar_s[:], 1.0)
                for lh in range(NLH):
                    for k in range(NK):
                        gidx = lh * NK + k
                        e.wait_ge(zk, gidx + 1)
                        e.activation(out=sp2[:], in_=zz[:], func=AF.Abs)
                        e.activation(out=sp1[:], in_=sp2[:], func=AF.Exp, scale=-1.0)
                        e.activation(out=sp1[:], in_=sp1[:], func=AF.Ln,
                                     bias=1.0).then_inc(sp, 1)
                        e.wait_ge(dl, gidx + 1)
                        for n in range(NS):
                            g = gidx * NS + n
                            if g - 1 >= 1:
                                e.wait_ge(hc, g - 1)
                            e.activation(out=da_bufs[n % 2][:], in_=dlt[:], func=AF.Exp,
                                         scale=negA[:, (k * NS + n):(k * NS + n + 1)]
                                         ).then_inc(da, 1)
                for k in range(NK):
                    e.wait_ge(res, NK + k + 1)
                    if k >= 1:
                        e.wait_ge(lnk, k)
                    e.activation(out=sq[:], in_=xt[:, k * L:(k + 1) * L],
                                 func=AF.Square).then_inc(sqs, 1)
                e.wait_ge(st_, 1)
                e.activation(out=varr[:], in_=varr[:], func=AF.Ln, bias=cvals[0:1, 0:1])
                e.activation(out=varr[:], in_=varr[:], func=AF.Exp,
                             scale=cvals[0:1, 1:2]).then_inc(rst, 1)
        _barrier(nc, "p1bar")


def prep_p1(inputs):
    sel = np.zeros((16, 2048), BF)
    for n in range(16):
        sel[n, n * 128:(n + 1) * 128] = 1.0
    return {
        "wx": np.ascontiguousarray(
            inputs["L0_attn_W_xproj"].T.reshape(NK, 128, 272).transpose(1, 0, 2).reshape(128, NK * 272)),
        "wdt": np.ascontiguousarray(
            inputs["L0_attn_W_dt"].T.reshape(2, 128, DM).transpose(1, 0, 2).reshape(128, 2 * DM)),
        "bdt": np.ascontiguousarray(inputs["L0_attn_b_dt"].reshape(NK, 128).T),
        "alog": np.ascontiguousarray(
            inputs["L0_attn_A_log"].reshape(NK, 128, NS).transpose(1, 0, 2).reshape(128, NK * NS)),
        "dpar": np.ascontiguousarray(inputs["L0_attn_D_param"].reshape(NK, 128).T),
        "sel": sel,
    }


def emit_p2(nc, cx):
    wgT = nc.dram_tensor("wgT", [NT * 128, NK * 128], I32, kind="ExternalInput").ap()
    wuT = nc.dram_tensor("wuT", [NT * 128, NK * 128], I32, kind="ExternalInput").ap()
    wdT = nc.dram_tensor("wdT", [NK * 128, NT * 128], I32, kind="ExternalInput").ap()
    sg_d = nc.dram_tensor("sg", [128, NT], F32, kind="ExternalInput").ap()
    su_d = nc.dram_tensor("su", [128, NT], F32, kind="ExternalInput").ap()
    sd_d = nc.dram_tensor("sd", [128, NK], F32, kind="ExternalInput").ap()
    xt = cx.xt
    onesc = cx.onesc
    onesr = cx.onesr
    cvals = cx.cvals
    with ExitStack() as stk:
        def sb(name, shape, dtype):
            return stk.enter_context(nc.sbuf_tensor(name, shape, dtype))

        def sem(name):
            return stk.enter_context(nc.semaphore(name))

        hact = sb("hact", [128, NT * LB], BF16)
        wg_b = [sb(f"wg{i}", [128, NK * 128], BF16) for i in range(2)]
        wu_b = [sb(f"wu{i}", [128, NK * 128], BF16) for i in range(2)]
        wd_b = [sb("wd0", [128, NT * 128], BF16)]
        gact = sb("gact", [128, LB], BF16)
        uact = sb("uact", [128, LB], BF16)
        sg_s = sb("sg_s", [128, NT], F32)
        su_s = sb("su_s", [128, NT], F32)
        sd_s = sb("sd_s", [128, NK], F32)
        sq2 = sb("sq2", [128, LB], BF16)
        at2 = sb("at2", [128, LB], BF16)
        meanr2 = sb("meanr2", [1, LB], F32)
        varr2 = sb("varr2", [1, LB], F32)
        tmpr2 = sb("tmpr2", [1, LB], F32)
        psG0 = stk.enter_context(nc.psum_tensor("psG0", [128, LB], F32))
        psG1 = stk.enter_context(nc.psum_tensor("psG1", [128, LB], F32))
        psU = stk.enter_context(nc.psum_tensor("psU", [128, LB], F32))
        psD = stk.enter_context(nc.psum_tensor("psD", [128, LB], F32))
        psG = [psG0, psG1]
        psUp = [psU, psD]
        ld2 = sem("p2ld")
        wr2 = sem("p2wr")
        m2 = sem("p2m")
        s2 = sem("p2s")
        e2 = sem("p2e")
        ldd = sem("p2ldd")
        wrd = sem("p2wrd")
        m2d = sem("p2md")
        ev2d = sem("p2evd")
        sq2s = sem("p2sqs")
        lnk2 = sem("p2lnk")
        m2ln = sem("p2mln")
        st2 = sem("p2st")
        rst2 = sem("p2rst")
        bc2 = sem("p2bc")
        app2 = sem("p2app")
        ld0 = sem("p2ld0")

        with nc.Block() as blk:
            @blk.sync
            def _(e):
                e.dma_start(out=sg_s[:], in_=sg_d).then_inc(ld0, 16)
                e.dma_start(out=su_s[:], in_=su_d).then_inc(ld0, 16)
                e.dma_start(out=sd_s[:], in_=sd_d).then_inc(ld0, 16)

            @blk.gpsimd
            def _(e):
                for B in range(NB):
                    for t in range(NT):
                        g = B * NT + t
                        if g >= 2:
                            e.wait_ge(m2, 2 * (g - 2) + 2)
                        e.dma_start(out=wg_b[g % 2][:],
                                    in_=wgT[t * 128:(t + 1) * 128, :]).then_inc(ld2, 16)
                        e.dma_start(out=wu_b[g % 2][:],
                                    in_=wuT[t * 128:(t + 1) * 128, :]).then_inc(ld2, 16)
                    for k in range(NK):
                        kg = B * NK + k
                        if kg >= 1:
                            e.wait_ge(m2d, kg)
                        e.dma_start(out=wd_b[0][:],
                                    in_=wdT[k * 128:(k + 1) * 128, :]).then_inc(ldd, 16)

            @blk.tensor
            def _(e):
                for B in range(NB):
                    for t in range(NT):
                        g = B * NT + t
                        e.wait_ge(wr2, 2 * g + 1)
                        if g >= 2:
                            e.wait_ge(s2, 2 * (g - 2) + 1)
                        if t < 2 and B >= 1:
                            e.wait_ge(app2, B * NK)
                        for k in range(NK):
                            for lc in range(2):
                                ins = e.matmul(
                                    psG[g % 2][:, lc * 512:(lc + 1) * 512],
                                    wg_b[g % 2][:, k * 128:(k + 1) * 128],
                                    xt[:, k * L + B * LB + lc * 512: k * L + B * LB + (lc + 1) * 512],
                                    start=(k == 0), stop=(k == NK - 1))
                                if lc == 1 and k == NK - 1:
                                    ins.then_inc(m2, 1)
                        e.wait_ge(wr2, 2 * g + 2)
                        if g >= 2:
                            e.wait_ge(s2, 2 * (g - 2) + 2)
                        for k in range(NK):
                            for lc in range(2):
                                ins = e.matmul(
                                    psUp[g % 2][:, lc * 512:(lc + 1) * 512],
                                    wu_b[g % 2][:, k * 128:(k + 1) * 128],
                                    xt[:, k * L + B * LB + lc * 512: k * L + B * LB + (lc + 1) * 512],
                                    start=(k == 0), stop=(k == NK - 1))
                                if lc == 1 and k == NK - 1:
                                    ins.then_inc(m2, 1)
                    for k in range(NK):
                        kg = B * NK + k
                        e.wait_ge(wrd, kg + 1)
                        if k == 0:
                            e.wait_ge(e2, (B + 1) * NT)
                        e.wait_ge(ev2d, kg)
                        for kk in range(NT):
                            for lc in range(2):
                                ins = e.matmul(
                                    psD[:, lc * 512:(lc + 1) * 512],
                                    wd_b[0][:, kk * 128:(kk + 1) * 128],
                                    hact[:, kk * LB + lc * 512: kk * LB + (lc + 1) * 512],
                                    start=(kk == 0), stop=(kk == NT - 1))
                                if lc == 1 and kk == NT - 1:
                                    ins.then_inc(m2d, 1)
                    if B >= 1:
                        e.wait_ge(app2, B * NK)
                    for k in range(NK):
                        e.wait_ge(ev2d, B * NK + k + 1)
                        e.wait_ge(sq2s, B * NK + k + 1)
                        for lc in range(2):
                            ins1 = e.matmul(psG0[0:1, lc * 512:(lc + 1) * 512], onesc[:],
                                            xt[:, k * L + B * LB + lc * 512: k * L + B * LB + (lc + 1) * 512],
                                            start=(k == 0), stop=(k == NK - 1))
                            ins2 = e.matmul(psG1[0:1, lc * 512:(lc + 1) * 512], onesc[:],
                                            sq2[:, lc * 512:(lc + 1) * 512],
                                            start=(k == 0), stop=(k == NK - 1))
                            if lc == 1:
                                ins2.then_inc(lnk2, 1)
                            if k == NK - 1 and lc == 1:
                                ins1.then_inc(m2ln, 1)
                    e.wait_ge(rst2, B + 1)
                    for lc in range(2):
                        e.matmul(psG0[:, lc * 512:(lc + 1) * 512], onesr[:],
                                 meanr2[:, lc * 512:(lc + 1) * 512], start=True, stop=True)
                        ins = e.matmul(psG1[:, lc * 512:(lc + 1) * 512], onesr[:],
                                       varr2[:, lc * 512:(lc + 1) * 512], start=True, stop=True)
                        if lc == 1:
                            ins.then_inc(bc2, 1)

            @blk.scalar
            def _(e):
                e.wait_ge(ld0, 48)
                for B in range(NB):
                    for t in range(NT):
                        g = B * NT + t
                        e.wait_ge(m2, 2 * g + 1)
                        if g >= 1:
                            e.wait_ge(e2, g)
                        e.activation(out=gact[:], in_=psG[g % 2][:, :], func=AF.Silu,
                                     scale=sg_s[:, t:t + 1]).then_inc(s2, 1)
                        e.wait_ge(m2, 2 * g + 2)
                        e.activation(out=uact[:], in_=psUp[g % 2][:, :], func=AF.Copy,
                                     scale=su_s[:, t:t + 1]).then_inc(s2, 1)
                    for k in range(NK):
                        e.wait_ge(ev2d, B * NK + k + 1)
                        if B * NK + k >= 1:
                            e.wait_ge(lnk2, B * NK + k)
                        e.activation(out=sq2[:],
                                     in_=xt[:, k * L + B * LB: k * L + B * LB + LB],
                                     func=AF.Square).then_inc(sq2s, 1)
                    e.wait_ge(st2, B + 1)
                    e.activation(out=varr2[:], in_=varr2[:], func=AF.Ln, bias=cvals[0:1, 0:1])
                    e.activation(out=varr2[:], in_=varr2[:], func=AF.Exp,
                                 scale=cvals[0:1, 1:2]).then_inc(rst2, 1)

            @blk.vector
            def _(e):
                def prep_gu(gg):
                    e.wait_ge(ld2, 32 * (gg + 1))
                    e.tensor_scalar(wg_b[gg % 2][:], wg_b[gg % 2][:], -128.0, None,
                                    ALU.add).then_inc(wr2, 1)
                    e.tensor_scalar(wu_b[gg % 2][:], wu_b[gg % 2][:], -128.0, None,
                                    ALU.add).then_inc(wr2, 1)
                for B in range(NB):
                    prep_gu(B * NT)
                    prep_gu(B * NT + 1)
                    for t in range(NT):
                        g = B * NT + t
                        if t + 2 < NT:
                            prep_gu(g + 2)
                        e.wait_ge(s2, 2 * g + 2)
                        e.tensor_tensor(hact[:, t * LB:(t + 1) * LB], gact[:], uact[:],
                                        ALU.mult).then_inc(e2, 1)
                    for k in range(NK):
                        kg = B * NK + k
                        e.wait_ge(ldd, 16 * (kg + 1))
                        e.tensor_scalar(wd_b[0][:], wd_b[0][:], -128.0, None,
                                        ALU.add).then_inc(wrd, 1)
                        e.wait_ge(m2d, kg + 1)
                        xsl = xt[:, k * L + B * LB: k * L + B * LB + LB]
                        e.scalar_tensor_tensor(xsl, psD[:, :], sd_s[:, k:k + 1], xsl,
                                               ALU.mult, ALU.add).then_inc(ev2d, 1)
                    e.wait_ge(m2ln, B + 1)
                    e.wait_ge(lnk2, (B + 1) * NK)
                    e.tensor_scalar(meanr2[:], psG0[0:1, :], 1.0 / DM, None, ALU.mult)
                    e.tensor_scalar(varr2[:], psG1[0:1, :], 1.0 / DM, None, ALU.mult)
                    e.tensor_tensor(tmpr2[:], meanr2[:], meanr2[:], ALU.mult)
                    e.tensor_tensor(varr2[:], varr2[:], tmpr2[:], ALU.subtract).then_inc(st2, 1)
                    e.wait_ge(bc2, B + 1)
                    for k in range(NK):
                        xsl = xt[:, k * L + B * LB: k * L + B * LB + LB]
                        e.tensor_tensor(at2[:], xsl, psG0[:, :], ALU.subtract)
                        e.tensor_tensor(xsl, at2[:], psG1[:, :], ALU.mult).then_inc(app2, 1)
        _barrier(nc, "p2bar")


def prep_p2(inputs):
    return {
        "wgT": np.ascontiguousarray(
            inputs["L0_ffn_wq_gate"].reshape(NT, 128, NK, 128).transpose(0, 3, 2, 1).reshape(NT * 128, NK * 128)),
        "wuT": np.ascontiguousarray(
            inputs["L0_ffn_wq_up"].reshape(NT, 128, NK, 128).transpose(0, 3, 2, 1).reshape(NT * 128, NK * 128)),
        "wdT": np.ascontiguousarray(
            inputs["L0_ffn_wq_down"].reshape(NK, 128, NT, 128).transpose(0, 3, 2, 1).reshape(NK * 128, NT * 128)),
        "sg": np.ascontiguousarray(inputs["L0_ffn_s_gate"].reshape(NT, 128).T),
        "su": np.ascontiguousarray(inputs["L0_ffn_s_up"].reshape(NT, 128).T),
        "sd": np.ascontiguousarray(inputs["L0_ffn_s_down"].reshape(NK, 128).T),
    }


def emit_p3(nc, cx):
    wq3 = nc.dram_tensor("wq3", [16 * 128, NK * 128], F32, kind="ExternalInput").ap()
    wkv3 = nc.dram_tensor("wkv3", [8 * 128, NK * 128], F32, kind="ExternalInput").ap()
    wo3 = nc.dram_tensor("wo3", [16 * 128, NK * 128], F32, kind="ExternalInput").ap()
    bq_d = nc.dram_tensor("bq3", [128, 16], F32, kind="ExternalInput").ap()
    bkv_d = nc.dram_tensor("bkv3", [128, 8], F32, kind="ExternalInput").ap()
    pe_d = nc.dram_tensor("pe3", [128, L], BF16, kind="ExternalInput").ap()
    tri_d = nc.dram_tensor("tri3", [128, 896], BF16, kind="ExternalInput").ap()
    id_d = nc.dram_tensor("id3", [128, 128], BF16, kind="ExternalInput").ap()
    xt = cx.xt
    onesc = cx.onesc
    onesr = cx.onesr
    cvals = cx.cvals
    NH = 16
    NQC = 4
    with ExitStack() as stk:
        def sb(name, shape, dtype):
            return stk.enter_context(nc.sbuf_tensor(name, shape, dtype))

        def sem(name):
            return stk.enter_context(nc.semaphore(name))

        qT = sb("qT", [128, NH * L], BF16)
        kT = sb("kT", [128, 4 * L], BF16)
        vaug = sb("vaug", [128, 4 * 16 * 129], BF16)
        expT = sb("expT", [128, 16 * 512], BF16)
        wqb = [sb(f"wqb{i}", [128, NK * 128], BF16) for i in range(2)]
        vtmp = sb("vtmp", [128, L], BF16)
        anat = sb("anat", [128, 512], BF16)
        rcol = sb("rcol", [128, 4], F32)
        bq_s = sb("bq_s", [128, 16], F32)
        bkv_s = sb("bkv_s", [128, 8], F32)
        pe_s = sb("pe_s", [128, L], BF16)
        tri_s = sb("tri_s", [128, 896], BF16)
        id_s = sb("id_s", [128, 128], BF16)
        psP = stk.enter_context(nc.psum_tensor("psP", [128, 2048], F32))
        psO = stk.enter_context(nc.psum_tensor("psO", [128, 2048], F32))
        psT = psP[:, 1536:1792].bitcast(BF16)
        meanr3 = expT[0:1, 0:4096].bitcast(F32)
        varr3 = expT[0:1, 4096:8192].bitcast(F32)
        tmpr3 = qT[0:1, 0:4096].bitcast(F32)
        sq3 = kT[:, 0:L]
        ld3 = sem("p3ld")
        ldg3 = sem("p3ldg")
        m3p = sem("p3mp")
        e3p = sem("p3ep")
        m3t = sem("p3mt")
        v3a = sem("p3va")
        m3s = sem("p3ms")
        e3x = sem("p3ex")
        v3m = sem("p3vm")
        m3o = sem("p3mo")
        v3r = sem("p3vr")
        s3a = sem("p3sa")
        m3tt = sem("p3mtt")
        v3q = sem("p3vq")
        m3wo = sem("p3mwo")
        e3wo = sem("p3ewo")
        sq3s = sem("p3sqs")
        ln3k = sem("p3lnk")
        m3ln = sem("p3mln")
        st3 = sem("p3st")
        rst3 = sem("p3rst")
        bc3 = sem("p3bc")
        app3 = sem("p3app")

        NU = 24
        sched = []
        pair_first_sc = {}
        pair_first_pc = {}
        sc = 0
        pc = 0
        for h in range(NH):
            for qc in range(NQC):
                pair_first_sc[(h, qc)] = sc
                pair_first_pc[(h, qc)] = pc
                for kt in range(4 * qc + 4):
                    partial = (kt >= 4 * qc)
                    sched.append((h, qc, kt, partial))
                    sc += 1
                    if partial:
                        pc += 1
        NSC = sc
        NPC = pc

        with nc.Block() as blk:
            @blk.sync
            def _(e):
                e.dma_start(out=bq_s[:], in_=bq_d).then_inc(ld3, 16)
                e.dma_start(out=bkv_s[:], in_=bkv_d).then_inc(ld3, 16)
                e.dma_start(out=pe_s[:], in_=pe_d).then_inc(ld3, 16)
                e.dma_start(out=tri_s[:], in_=tri_d).then_inc(ld3, 16)
                e.dma_start(out=id_s[:], in_=id_d).then_inc(ld3, 16)

            @blk.gpsimd
            def _(e):
                for u in range(40):
                    if 2 <= u < 26:
                        e.wait_ge(m3p, 4 * (u - 1))
                    elif u >= 26:
                        e.wait_ge(m3wo, 4 * (u - 24) - 4)
                    if u < 16:
                        src = wq3[u * 128:(u + 1) * 128, :]
                    elif u < 24:
                        src = wkv3[(u - 16) * 128:(u - 15) * 128, :]
                    else:
                        src = wo3[(u - 24) * 128:(u - 23) * 128, :]
                    e.dma_start(out=wqb[u % 2][:], in_=src).then_inc(ldg3, 16)

            @blk.tensor
            def _(e):
                for u in range(NU):
                    e.wait_ge(ldg3, 16 * (u + 1))
                    if u == 0:
                        e.wait_ge(ld3, 80)
                    for lcp in range(2):
                        idx0 = 4 * u + 2 * lcp
                        if idx0 >= 2:
                            e.wait_ge(e3p, idx0)
                        for k in range(NK):
                            for lc in (2 * lcp, 2 * lcp + 1):
                                ins = e.matmul(
                                    psP[:, (lc % 2) * 512:(lc % 2) * 512 + 512],
                                    wqb[u % 2][:, k * 128:(k + 1) * 128],
                                    xt[:, k * L + lc * 512: k * L + (lc + 1) * 512],
                                    start=(k == 0), stop=(k == NK - 1))
                                if k == NK - 1:
                                    ins.then_inc(m3p, 1)
                    if u >= 16 and (u - 16) % 2 == 1:
                        g = (u - 16) // 2
                        e.wait_ge(e3p, 4 * (u + 1))
                        for grp in range(4):
                            idx2 = 4 * g + grp
                            if idx2 >= 1:
                                e.wait_ge(v3a, idx2)
                            for j in range(4):
                                lt = grp * 4 + j
                                ins = e.transpose(
                                    psT[:, j * 128:(j + 1) * 128],
                                    vtmp[:, lt * 128:(lt + 1) * 128],
                                    id_s[:])
                                if j == 3:
                                    ins.then_inc(m3t, 1)
                sc2 = 0
                for h in range(NH):
                    g = h // 4
                    for qc in range(NQC):
                        idx3 = h * NQC + qc
                        for kt in range(4 * qc + 4):
                            partial = (kt >= 4 * qc)
                            if sc2 == 0:
                                e.wait_ge(e3p, 4 * NU)
                                e.wait_ge(m3t, 16)
                            if sc2 >= 3:
                                e.wait_ge(e3x, sc2 - 2)
                            ins = e.matmul(
                                psP[:, (sc2 % 3) * 512:(sc2 % 3) * 512 + 512],
                                kT[:, g * L + kt * 128: g * L + (kt + 1) * 128],
                                qT[:, h * L + qc * 512: h * L + (qc + 1) * 512],
                                start=True, stop=True).then_inc(m3s, 1)
                            sc2 += 1
                        e.wait_ge(e3x, sc2)
                        npc = pair_first_pc[(h, qc)] + 4
                        e.wait_ge(v3m, npc)
                        if idx3 == 0:
                            e.wait_ge(v3a, 16)
                        for qs in range(4):
                            oc = idx3 * 4 + qs
                            if oc - 3 >= 1:
                                e.wait_ge(s3a, oc - 3)
                            nkt = 4 * qc + 4
                            for kt in range(nkt):
                                ins = e.matmul(
                                    psO[:, qs * 512: qs * 512 + 129],
                                    expT[:, kt * 512 + qs * 128: kt * 512 + (qs + 1) * 128],
                                    vaug[:, g * 2064 + kt * 129: g * 2064 + (kt + 1) * 129],
                                    start=(kt == 0), stop=(kt == nkt - 1))
                                if kt == nkt - 1:
                                    ins.then_inc(m3o, 1)
                        e.wait_ge(s3a, 4 * (idx3 + 1))
                        if idx3 >= 1:
                            e.wait_ge(v3q, idx3)
                        for qs in range(4):
                            ins = e.transpose(
                                psT[:, qs * 128:(qs + 1) * 128],
                                anat[:, qs * 128:(qs + 1) * 128],
                                id_s[:])
                            if qs == 3:
                                ins.then_inc(m3tt, 1)
                for u in range(24, 40):
                    m = u - 24
                    e.wait_ge(ldg3, 16 * (u + 1))
                    if u == 24:
                        e.wait_ge(v3q, 64)
                    for lcp in range(2):
                        widx0 = 4 * m + 2 * lcp
                        if widx0 >= 2:
                            e.wait_ge(e3wo, widx0)
                        for k in range(NK):
                            for lc in (2 * lcp, 2 * lcp + 1):
                                ins = e.matmul(
                                    psP[:, (lc % 2) * 512:(lc % 2) * 512 + 512],
                                    wqb[u % 2][:, k * 128:(k + 1) * 128],
                                    qT[:, k * L + lc * 512: k * L + (lc + 1) * 512],
                                    start=(k == 0), stop=(k == NK - 1))
                                if k == NK - 1:
                                    ins.then_inc(m3wo, 1)
                for k in range(NK):
                    e.wait_ge(e3wo, 4 * NK)
                    e.wait_ge(sq3s, k + 1)
                    for lc in range(4):
                        ins1 = e.matmul(psP[0:1, lc * 512:(lc + 1) * 512], onesc[:],
                                        xt[:, k * L + lc * 512: k * L + (lc + 1) * 512],
                                        start=(k == 0), stop=(k == NK - 1))
                        ins2 = e.matmul(psO[0:1, lc * 512:(lc + 1) * 512], onesc[:],
                                        sq3[:, lc * 512:(lc + 1) * 512],
                                        start=(k == 0), stop=(k == NK - 1))
                        if lc == 3:
                            ins2.then_inc(ln3k, 1)
                        if k == NK - 1 and lc == 3:
                            ins1.then_inc(m3ln, 1)
                e.wait_ge(rst3, 1)
                for lc in range(4):
                    e.matmul(psP[:, lc * 512:(lc + 1) * 512], onesr[:],
                             meanr3[:, lc * 512:(lc + 1) * 512], start=True, stop=True)
                    ins = e.matmul(psO[:, lc * 512:(lc + 1) * 512], onesr[:],
                                   varr3[:, lc * 512:(lc + 1) * 512], start=True, stop=True)
                    if lc == 3:
                        ins.then_inc(bc3, 1)

            @blk.scalar
            def _(e):
                sc2 = 0
                for h in range(NH):
                    for qc in range(NQC):
                        idx3 = h * NQC + qc
                        if idx3 >= 1:
                            e.wait_ge(m3o, 4 * idx3)
                        for kt in range(4 * qc + 4):
                            e.wait_ge(m3s, sc2 + 1)
                            e.activation(out=expT[:, kt * 512:(kt + 1) * 512],
                                         in_=psP[:, (sc2 % 3) * 512:(sc2 % 3) * 512 + 512],
                                         func=AF.Exp,
                                         scale=cvals[:, 3:4]).then_inc(e3x, 1)
                            sc2 += 1
                        if idx3 >= 1:
                            e.wait_ge(m3tt, idx3)
                        for qs in range(4):
                            oc = idx3 * 4 + qs
                            e.wait_ge(v3r, oc + 1)
                            e.activation(out=anat[:, qs * 128:(qs + 1) * 128],
                                         in_=psO[:, qs * 512: qs * 512 + 128],
                                         func=AF.Copy,
                                         scale=rcol[:, qs:qs + 1]).then_inc(s3a, 1)
                for k in range(NK):
                    e.wait_ge(e3wo, 4 * NK)
                    if k >= 1:
                        e.wait_ge(ln3k, k)
                    e.activation(out=sq3[:], in_=xt[:, k * L:(k + 1) * L],
                                 func=AF.Square).then_inc(sq3s, 1)
                e.wait_ge(st3, 1)
                e.activation(out=varr3[:], in_=varr3[:], func=AF.Ln, bias=cvals[0:1, 0:1])
                e.activation(out=varr3[:], in_=varr3[:], func=AF.Exp,
                             scale=cvals[0:1, 1:2]).then_inc(rst3, 1)

            @blk.vector
            def _(e):
                e.memset(vaug[:], 1.0)
                for u in range(NU):
                    is_v = (u >= 16 and (u - 16) % 2 == 1)
                    if is_v:
                        g = (u - 16) // 2
                        if g >= 1:
                            e.wait_ge(m3t, 4 * g)
                        for lc in range(4):
                            idx = 4 * u + lc
                            e.wait_ge(m3p, idx + 1)
                            e.tensor_scalar(vtmp[:, lc * 512:(lc + 1) * 512],
                                            psP[:, (lc % 2) * 512:(lc % 2) * 512 + 512],
                                            bkv_s[:, (u - 16):(u - 15)], None,
                                            ALU.add).then_inc(e3p, 1)
                        for grp in range(4):
                            idx2 = 4 * g + grp
                            e.wait_ge(m3t, idx2 + 1)
                            e.tensor_copy(
                                vaug[:, g * 2064 + grp * 4 * 129: g * 2064 + (grp * 4 + 4) * 129]
                                .rearrange("p (t c) -> p t c", t=4)[:, :, 0:128],
                                psT[:, 0:512].rearrange("p (t c) -> p t c", t=4),
                            ).then_inc(v3a, 1)
                        continue
                    for lc in range(4):
                        idx = 4 * u + lc
                        e.wait_ge(m3p, idx + 1)
                        if u < 16:
                            dst = qT[:, u * L + lc * 512: u * L + (lc + 1) * 512]
                            bcol = bq_s[:, u:u + 1]
                        else:
                            g = (u - 16) // 2
                            dst = kT[:, g * L + lc * 512: g * L + (lc + 1) * 512]
                            bcol = bkv_s[:, (u - 16):(u - 15)]
                        e.scalar_tensor_tensor(
                            dst, psP[:, (lc % 2) * 512:(lc % 2) * 512 + 512], bcol,
                            pe_s[:, lc * 512:(lc + 1) * 512],
                            ALU.add, ALU.add).then_inc(e3p, 1)
                sc2 = 0
                pc2 = 0
                for h in range(NH):
                    for qc in range(NQC):
                        idx3 = h * NQC + qc
                        for kt in range(4 * qc + 4):
                            partial = (kt >= 4 * qc)
                            if partial:
                                e.wait_ge(e3x, sc2 + 1)
                                off = 384 - (kt - 4 * qc) * 128
                                e.tensor_tensor(
                                    expT[:, kt * 512:(kt + 1) * 512],
                                    expT[:, kt * 512:(kt + 1) * 512],
                                    tri_s[:, off: off + 512],
                                    ALU.mult).then_inc(v3m, 1)
                                pc2 += 1
                            sc2 += 1
                        for qs in range(4):
                            oc = idx3 * 4 + qs
                            e.wait_ge(m3o, oc + 1)
                            e.reciprocal(rcol[:, qs:qs + 1],
                                         psO[:, qs * 512 + 128: qs * 512 + 129]).then_inc(v3r, 1)
                        e.wait_ge(m3tt, idx3 + 1)
                        e.tensor_copy(qT[:, h * L + qc * 512: h * L + (qc + 1) * 512],
                                      psT[:, 0:512]).then_inc(v3q, 1)
                for m in range(16):
                    for lc in range(4):
                        widx = 4 * m + lc
                        e.wait_ge(m3wo, widx + 1)
                        xsl = xt[:, m * L + lc * 512: m * L + (lc + 1) * 512]
                        e.tensor_tensor(xsl, psP[:, (lc % 2) * 512:(lc % 2) * 512 + 512],
                                        xsl, ALU.add).then_inc(e3wo, 1)
                e.wait_ge(m3ln, 1)
                e.wait_ge(ln3k, 16)
                e.tensor_scalar(meanr3[:], psP[0:1, :], 1.0 / DM, None, ALU.mult)
                e.tensor_scalar(varr3[:], psO[0:1, :], 1.0 / DM, None, ALU.mult)
                e.tensor_tensor(tmpr3[:], meanr3[:], meanr3[:], ALU.mult)
                e.tensor_tensor(varr3[:], varr3[:], tmpr3[:], ALU.subtract).then_inc(st3, 1)
                e.wait_ge(bc3, 1)
                for k in range(NK):
                    xsl = xt[:, k * L:(k + 1) * L]
                    e.tensor_tensor(sq3[:], xsl, psP[:, :], ALU.subtract)
                    e.tensor_tensor(xsl, sq3[:], psO[:, :], ALU.mult).then_inc(app3, 1)
        _barrier(nc, "p3bar")


def prep_p3(inputs):
    t, d = L, 128
    pos = np.arange(t, dtype=np.float32)[:, None]
    i = np.arange(0, d, 2, dtype=np.float32)
    inv_freq = np.exp(-(np.log(10000.0) * i / d))
    ang = pos * inv_freq
    pe = np.zeros((t, d), np.float32)
    pe[:, 0::2] = np.sin(ang)
    pe[:, 1::2] = np.cos(ang)
    peT = np.ascontiguousarray(pe.T).astype(BF)
    tri = np.zeros((128, 896), BF)
    for kk in range(128):
        tri[kk, 384 + kk:] = 1.0
    return {
        "wq3": np.ascontiguousarray(
            inputs["L1_attn_wq"].reshape(NK, 128, 16, 128).transpose(2, 1, 0, 3).reshape(16 * 128, NK * 128)),
        "wkv3": np.ascontiguousarray(
            inputs["L1_attn_wkv"].reshape(NK, 128, 8, 128).transpose(2, 1, 0, 3).reshape(8 * 128, NK * 128)),
        "wo3": np.ascontiguousarray(
            inputs["L1_attn_wo"].reshape(NK, 128, 16, 128).transpose(2, 1, 0, 3).reshape(16 * 128, NK * 128)),
        "bq3": np.ascontiguousarray(inputs["L1_attn_bq"].reshape(16, 128).T),
        "bkv3": np.ascontiguousarray(inputs["L1_attn_bkv"].reshape(8, 128).T),
        "pe3": peT,
        "tri3": tri,
        "id3": np.eye(128, dtype=np.float32).astype(BF),
    }


NJOBS = 136
JPR = NJOBS // 8


def emit_p4a(nc, cx):
    jpk_d = nc.dram_tensor("jpk", [JPR * 128, 256], I32, kind="ExternalInput").ap()
    jsc_d = nc.dram_tensor("jsc", [128, JPR * 16], F16, kind="ExternalInput").ap()
    id4_d = nc.dram_tensor("id4", [128, 128], BF16, kind="ExternalInput").ap()
    scr_my = nc.dram_tensor("scr_my", [JPR * 128, 2048], BF16, kind="Internal")
    scr_all = nc.dram_tensor("scr_all", [NJOBS * 128, 2048], BF16, kind="Internal")
    cx.scr_all = scr_all.ap()
    with ExitStack() as stk:
        def sb(name, shape, dtype):
            return stk.enter_context(nc.sbuf_tensor(name, shape, dtype))

        def sem(name):
            return stk.enter_context(nc.semaphore(name))

        pk = [sb(f"pk{i}", [128, 512], U16) for i in range(2)]
        nib = sb("nib", [128, 2048], U16)
        tt1 = sb("tt1", [128, 2048], U16)
        tt2 = sb("tt2", [128, 2048], U16)
        bits = sb("bits", [128, 2048], U16)
        m01 = sb("m01", [128, 2048], U16)
        spc = sb("spc", [128, 2048], U16)
        vbits = sb("vbits", [128, 2048], U16)
        wbf = sb("wbf", [128, 2048], BF16)
        wout = [sb(f"wout{i}", [128, 2048], BF16) for i in range(2)]
        jsc_s = sb("jsc_s", [128, JPR * 16], F32)
        jsch = sb("jsch", [128, JPR * 16], F16)
        id4_s = sb("id4_s", [128, 128], BF16)
        psX = stk.enter_context(nc.psum_tensor("psX", [128, 2048], F32))
        s4l = sem("p4sl")
        v4x = sem("p4vx")
        v4d = sem("p4vd")
        m4t = sem("p4mt")
        v4e = sem("p4ve")
        w4o = sem("p4wo")
        c4l = sem("p4cl")
        vsc = sem("p4vsc")

        with nc.Block() as blk:
            @blk.sync
            def _(e):
                e.dma_start(out=jsch[:], in_=jsc_d).then_inc(c4l, 16)
                e.dma_start(out=id4_s[:], in_=id4_d).then_inc(c4l, 16)
                for j in range(JPR):
                    if j >= 2:
                        e.wait_ge(v4x, j - 1)
                    e.dma_start(out=pk[j % 2][:].bitcast(I32),
                                in_=jpk_d[j * 128:(j + 1) * 128, :]).then_inc(s4l, 16)

            @blk.scalar
            def _(e):
                for j in range(JPR):
                    e.wait_ge(v4e, 2 * j + 2)
                    e.dma_start(out=scr_my.ap()[j * 128:(j + 1) * 128, :],
                                in_=wout[j % 2][:]).then_inc(w4o, 16)
                e.wait_ge(w4o, 16 * JPR)

            @blk.vector
            def _(e):
                e.wait_ge(c4l, 32)
                e.tensor_copy(jsc_s[:], jsch[:]).then_inc(vsc, 1)
                for j in range(JPR):
                    nd = 2048
                    e.wait_ge(s4l, 16 * (j + 1))
                    for jj in range(4):
                        ins = e.tensor_scalar(nib[:, jj:nd:4], pk[j % 2][:],
                                              4 * jj, 15, ALU.logical_shift_right,
                                              ALU.bitwise_and)
                        if jj == 3:
                            ins.then_inc(v4x, 1)
                    e.tensor_scalar(tt1[:], nib[:], 8, 12,
                                    ALU.bitwise_and, ALU.logical_shift_left)
                    e.tensor_scalar(tt2[:], nib[:], 7, 6,
                                    ALU.bitwise_and, ALU.logical_shift_left)
                    e.tensor_tensor(bits[:], tt1[:], tt2[:], ALU.add)
                    e.tensor_scalar(bits[:], bits[:], 16128, None, ALU.add)
                    e.tensor_scalar(m01[:], nib[:], 6, None, ALU.bitwise_and)
                    e.tensor_scalar(m01[:], m01[:], 0, None, ALU.is_equal)
                    e.tensor_scalar(spc[:], nib[:], 1, None, ALU.bitwise_and)
                    e.tensor_scalar(spc[:], spc[:], 0x3F00, None, ALU.mult)
                    e.tensor_tensor(spc[:], spc[:], tt1[:], ALU.add)
                    if j >= 1:
                        e.wait_ge(m4t, 2 * j)
                    e.select(vbits[:], m01[:], spc[:], bits[:])
                    for gidx in range(16):
                        scol = jsc_s[:, j * 16 + gidx: j * 16 + gidx + 1]
                        ins = e.tensor_scalar(wbf[:, gidx * 128:(gidx + 1) * 128],
                                              vbits[:, gidx * 128:(gidx + 1) * 128].bitcast(BF16),
                                              scol, None, ALU.mult)
                        if gidx == 15:
                            ins.then_inc(v4d, 1)
                    for grp in range(2):
                        e.wait_ge(m4t, 2 * j + grp + 1)
                        if j >= 2:
                            e.wait_ge(w4o, 16 * (j - 1))
                        e.tensor_copy(
                            wout[j % 2][:, grp * 1024: grp * 1024 + 1024],
                            psX[:, grp * 512: grp * 512 + 512].bitcast(BF16),
                        ).then_inc(v4e, 1)

            @blk.tensor
            def _(e):
                e.wait_ge(c4l, 32)
                for j in range(JPR):
                    e.wait_ge(v4d, j + 1)
                    for grp in range(2):
                        if j >= 1:
                            e.wait_ge(v4e, 2 * j - 1 + grp)
                        for i in range(8):
                            gidx = grp * 8 + i
                            ins = e.transpose(
                                psX[:, grp * 512: grp * 512 + 512].bitcast(BF16)[:, i * 128:(i + 1) * 128],
                                wbf[:, gidx * 128:(gidx + 1) * 128],
                                id4_s[:])
                            if i == 7:
                                ins.then_inc(m4t, 1)

            @blk.gpsimd
            def _(e):
                e.wait_ge(w4o, 16 * JPR)
                e.collective_compute(
                    "AllGather",
                    ALU.bypass,
                    replica_groups=[list(range(8))],
                    ins=[scr_my.ap().opt()],
                    outs=[scr_all.ap().opt()],
                ).then_inc(cx.ccS, 1)
        _barrier(nc, "p4abar")


def prep_p4a(inputs):
    gup = inputs["L1_ffn_gate_up_packed"]
    dnp = inputs["L1_ffn_down_packed"]
    sgu = inputs["L1_ffn_gate_up_scales"]
    sdn = inputs["L1_ffn_down_scales"]
    jobs_pk = []
    jobs_sc = []
    for t in range(88):
        jobs_pk.append(gup[t * 128:(t + 1) * 128, :])
        jobs_sc.append(np.ascontiguousarray(sgu[:, t * 128:(t + 1) * 128]).T)
    for t in range(16):
        for c in range(3):
            blkp = np.zeros((128, 256), np.int32)
            w32 = min(256, 704 - c * 256)
            blkp[:, 0:w32] = dnp[t * 128:(t + 1) * 128, c * 256: c * 256 + w32]
            jobs_pk.append(blkp)
            blks = np.zeros((128, 16), np.float16)
            ngc = min(16, 44 - c * 16)
            blks[:, 0:ngc] = sdn[c * 16: c * 16 + ngc, t * 128:(t + 1) * 128].T
            jobs_sc.append(blks)
    per_core_pk = []
    per_core_sc = []
    for r in range(8):
        per_core_pk.append(np.ascontiguousarray(
            np.concatenate(jobs_pk[r * JPR:(r + 1) * JPR], axis=0)))
        per_core_sc.append(np.ascontiguousarray(
            np.concatenate(jobs_sc[r * JPR:(r + 1) * JPR], axis=1).astype(np.float16)))
    return {
        "jpk": per_core_pk,
        "jsc": per_core_sc,
        "id4": np.eye(128, dtype=np.float32).astype(BF),
    }


LB4 = 512
NB4 = L // LB4


def emit_p4b(nc, cx):
    scr_all = cx.scr_all
    xt = cx.xt
    onesc = cx.onesc
    onesr = cx.onesr
    cvals = cx.cvals
    with ExitStack() as stk:
        def sb(name, shape, dtype):
            return stk.enter_context(nc.sbuf_tensor(name, shape, dtype))

        def sem(name):
            return stk.enter_context(nc.semaphore(name))

        hact = sb("hact4", [128, NT * LB4], BF16)
        yblk = sb("yblk", [128, NK * LB4], BF16)
        g4 = [sb(f"g4{i}", [128, NK * 128], BF16) for i in range(2)]
        u4 = [sb(f"u4{i}", [128, NK * 128], BF16) for i in range(2)]
        dn4 = [sb(f"dn4{i}", [128, NT * 128], BF16) for i in range(2)]
        gact = sb("gact4", [128, LB4], BF16)
        uact = sb("uact4", [128, LB4], BF16)
        sq4 = sb("sq4", [128, LB4], BF16)
        tmp4 = sb("tmp4", [128, LB4], BF16)
        rmsrow = sb("rmsrow", [1, LB4], F32)
        meanr4 = sb("meanr4", [1, LB4], F32)
        varr4 = sb("varr4", [1, LB4], F32)
        tmpr4 = sb("tmpr4", [1, LB4], F32)
        psA4 = stk.enter_context(nc.psum_tensor("psA4", [128, 2048], F32))
        psB4 = stk.enter_context(nc.psum_tensor("psB4", [128, 2048], F32))
        l4g = sem("p4lg")
        m4 = sem("p4m")
        s4 = sem("p4s")
        e4 = sem("p4e")
        l4d = sem("p4ld2")
        m4d = sem("p4md")
        ev4d = sem("p4evd")
        sqy4s = sem("p4sqy")
        mr4k = sem("p4mrk")
        vr4 = sem("p4vr")
        rst4r = sem("p4rstr")
        bcr4 = sem("p4bcr")
        appr4 = sem("p4appr")
        sq4s = sem("p4sqs")
        ln4k = sem("p4lnk")
        m4ln = sem("p4mln")
        st4 = sem("p4st")
        rst4 = sem("p4rst")
        bc4 = sem("p4bc")
        app4 = sem("p4app")

        with nc.Block() as blk:
            @blk.sync
            def _(e):
                e.wait_ge(cx.ccS, 1)
                for B in range(NB4):
                    for t in range(NT):
                        g = B * NT + t
                        if g >= 2:
                            e.wait_ge(m4, 2 * (g - 2) + 2)
                        e.dma_start(out=g4[g % 2][:],
                                    in_=scr_all[t * 128:(t + 1) * 128, :]).then_inc(l4g, 16)
                        e.dma_start(out=u4[g % 2][:],
                                    in_=scr_all[(t + 44) * 128:(t + 45) * 128, :]).then_inc(l4g, 16)
                    for k in range(NK):
                        kg = B * NK + k
                        if kg >= 2:
                            e.wait_ge(m4d, kg - 1)
                        for cc in range(3):
                            jj = 88 + 3 * k + cc
                            wcols = 1536 if cc == 2 else 2048
                            e.dma_start(
                                out=dn4[k % 2][:, cc * 2048: cc * 2048 + wcols],
                                in_=scr_all[jj * 128:(jj + 1) * 128, 0:wcols]).then_inc(l4d, 16)

            @blk.tensor
            def _(e):
                for B in range(NB4):
                    for t in range(NT):
                        g = B * NT + t
                        e.wait_ge(l4g, 32 * (g + 1))
                        if g >= 2:
                            e.wait_ge(s4, 2 * (g - 2) + 1)
                        if t < 2 and B >= 1:
                            e.wait_ge(app4, B * NK)
                        for k in range(NK):
                            ins = e.matmul(
                                psA4[:, (g % 2) * 512:(g % 2) * 512 + 512],
                                g4[g % 2][:, k * 128:(k + 1) * 128],
                                xt[:, k * L + B * LB4: k * L + (B + 1) * LB4],
                                start=(k == 0), stop=(k == NK - 1))
                            if k == NK - 1:
                                ins.then_inc(m4, 1)
                        if g >= 2:
                            e.wait_ge(s4, 2 * (g - 2) + 2)
                        for k in range(NK):
                            ins = e.matmul(
                                psA4[:, 1024 + (g % 2) * 512: 1024 + (g % 2) * 512 + 512],
                                u4[g % 2][:, k * 128:(k + 1) * 128],
                                xt[:, k * L + B * LB4: k * L + (B + 1) * LB4],
                                start=(k == 0), stop=(k == NK - 1))
                            if k == NK - 1:
                                ins.then_inc(m4, 1)
                    for k in range(NK):
                        kg = B * NK + k
                        e.wait_ge(l4d, 48 * (kg + 1))
                        if k == 0:
                            e.wait_ge(e4, (B + 1) * NT)
                        if kg >= 2:
                            e.wait_ge(ev4d, kg - 1)
                        for kk in range(NT):
                            ins = e.matmul(
                                psB4[:, (k % 2) * 512:(k % 2) * 512 + 512],
                                dn4[k % 2][:, kk * 128:(kk + 1) * 128],
                                hact[:, kk * LB4:(kk + 1) * LB4],
                                start=(kk == 0), stop=(kk == NT - 1))
                            if kk == NT - 1:
                                ins.then_inc(m4d, 1)
                    for k in range(NK):
                        e.wait_ge(sqy4s, B * NK + k + 1)
                        ins = e.matmul(psB4[0:1, 1024:1536], onesc[:], sq4[:],
                                       start=(k == 0), stop=(k == NK - 1))
                        ins.then_inc(mr4k, 1)
                    e.wait_ge(rst4r, B + 1)
                    e.matmul(psB4[:, 1024:1536], onesr[:], rmsrow[:],
                             start=True, stop=True).then_inc(bcr4, 1)
                    for k in range(NK):
                        e.wait_ge(appr4, B * NK + k + 1)
                        e.wait_ge(sq4s, B * NK + k + 1)
                        ins1 = e.matmul(psB4[0:1, 1536:2048], onesc[:],
                                        xt[:, k * L + B * LB4: k * L + (B + 1) * LB4],
                                        start=(k == 0), stop=(k == NK - 1))
                        ins2 = e.matmul(psA4[0:1, 0:512], onesc[:], sq4[:],
                                        start=(k == 0), stop=(k == NK - 1))
                        ins2.then_inc(ln4k, 1)
                        if k == NK - 1:
                            ins1.then_inc(m4ln, 1)
                    e.wait_ge(rst4, B + 1)
                    e.matmul(psA4[:, 0:512], onesr[:], meanr4[:],
                             start=True, stop=True)
                    e.matmul(psA4[:, 512:1024], onesr[:], varr4[:],
                             start=True, stop=True).then_inc(bc4, 1)

            @blk.scalar
            def _(e):
                for B in range(NB4):
                    for t in range(NT):
                        g = B * NT + t
                        e.wait_ge(m4, 2 * g + 1)
                        if g >= 1:
                            e.wait_ge(e4, g)
                        e.activation(out=gact[:], in_=psA4[:, (g % 2) * 512:(g % 2) * 512 + 512],
                                     func=AF.Silu).then_inc(s4, 1)
                        e.wait_ge(m4, 2 * g + 2)
                        e.activation(out=uact[:], in_=psA4[:, 1024 + (g % 2) * 512: 1024 + (g % 2) * 512 + 512],
                                     func=AF.Copy).then_inc(s4, 1)
                    for k in range(NK):
                        e.wait_ge(ev4d, B * NK + k + 1)
                        if B * NK + k >= 1:
                            e.wait_ge(mr4k, B * NK + k)
                        e.activation(out=sq4[:], in_=yblk[:, k * LB4:(k + 1) * LB4],
                                     func=AF.Square).then_inc(sqy4s, 1)
                    e.wait_ge(vr4, B + 1)
                    e.activation(out=rmsrow[:], in_=rmsrow[:], func=AF.Ln, bias=cvals[0:1, 2:3])
                    e.activation(out=rmsrow[:], in_=rmsrow[:], func=AF.Exp,
                                 scale=cvals[0:1, 1:2]).then_inc(rst4r, 1)
                    for k in range(NK):
                        e.wait_ge(appr4, B * NK + k + 1)
                        if B * NK + k >= 1:
                            e.wait_ge(ln4k, B * NK + k)
                        e.activation(out=sq4[:], in_=xt[:, k * L + B * LB4: k * L + (B + 1) * LB4],
                                     func=AF.Square).then_inc(sq4s, 1)
                    e.wait_ge(st4, B + 1)
                    e.activation(out=varr4[:], in_=varr4[:], func=AF.Ln, bias=cvals[0:1, 0:1])
                    e.activation(out=varr4[:], in_=varr4[:], func=AF.Exp,
                                 scale=cvals[0:1, 1:2]).then_inc(rst4, 1)

            @blk.vector
            def _(e):
                for B in range(NB4):
                    for t in range(NT):
                        g = B * NT + t
                        e.wait_ge(s4, 2 * g + 2)
                        e.tensor_tensor(hact[:, t * LB4:(t + 1) * LB4], gact[:], uact[:],
                                        ALU.mult).then_inc(e4, 1)
                    for k in range(NK):
                        kg = B * NK + k
                        e.wait_ge(m4d, kg + 1)
                        e.tensor_copy(yblk[:, k * LB4:(k + 1) * LB4],
                                      psB4[:, (k % 2) * 512:(k % 2) * 512 + 512]).then_inc(ev4d, 1)
                    e.wait_ge(mr4k, (B + 1) * NK)
                    e.tensor_scalar(rmsrow[:], psB4[0:1, 1024:1536], 1.0 / DM, None,
                                    ALU.mult).then_inc(vr4, 1)
                    e.wait_ge(bcr4, B + 1)
                    for k in range(NK):
                        xsl = xt[:, k * L + B * LB4: k * L + (B + 1) * LB4]
                        e.tensor_tensor(tmp4[:], yblk[:, k * LB4:(k + 1) * LB4],
                                        psB4[:, 1024:1536], ALU.mult)
                        e.tensor_tensor(xsl, xsl, tmp4[:], ALU.add).then_inc(appr4, 1)
                    e.wait_ge(m4ln, B + 1)
                    e.wait_ge(ln4k, (B + 1) * NK)
                    e.tensor_scalar(meanr4[:], psB4[0:1, 1536:2048], 1.0 / DM, None, ALU.mult)
                    e.tensor_scalar(varr4[:], psA4[0:1, 0:512], 1.0 / DM, None, ALU.mult)
                    e.tensor_tensor(tmpr4[:], meanr4[:], meanr4[:], ALU.mult)
                    e.tensor_tensor(varr4[:], varr4[:], tmpr4[:], ALU.subtract).then_inc(st4, 1)
                    e.wait_ge(bc4, B + 1)
                    for k in range(NK):
                        xsl = xt[:, k * L + B * LB4: k * L + (B + 1) * LB4]
                        e.tensor_tensor(sq4[:], xsl, psA4[:, 0:512], ALU.subtract)
                        e.tensor_tensor(xsl, sq4[:], psA4[:, 512:1024], ALU.mult).then_inc(app4, 1)
        _barrier(nc, "p4bbar")


def prep_p4(inputs):
    sgu = inputs["L1_ffn_gate_up_scales"]
    sdn = inputs["L1_ffn_down_scales"]
    sgu_h = np.ascontiguousarray(
        sgu.T.reshape(88, 128, 16).transpose(1, 0, 2).reshape(128, 88 * 16))
    sdn_h = np.ascontiguousarray(
        sdn.T.reshape(16, 128, 44).transpose(1, 0, 2).reshape(128, 16 * 44))
    return {
        "gup": np.ascontiguousarray(inputs["L1_ffn_gate_up_packed"]),
        "dnp": np.ascontiguousarray(inputs["L1_ffn_down_packed"]),
        "sgu": sgu_h,
        "sdn": sdn_h,
        "id4": np.eye(128, dtype=np.float32).astype(BF),
    }


def prep_p4b(inputs):
    return {}


PHASE_EMIT = {"p1": emit_p1, "p2": emit_p2, "p3": emit_p3,
              "p4a": emit_p4a, "p4b": emit_p4b}
PHASE_PREP = {"p1": prep_p1, "p2": prep_p2, "p3": prep_p3,
              "p4a": prep_p4a, "p4b": prep_p4b}

PLAN = [("p4a", "p1", "p2", "p3", "p4b")]


def build_launch(phases):
    nc = bass.Bass("TRN2", target_bir_lowering=False, debug=False)
    nc._stk = ExitStack()
    xt_in = nc.dram_tensor("xt_in", [128, NK * L], F32, kind="ExternalInput").ap()
    onesc_d = nc.dram_tensor("onesc", [128, 1], BF16, kind="ExternalInput").ap()
    onesr_d = nc.dram_tensor("onesr", [1, 128], F32, kind="ExternalInput").ap()
    xt_out = nc.dram_tensor("xt_out", [128, NK * L], F32, kind="ExternalOutput").ap()
    with nc._stk:
        cx = Ctx()
        cx.xt = nc._stk.enter_context(nc.sbuf_tensor("xt", [128, NK * L], BF16))
        cx.onesc = nc._stk.enter_context(nc.sbuf_tensor("onesc_sb", [128, 1], BF16))
        cx.onesr = nc._stk.enter_context(nc.sbuf_tensor("onesr_sb", [1, 128], F32))
        cx.cvals = nc._stk.enter_context(nc.sbuf_tensor("cvals", [128, 4], F32))
        cx.ccS = nc._stk.enter_context(nc.semaphore("ccS"))
        hld = nc._stk.enter_context(nc.semaphore("hld"))
        hldg = nc._stk.enter_context(nc.semaphore("hldg"))
        with nc.Block() as blk:
            @blk.sync
            def _(e):
                e.dma_start(out=cx.onesc[:], in_=onesc_d).then_inc(hld, 16)
                e.dma_start(out=cx.onesr[:], in_=onesr_d).then_inc(hld, 16)
                e.wait_ge(hld, 32)
                e.wait_ge(hldg, 64)

            @blk.gpsimd
            def _(e):
                for j in range(4):
                    e.dma_start(
                        out=cx.xt[:, j * 4 * L:(j + 1) * 4 * L],
                        in_=xt_in[:, j * 4 * L:(j + 1) * 4 * L],
                    ).then_inc(hldg, 16)
                e.wait_ge(hldg, 64)
                e.wait_ge(hld, 32)

            @blk.vector
            def _(e):
                e.memset(cx.cvals[:, 0:1], 1e-5)
                e.memset(cx.cvals[:, 1:2], -0.5)
                e.memset(cx.cvals[:, 2:3], 1e-6)
                e.memset(cx.cvals[:, 3:4], 0.08838834764831845)
        _barrier(nc, "hbar")
        for ph in phases:
            PHASE_EMIT[ph](nc, cx)
        outS = nc._stk.enter_context(nc.semaphore("outS"))
        with nc.Block() as blk:
            @blk.gpsimd
            def _(e):
                for k in range(NK):
                    e.dma_start(out=xt_out[:, k * L:(k + 1) * L],
                                in_=cx.xt[:, k * L:(k + 1) * L]).then_inc(outS, 16)
                e.wait_ge(outS, 16 * NK)
    return nc


_BUILD_CACHE = {}


def kernel(**inputs):
    x = inputs["L0_attn_x"]
    out = np.empty((8, 2048, 2048), np.float32)
    def tile_xt(a):
        return np.ascontiguousarray(a.T.reshape(NK, 128, L).transpose(1, 0, 2).reshape(128, NK * L))

    def untile_xt(a):
        return a.reshape(128, NK, L).transpose(1, 0, 2).reshape(DM, L)

    cur = [tile_xt(x[b]) for b in range(8)]
    consts = {"onesc": np.ones((128, 1), BF), "onesr": np.ones((1, 128), np.float32)}
    for phases in PLAN:
        if phases not in _BUILD_CACHE:
            _BUILD_CACHE[phases] = build_launch(phases)
        nc = _BUILD_CACHE[phases]
        common = dict(consts)
        for ph in phases:
            common.update(PHASE_PREP[ph](inputs))
        in_maps = []
        for b in range(8):
            m = {}
            for kk, vv in common.items():
                m[kk] = vv[b] if isinstance(vv, list) else vv
            m["xt_in"] = cur[b]
            in_maps.append(m)
        res = run_bass_kernel_spmd(nc, in_maps, core_ids=list(range(8)))
        cur = [res.results[b]["xt_out"] for b in range(8)]
    for b in range(8):
        out[b] = untile_xt(cur[b]).T
    return out


# version 09
# speedup vs baseline: 1.1167x; 1.1167x over previous
import numpy as np
import ml_dtypes
from contextlib import ExitStack
import concourse.bass as bass
import concourse.mybir as mybir
from concourse.bass_utils import run_bass_kernel_spmd

F32 = mybir.dt.float32
BF16 = mybir.dt.bfloat16
F16 = mybir.dt.float16
I32 = mybir.dt.int32
U8 = mybir.dt.uint8
U16 = mybir.dt.uint16
AF = mybir.ActivationFunctionType
ALU = mybir.AluOpType
BF = ml_dtypes.bfloat16

L = 2048
DM = 2048
NK = 16
NS = 8
NLH = 2
LHALF = L // NLH
HID = 5632
NT = HID // 128
LB = 1024
NB = L // LB


class Ctx:
    pass


def _barrier(nc, name):
    ph = nc._stk.enter_context(nc.semaphore(name))
    with nc.Block() as blk:
        def mk(e):
            e.drain().then_inc(ph, 1)
            e.wait_ge(ph, 5)
        blk.sync(mk)
        blk.scalar(mk)
        blk.vector(mk)
        blk.tensor(mk)
        blk.gpsimd(mk)
    return ph


def emit_p1(nc, cx):
    wx = nc.dram_tensor("wx", [128, NK * 272], F32, kind="ExternalInput").ap()
    wdt = nc.dram_tensor("wdt", [128, 2 * DM], F32, kind="ExternalInput").ap()
    bdt_d = nc.dram_tensor("bdt", [128, NK], F32, kind="ExternalInput").ap()
    alog_d = nc.dram_tensor("alog", [128, NK * NS], F32, kind="ExternalInput").ap()
    dpar_d = nc.dram_tensor("dpar", [128, NK], F32, kind="ExternalInput").ap()
    sel_d = nc.dram_tensor("sel", [16, 2048], BF16, kind="ExternalInput").ap()
    xt = cx.xt
    onesc = cx.onesc
    onesr = cx.onesr
    cvals = cx.cvals
    with ExitStack() as stk:
        def sb(name, shape, dtype):
            return stk.enter_context(nc.sbuf_tensor(name, shape, dtype))

        def sem(name):
            return stk.enter_context(nc.semaphore(name))

        wx_s = sb("wx_s", [128, NK * 272], BF16)
        wdt_s = sb("wdt_s", [128, 2 * DM], BF16)
        bdt_s = sb("bdt_s", [128, NK], F32)
        alog_s = sb("alog_s", [128, NK * NS], F32)
        negA = sb("negA", [128, NK * NS], F32)
        dpar_s = sb("dpar_s", [128, NK], F32)
        dp1 = sb("dp1", [128, NK], F32)
        sel_s = sb("sel_s", [16, 2048], BF16)
        dtr = sb("dtr", [128, 2 * L], BF16)
        bc16 = sb("bc16", [16, L], BF16)
        bcb = sb("bcb", [128, NS * LHALF], BF16)
        ccb = sb("ccb", [128, NS * LHALF], BF16)
        zz = sb("zz", [128, LHALF], F32)
        sp1 = sb("sp1", [128, LHALF], F32)
        sp2 = sb("sp2", [128, LHALF], F32)
        dlt = sb("dlt", [128, LHALF], F32)
        uu = sb("uu", [128, LHALF], BF16)
        dbu = sb("dbu", [128, LHALF], BF16)
        hh = sb("hh", [128, LHALF], BF16)
        tmp = sb("tmp", [128, LHALF], BF16)
        yy = sb("yy", [128, LHALF], BF16)
        da0 = sb("da0", [128, LHALF], BF16)
        da1 = sb("da1", [128, LHALF], BF16)
        state = sb("state", [128, 128], F32)
        sq = sb("sq", [128, L], BF16)
        meanr = sb("meanr", [1, L], F32)
        varr = sb("varr", [1, L], F32)
        tmpr = sb("tmpr", [1, L], F32)
        psA = stk.enter_context(nc.psum_tensor("psA", [128, 2048], F32))
        psB = stk.enter_context(nc.psum_tensor("psB", [128, 2048], F32))
        ld = sem("p1ld")
        ldg = sem("p1ldg")
        mmA = sem("p1mmA")
        evA = sem("p1evA")
        evBC = sem("p1evBC")
        zk = sem("p1zk")
        sp = sem("p1sp")
        da = sem("p1da")
        hc = sem("p1hc")
        dl = sem("p1dl")
        res = sem("p1res")
        sqs = sem("p1sqs")
        lnk = sem("p1lnk")
        st_ = sem("p1st")
        rst = sem("p1rst")
        app = sem("p1app")
        da_bufs = [da0, da1]

        with nc.Block() as blk:
            @blk.sync
            def _(e):
                e.dma_start(out=bdt_s[:], in_=bdt_d).then_inc(ld, 16)
                e.dma_start(out=alog_s[:], in_=alog_d).then_inc(ld, 16)
                e.dma_start(out=dpar_s[:], in_=dpar_d).then_inc(ld, 16)
                e.dma_start(out=sel_s[:], in_=sel_d).then_inc(ld, 16)

            @blk.gpsimd
            def _(e):
                e.dma_start(out=wx_s[:], in_=wx).then_inc(ldg, 16)
                e.dma_start(out=wdt_s[:], in_=wdt).then_inc(ldg, 16)

            @blk.tensor
            def _(e):
                e.wait_ge(ld, 64)
                e.wait_ge(ldg, 32)
                e.wait_ge(cx.hldg, 64)
                for c in range(3):
                    mwid = 128 if c < 2 else 16
                    if c >= 1:
                        e.wait_ge(evA, 4 * c)
                    for k in range(NK):
                        for lc in range(4):
                            ins = e.matmul(
                                psA[0:mwid, lc * 512:(lc + 1) * 512],
                                wx_s[:, k * 272 + c * 128: k * 272 + c * 128 + mwid],
                                xt[:, k * L + lc * 512: k * L + (lc + 1) * 512],
                                start=(k == 0), stop=(k == NK - 1),
                            )
                            if k == NK - 1:
                                ins.then_inc(mmA, 1)
                for lh in range(NLH):
                    for n in range(16):
                        idx = lh * 16 + n
                        if idx - 1 >= 1:
                            e.wait_ge(evBC, idx - 1)
                        if idx == 0:
                            e.wait_ge(evA, 12)
                        if n == 0 and lh >= 1:
                            e.wait_ge(zk, lh * NK)
                        ps = psA if (n % 2 == 0) else psB
                        for lc in range(2):
                            ins = e.matmul(
                                ps[:, lc * 512:(lc + 1) * 512],
                                sel_s[:, n * 128:(n + 1) * 128],
                                bc16[:, lh * LHALF + lc * 512: lh * LHALF + (lc + 1) * 512],
                                start=True, stop=True,
                            )
                            if lc == 1:
                                ins.then_inc(mmA, 1)
                    for k in range(NK):
                        gidx = lh * NK + k
                        if gidx == 0:
                            e.wait_ge(evBC, 16)
                        if gidx >= 1:
                            e.wait_ge(zk, gidx)
                        if gidx == NK:
                            e.wait_ge(evBC, 32)
                        for r in range(2):
                            for lc in range(2):
                                ins = e.matmul(
                                    psA[:, lc * 512:(lc + 1) * 512],
                                    wdt_s[:, r * DM + k * 128: r * DM + (k + 1) * 128],
                                    dtr[:, r * L + lh * LHALF + lc * 512: r * L + lh * LHALF + (lc + 1) * 512],
                                    start=(r == 0), stop=(r == 1),
                                )
                                if lc == 1 and r == 1:
                                    ins.then_inc(mmA, 1)
                e.wait_ge(zk, 32)
                for k in range(NK):
                    e.wait_ge(res, NK + k + 1)
                    e.wait_ge(sqs, k + 1)
                    for lc in range(4):
                        ins1 = e.matmul(psA[0:1, lc * 512:(lc + 1) * 512], onesc[:],
                                        xt[:, k * L + lc * 512: k * L + (lc + 1) * 512],
                                        start=(k == 0), stop=(k == NK - 1))
                        ins2 = e.matmul(psB[0:1, lc * 512:(lc + 1) * 512], onesc[:],
                                        sq[:, lc * 512:(lc + 1) * 512],
                                        start=(k == 0), stop=(k == NK - 1))
                        if lc == 3:
                            ins2.then_inc(lnk, 1)
                        if k == NK - 1 and lc == 3:
                            ins1.then_inc(mmA, 1)
                e.wait_ge(rst, 1)
                for lc in range(4):
                    e.matmul(psA[:, lc * 512:(lc + 1) * 512], onesr[:],
                             meanr[:, lc * 512:(lc + 1) * 512], start=True, stop=True)
                    ins = e.matmul(psB[:, lc * 512:(lc + 1) * 512], onesr[:],
                                   varr[:, lc * 512:(lc + 1) * 512], start=True, stop=True)
                    if lc == 3:
                        ins.then_inc(mmA, 1)

            @blk.vector
            def _(e):
                e.wait_ge(cx.hldg, 64)
                for c in range(3):
                    for lc in range(4):
                        i = c * 4 + lc
                        e.wait_ge(mmA, i + 1)
                        if c < 2:
                            e.tensor_copy(dtr[:, c * L + lc * 512: c * L + (lc + 1) * 512],
                                          psA[:, lc * 512:(lc + 1) * 512]).then_inc(evA, 1)
                        else:
                            e.tensor_copy(bc16[:, lc * 512:(lc + 1) * 512],
                                          psA[0:16, lc * 512:(lc + 1) * 512]).then_inc(evA, 1)
                for lh in range(NLH):
                    for n in range(16):
                        e.wait_ge(mmA, 12 + lh * 32 + n + 1)
                        ps = psA if (n % 2 == 0) else psB
                        dstt = bcb if n < 8 else ccb
                        nn = n % 8
                        e.tensor_copy(dstt[:, nn * LHALF:(nn + 1) * LHALF],
                                      ps[:, 0:1024]).then_inc(evBC, 1)
                    for k in range(NK):
                        gidx = lh * NK + k
                        e.wait_ge(mmA, 12 + lh * 32 + 16 + k + 1)
                        e.tensor_scalar(zz[:], psA[:, 0:1024], bdt_s[:, k:k + 1], None,
                                        ALU.add).then_inc(zk, 1)
                        e.wait_ge(sp, gidx + 1)
                        e.scalar_tensor_tensor(dlt[:], zz[:], 0.0, sp1[:],
                                               ALU.max, ALU.add).then_inc(dl, 1)
                        e.tensor_tensor(uu[:], dlt[:],
                                        xt[:, k * L + lh * LHALF: k * L + lh * LHALF + LHALF],
                                        ALU.mult)
                        for n in range(NS):
                            g = gidx * NS + n
                            e.tensor_tensor(dbu[:], uu[:],
                                            bcb[:, n * LHALF:(n + 1) * LHALF], ALU.mult)
                            e.wait_ge(da, g + 1)
                            init = 0.0 if lh == 0 else state[:, (k * NS + n):(k * NS + n + 1)]
                            e.tensor_tensor_scan(hh[:], da_bufs[n % 2][:], dbu[:], init,
                                                 ALU.mult, ALU.add).then_inc(hc, 1)
                            if n == 0:
                                e.tensor_tensor(yy[:], hh[:],
                                                ccb[:, n * LHALF:(n + 1) * LHALF], ALU.mult)
                            else:
                                e.tensor_tensor(tmp[:], hh[:],
                                                ccb[:, n * LHALF:(n + 1) * LHALF], ALU.mult)
                                e.tensor_tensor(yy[:], yy[:], tmp[:], ALU.add)
                            if lh == 0:
                                e.tensor_copy(state[:, (k * NS + n):(k * NS + n + 1)],
                                              hh[:, LHALF - 1:LHALF])
                        xsl = xt[:, k * L + lh * LHALF: k * L + lh * LHALF + LHALF]
                        e.scalar_tensor_tensor(xsl, xsl, dp1[:, k:k + 1], yy[:],
                                               ALU.mult, ALU.add).then_inc(res, 1)
                e.wait_ge(mmA, 77)
                e.wait_ge(lnk, 16)
                e.tensor_scalar(meanr[:], psA[0:1, :], 1.0 / DM, None, ALU.mult)
                e.tensor_scalar(varr[:], psB[0:1, :], 1.0 / DM, None, ALU.mult)
                e.tensor_tensor(tmpr[:], meanr[:], meanr[:], ALU.mult)
                e.tensor_tensor(varr[:], varr[:], tmpr[:], ALU.subtract).then_inc(st_, 1)
                e.wait_ge(mmA, 78)
                for k in range(NK):
                    xsl = xt[:, k * L:(k + 1) * L]
                    e.tensor_tensor(sq[:], xsl, psA[:, :], ALU.subtract)
                    e.tensor_tensor(xsl, sq[:], psB[:, :], ALU.mult).then_inc(app, 1)

            @blk.scalar
            def _(e):
                e.wait_ge(ld, 64)
                e.wait_ge(ldg, 32)
                e.wait_ge(cx.hldg, 64)
                e.activation(out=negA[:], in_=alog_s[:], func=AF.Exp)
                e.mul(negA[:], negA[:], -1.0)
                e.add(dp1[:], dpar_s[:], 1.0)
                for lh in range(NLH):
                    for k in range(NK):
                        gidx = lh * NK + k
                        e.wait_ge(zk, gidx + 1)
                        e.activation(out=sp2[:], in_=zz[:], func=AF.Abs)
                        e.activation(out=sp1[:], in_=sp2[:], func=AF.Exp, scale=-1.0)
                        e.activation(out=sp1[:], in_=sp1[:], func=AF.Ln,
                                     bias=1.0).then_inc(sp, 1)
                        e.wait_ge(dl, gidx + 1)
                        for n in range(NS):
                            g = gidx * NS + n
                            if g - 1 >= 1:
                                e.wait_ge(hc, g - 1)
                            e.activation(out=da_bufs[n % 2][:], in_=dlt[:], func=AF.Exp,
                                         scale=negA[:, (k * NS + n):(k * NS + n + 1)]
                                         ).then_inc(da, 1)
                for k in range(NK):
                    e.wait_ge(res, NK + k + 1)
                    if k >= 1:
                        e.wait_ge(lnk, k)
                    e.activation(out=sq[:], in_=xt[:, k * L:(k + 1) * L],
                                 func=AF.Square).then_inc(sqs, 1)
                e.wait_ge(st_, 1)
                e.activation(out=varr[:], in_=varr[:], func=AF.Ln, bias=cvals[0:1, 0:1])
                e.activation(out=varr[:], in_=varr[:], func=AF.Exp,
                             scale=cvals[0:1, 1:2]).then_inc(rst, 1)
        _barrier(nc, "p1bar")


def prep_p1(inputs):
    sel = np.zeros((16, 2048), BF)
    for n in range(16):
        sel[n, n * 128:(n + 1) * 128] = 1.0
    return {
        "wx": np.ascontiguousarray(
            inputs["L0_attn_W_xproj"].T.reshape(NK, 128, 272).transpose(1, 0, 2).reshape(128, NK * 272)),
        "wdt": np.ascontiguousarray(
            inputs["L0_attn_W_dt"].T.reshape(2, 128, DM).transpose(1, 0, 2).reshape(128, 2 * DM)),
        "bdt": np.ascontiguousarray(inputs["L0_attn_b_dt"].reshape(NK, 128).T),
        "alog": np.ascontiguousarray(
            inputs["L0_attn_A_log"].reshape(NK, 128, NS).transpose(1, 0, 2).reshape(128, NK * NS)),
        "dpar": np.ascontiguousarray(inputs["L0_attn_D_param"].reshape(NK, 128).T),
        "sel": sel,
    }


def emit_p2(nc, cx):
    wgT = nc.dram_tensor("wgT", [NT * 128, NK * 128], I32, kind="ExternalInput").ap()
    wuT = nc.dram_tensor("wuT", [NT * 128, NK * 128], I32, kind="ExternalInput").ap()
    wdT = nc.dram_tensor("wdT", [NK * 128, NT * 128], I32, kind="ExternalInput").ap()
    sg_d = nc.dram_tensor("sg", [128, NT], F32, kind="ExternalInput").ap()
    su_d = nc.dram_tensor("su", [128, NT], F32, kind="ExternalInput").ap()
    sd_d = nc.dram_tensor("sd", [128, NK], F32, kind="ExternalInput").ap()
    xt = cx.xt
    onesc = cx.onesc
    onesr = cx.onesr
    cvals = cx.cvals
    with ExitStack() as stk:
        def sb(name, shape, dtype):
            return stk.enter_context(nc.sbuf_tensor(name, shape, dtype))

        def sem(name):
            return stk.enter_context(nc.semaphore(name))

        hact = sb("hact", [128, NT * LB], BF16)
        wg_b = [sb(f"wg{i}", [128, NK * 128], BF16) for i in range(2)]
        wu_b = [sb(f"wu{i}", [128, NK * 128], BF16) for i in range(2)]
        wd_b = [sb(f"wd{i}", [128, NT * 128], BF16) for i in range(2)]
        gact = sb("gact", [128, LB], BF16)
        uact = sb("uact", [128, LB], BF16)
        sg_s = sb("sg_s", [128, NT], F32)
        su_s = sb("su_s", [128, NT], F32)
        sd_s = sb("sd_s", [128, NK], F32)
        sq2 = sb("sq2", [128, LB], BF16)
        at2 = sb("at2", [128, LB], BF16)
        meanr2 = hact[0:1, 0:2048].bitcast(F32)
        varr2 = hact[0:1, 2048:4096].bitcast(F32)
        tmpr2 = hact[0:1, 4096:6144].bitcast(F32)
        psG0 = stk.enter_context(nc.psum_tensor("psG0", [128, LB], F32))
        psG1 = stk.enter_context(nc.psum_tensor("psG1", [128, LB], F32))
        psU = stk.enter_context(nc.psum_tensor("psU", [128, LB], F32))
        psD = stk.enter_context(nc.psum_tensor("psD", [128, LB], F32))
        psG = [psG0, psG1]
        psUp = [psU, psD]
        ld2 = sem("p2ld")
        wr2 = sem("p2wr")
        m2 = sem("p2m")
        s2 = sem("p2s")
        e2 = sem("p2e")
        ldd = sem("p2ldd")
        wrd = sem("p2wrd")
        m2d = sem("p2md")
        ev2d = sem("p2evd")
        sq2s = sem("p2sqs")
        lnk2 = sem("p2lnk")
        m2ln = sem("p2mln")
        st2 = sem("p2st")
        rst2 = sem("p2rst")
        bc2 = sem("p2bc")
        app2 = sem("p2app")
        ld0 = sem("p2ld0")

        with nc.Block() as blk:
            @blk.sync
            def _(e):
                e.dma_start(out=sg_s[:], in_=sg_d).then_inc(ld0, 16)
                e.dma_start(out=su_s[:], in_=su_d).then_inc(ld0, 16)
                e.dma_start(out=sd_s[:], in_=sd_d).then_inc(ld0, 16)

            @blk.gpsimd
            def _(e):
                for B in range(NB):
                    for t in range(NT):
                        g = B * NT + t
                        if g >= 2:
                            e.wait_ge(m2, 2 * (g - 2) + 2)
                        e.dma_start(out=wg_b[g % 2][:],
                                    in_=wgT[t * 128:(t + 1) * 128, :]).then_inc(ld2, 16)
                        e.dma_start(out=wu_b[g % 2][:],
                                    in_=wuT[t * 128:(t + 1) * 128, :]).then_inc(ld2, 16)
                    for k in range(NK):
                        kg = B * NK + k
                        if kg >= 2:
                            e.wait_ge(m2d, kg - 1)
                        e.dma_start(out=wd_b[k % 2][:],
                                    in_=wdT[k * 128:(k + 1) * 128, :]).then_inc(ldd, 16)

            @blk.tensor
            def _(e):
                for B in range(NB):
                    for t in range(NT):
                        g = B * NT + t
                        e.wait_ge(wr2, 2 * g + 1)
                        if g >= 2:
                            e.wait_ge(s2, 2 * (g - 2) + 1)
                        if t < 2 and B >= 1:
                            e.wait_ge(app2, B * NK)
                        for k in range(NK):
                            for lc in range(2):
                                ins = e.matmul(
                                    psG[g % 2][:, lc * 512:(lc + 1) * 512],
                                    wg_b[g % 2][:, k * 128:(k + 1) * 128],
                                    xt[:, k * L + B * LB + lc * 512: k * L + B * LB + (lc + 1) * 512],
                                    start=(k == 0), stop=(k == NK - 1))
                                if lc == 1 and k == NK - 1:
                                    ins.then_inc(m2, 1)
                        e.wait_ge(wr2, 2 * g + 2)
                        if g >= 2:
                            e.wait_ge(s2, 2 * (g - 2) + 2)
                        for k in range(NK):
                            for lc in range(2):
                                ins = e.matmul(
                                    psUp[g % 2][:, lc * 512:(lc + 1) * 512],
                                    wu_b[g % 2][:, k * 128:(k + 1) * 128],
                                    xt[:, k * L + B * LB + lc * 512: k * L + B * LB + (lc + 1) * 512],
                                    start=(k == 0), stop=(k == NK - 1))
                                if lc == 1 and k == NK - 1:
                                    ins.then_inc(m2, 1)
                    for k in range(NK):
                        kg = B * NK + k
                        e.wait_ge(wrd, kg + 1)
                        if k == 0:
                            e.wait_ge(e2, (B + 1) * NT)
                        e.wait_ge(ev2d, kg)
                        for kk in range(NT):
                            for lc in range(2):
                                ins = e.matmul(
                                    psD[:, lc * 512:(lc + 1) * 512],
                                    wd_b[k % 2][:, kk * 128:(kk + 1) * 128],
                                    hact[:, kk * LB + lc * 512: kk * LB + (lc + 1) * 512],
                                    start=(kk == 0), stop=(kk == NT - 1))
                                if lc == 1 and kk == NT - 1:
                                    ins.then_inc(m2d, 1)
                    if B >= 1:
                        e.wait_ge(app2, B * NK)
                    for k in range(NK):
                        e.wait_ge(ev2d, B * NK + k + 1)
                        e.wait_ge(sq2s, B * NK + k + 1)
                        for lc in range(2):
                            ins1 = e.matmul(psG0[0:1, lc * 512:(lc + 1) * 512], onesc[:],
                                            xt[:, k * L + B * LB + lc * 512: k * L + B * LB + (lc + 1) * 512],
                                            start=(k == 0), stop=(k == NK - 1))
                            ins2 = e.matmul(psG1[0:1, lc * 512:(lc + 1) * 512], onesc[:],
                                            sq2[:, lc * 512:(lc + 1) * 512],
                                            start=(k == 0), stop=(k == NK - 1))
                            if lc == 1:
                                ins2.then_inc(lnk2, 1)
                            if k == NK - 1 and lc == 1:
                                ins1.then_inc(m2ln, 1)
                    e.wait_ge(rst2, B + 1)
                    for lc in range(2):
                        e.matmul(psG0[:, lc * 512:(lc + 1) * 512], onesr[:],
                                 meanr2[:, lc * 512:(lc + 1) * 512], start=True, stop=True)
                        ins = e.matmul(psG1[:, lc * 512:(lc + 1) * 512], onesr[:],
                                       varr2[:, lc * 512:(lc + 1) * 512], start=True, stop=True)
                        if lc == 1:
                            ins.then_inc(bc2, 1)

            @blk.scalar
            def _(e):
                e.wait_ge(ld0, 48)
                for B in range(NB):
                    for t in range(NT):
                        g = B * NT + t
                        e.wait_ge(m2, 2 * g + 1)
                        if g >= 1:
                            e.wait_ge(e2, g)
                        e.activation(out=gact[:], in_=psG[g % 2][:, :], func=AF.Silu,
                                     scale=sg_s[:, t:t + 1]).then_inc(s2, 1)
                        e.wait_ge(m2, 2 * g + 2)
                        e.activation(out=uact[:], in_=psUp[g % 2][:, :], func=AF.Copy,
                                     scale=su_s[:, t:t + 1]).then_inc(s2, 1)
                    for k in range(NK):
                        e.wait_ge(ev2d, B * NK + k + 1)
                        if B * NK + k >= 1:
                            e.wait_ge(lnk2, B * NK + k)
                        e.activation(out=sq2[:],
                                     in_=xt[:, k * L + B * LB: k * L + B * LB + LB],
                                     func=AF.Square).then_inc(sq2s, 1)
                    e.wait_ge(st2, B + 1)
                    e.activation(out=varr2[:], in_=varr2[:], func=AF.Ln, bias=cvals[0:1, 0:1])
                    e.activation(out=varr2[:], in_=varr2[:], func=AF.Exp,
                                 scale=cvals[0:1, 1:2]).then_inc(rst2, 1)

            @blk.vector
            def _(e):
                def prep_gu(gg):
                    e.wait_ge(ld2, 32 * (gg + 1))
                    e.tensor_scalar(wg_b[gg % 2][:], wg_b[gg % 2][:], -128.0, None,
                                    ALU.add).then_inc(wr2, 1)
                    e.tensor_scalar(wu_b[gg % 2][:], wu_b[gg % 2][:], -128.0, None,
                                    ALU.add).then_inc(wr2, 1)
                for B in range(NB):
                    prep_gu(B * NT)
                    prep_gu(B * NT + 1)
                    for t in range(NT):
                        g = B * NT + t
                        if t + 2 < NT:
                            prep_gu(g + 2)
                        e.wait_ge(s2, 2 * g + 2)
                        e.tensor_tensor(hact[:, t * LB:(t + 1) * LB], gact[:], uact[:],
                                        ALU.mult).then_inc(e2, 1)
                    for k in range(NK):
                        kg = B * NK + k
                        e.wait_ge(ldd, 16 * (kg + 1))
                        e.tensor_scalar(wd_b[k % 2][:], wd_b[k % 2][:], -128.0, None,
                                        ALU.add).then_inc(wrd, 1)
                        e.wait_ge(m2d, kg + 1)
                        xsl = xt[:, k * L + B * LB: k * L + B * LB + LB]
                        e.scalar_tensor_tensor(xsl, psD[:, :], sd_s[:, k:k + 1], xsl,
                                               ALU.mult, ALU.add).then_inc(ev2d, 1)
                    e.wait_ge(m2ln, B + 1)
                    e.wait_ge(lnk2, (B + 1) * NK)
                    e.tensor_scalar(meanr2[:], psG0[0:1, :], 1.0 / DM, None, ALU.mult)
                    e.tensor_scalar(varr2[:], psG1[0:1, :], 1.0 / DM, None, ALU.mult)
                    e.tensor_tensor(tmpr2[:], meanr2[:], meanr2[:], ALU.mult)
                    e.tensor_tensor(varr2[:], varr2[:], tmpr2[:], ALU.subtract).then_inc(st2, 1)
                    e.wait_ge(bc2, B + 1)
                    for k in range(NK):
                        xsl = xt[:, k * L + B * LB: k * L + B * LB + LB]
                        e.tensor_tensor(at2[:], xsl, psG0[:, :], ALU.subtract)
                        e.tensor_tensor(xsl, at2[:], psG1[:, :], ALU.mult).then_inc(app2, 1)
        _barrier(nc, "p2bar")


def prep_p2(inputs):
    return {
        "wgT": np.ascontiguousarray(
            inputs["L0_ffn_wq_gate"].reshape(NT, 128, NK, 128).transpose(0, 3, 2, 1).reshape(NT * 128, NK * 128)),
        "wuT": np.ascontiguousarray(
            inputs["L0_ffn_wq_up"].reshape(NT, 128, NK, 128).transpose(0, 3, 2, 1).reshape(NT * 128, NK * 128)),
        "wdT": np.ascontiguousarray(
            inputs["L0_ffn_wq_down"].reshape(NK, 128, NT, 128).transpose(0, 3, 2, 1).reshape(NK * 128, NT * 128)),
        "sg": np.ascontiguousarray(inputs["L0_ffn_s_gate"].reshape(NT, 128).T),
        "su": np.ascontiguousarray(inputs["L0_ffn_s_up"].reshape(NT, 128).T),
        "sd": np.ascontiguousarray(inputs["L0_ffn_s_down"].reshape(NK, 128).T),
    }


def emit_p3(nc, cx):
    wq3 = nc.dram_tensor("wq3", [16 * 128, NK * 128], F32, kind="ExternalInput").ap()
    wkv3 = nc.dram_tensor("wkv3", [8 * 128, NK * 128], F32, kind="ExternalInput").ap()
    wo3 = nc.dram_tensor("wo3", [16 * 128, NK * 128], F32, kind="ExternalInput").ap()
    bq_d = nc.dram_tensor("bq3", [128, 16], F32, kind="ExternalInput").ap()
    bkv_d = nc.dram_tensor("bkv3", [128, 8], F32, kind="ExternalInput").ap()
    pe_d = nc.dram_tensor("pe3", [128, L], BF16, kind="ExternalInput").ap()
    tri_d = nc.dram_tensor("tri3", [128, 896], BF16, kind="ExternalInput").ap()
    id_d = nc.dram_tensor("id3", [128, 128], BF16, kind="ExternalInput").ap()
    xt = cx.xt
    onesc = cx.onesc
    onesr = cx.onesr
    cvals = cx.cvals
    NH = 16
    NQC = 4
    with ExitStack() as stk:
        def sb(name, shape, dtype):
            return stk.enter_context(nc.sbuf_tensor(name, shape, dtype))

        def sem(name):
            return stk.enter_context(nc.semaphore(name))

        qT = sb("qT", [128, NH * L], BF16)
        kT = sb("kT", [128, 4 * L], BF16)
        vaug = sb("vaug", [128, 4 * 16 * 129], BF16)
        expT = sb("expT", [128, 16 * 512], BF16)
        wqb = [sb(f"wqb{i}", [128, NK * 128], BF16) for i in range(2)]
        vtmp = sb("vtmp", [128, L], BF16)
        anat = sb("anat", [128, 512], BF16)
        rcol = sb("rcol", [128, 4], F32)
        bq_s = sb("bq_s", [128, 16], F32)
        bkv_s = sb("bkv_s", [128, 8], F32)
        pe_s = sb("pe_s", [128, L], BF16)
        tri_s = sb("tri_s", [128, 896], BF16)
        id_s = sb("id_s", [128, 128], BF16)
        psP = stk.enter_context(nc.psum_tensor("psP", [128, 2048], F32))
        psO = stk.enter_context(nc.psum_tensor("psO", [128, 2048], F32))
        psT = psP[:, 1536:1792].bitcast(BF16)
        meanr3 = expT[0:1, 0:4096].bitcast(F32)
        varr3 = expT[0:1, 4096:8192].bitcast(F32)
        tmpr3 = qT[0:1, 0:4096].bitcast(F32)
        sq3 = kT[:, 0:L]
        ld3 = sem("p3ld")
        ldg3 = sem("p3ldg")
        m3p = sem("p3mp")
        e3p = sem("p3ep")
        m3t = sem("p3mt")
        v3a = sem("p3va")
        m3s = sem("p3ms")
        e3x = sem("p3ex")
        v3m = sem("p3vm")
        m3o = sem("p3mo")
        v3r = sem("p3vr")
        s3a = sem("p3sa")
        m3tt = sem("p3mtt")
        v3q = sem("p3vq")
        m3wo = sem("p3mwo")
        e3wo = sem("p3ewo")
        sq3s = sem("p3sqs")
        ln3k = sem("p3lnk")
        m3ln = sem("p3mln")
        st3 = sem("p3st")
        rst3 = sem("p3rst")
        bc3 = sem("p3bc")
        app3 = sem("p3app")

        NU = 24
        sched = []
        pair_first_sc = {}
        pair_first_pc = {}
        sc = 0
        pc = 0
        for h in range(NH):
            for qc in range(NQC):
                pair_first_sc[(h, qc)] = sc
                pair_first_pc[(h, qc)] = pc
                for kt in range(4 * qc + 4):
                    partial = (kt >= 4 * qc)
                    sched.append((h, qc, kt, partial))
                    sc += 1
                    if partial:
                        pc += 1
        NSC = sc
        NPC = pc

        with nc.Block() as blk:
            @blk.sync
            def _(e):
                e.dma_start(out=bq_s[:], in_=bq_d).then_inc(ld3, 16)
                e.dma_start(out=bkv_s[:], in_=bkv_d).then_inc(ld3, 16)
                e.dma_start(out=pe_s[:], in_=pe_d).then_inc(ld3, 16)
                e.dma_start(out=tri_s[:], in_=tri_d).then_inc(ld3, 16)
                e.dma_start(out=id_s[:], in_=id_d).then_inc(ld3, 16)

            @blk.gpsimd
            def _(e):
                for u in range(40):
                    if 2 <= u < 26:
                        e.wait_ge(m3p, 4 * (u - 1))
                    elif u >= 26:
                        e.wait_ge(m3wo, 4 * (u - 24) - 4)
                    if u < 16:
                        src = wq3[u * 128:(u + 1) * 128, :]
                    elif u < 24:
                        src = wkv3[(u - 16) * 128:(u - 15) * 128, :]
                    else:
                        src = wo3[(u - 24) * 128:(u - 23) * 128, :]
                    e.dma_start(out=wqb[u % 2][:], in_=src).then_inc(ldg3, 16)

            @blk.tensor
            def _(e):
                for u in range(NU):
                    e.wait_ge(ldg3, 16 * (u + 1))
                    if u == 0:
                        e.wait_ge(ld3, 80)
                    for lcp in range(2):
                        idx0 = 4 * u + 2 * lcp
                        if idx0 >= 2:
                            e.wait_ge(e3p, idx0)
                        for k in range(NK):
                            for lc in (2 * lcp, 2 * lcp + 1):
                                ins = e.matmul(
                                    psP[:, (lc % 2) * 512:(lc % 2) * 512 + 512],
                                    wqb[u % 2][:, k * 128:(k + 1) * 128],
                                    xt[:, k * L + lc * 512: k * L + (lc + 1) * 512],
                                    start=(k == 0), stop=(k == NK - 1))
                                if k == NK - 1:
                                    ins.then_inc(m3p, 1)
                    if u >= 16 and (u - 16) % 2 == 1:
                        g = (u - 16) // 2
                        e.wait_ge(e3p, 4 * (u + 1))
                        for grp in range(4):
                            idx2 = 4 * g + grp
                            if idx2 >= 1:
                                e.wait_ge(v3a, idx2)
                            for j in range(4):
                                lt = grp * 4 + j
                                ins = e.transpose(
                                    psT[:, j * 128:(j + 1) * 128],
                                    vtmp[:, lt * 128:(lt + 1) * 128],
                                    id_s[:])
                                if j == 3:
                                    ins.then_inc(m3t, 1)
                sc2 = 0
                for h in range(NH):
                    g = h // 4
                    for qc in range(NQC):
                        idx3 = h * NQC + qc
                        for kt in range(4 * qc + 4):
                            partial = (kt >= 4 * qc)
                            if sc2 == 0:
                                e.wait_ge(e3p, 4 * NU)
                                e.wait_ge(m3t, 16)
                            if sc2 >= 3:
                                e.wait_ge(e3x, sc2 - 2)
                            ins = e.matmul(
                                psP[:, (sc2 % 3) * 512:(sc2 % 3) * 512 + 512],
                                kT[:, g * L + kt * 128: g * L + (kt + 1) * 128],
                                qT[:, h * L + qc * 512: h * L + (qc + 1) * 512],
                                start=True, stop=True).then_inc(m3s, 1)
                            sc2 += 1
                        e.wait_ge(e3x, sc2)
                        npc = pair_first_pc[(h, qc)] + 4
                        e.wait_ge(v3m, npc)
                        if idx3 == 0:
                            e.wait_ge(v3a, 16)
                        for qs in range(4):
                            oc = idx3 * 4 + qs
                            if oc - 3 >= 1:
                                e.wait_ge(s3a, oc - 3)
                            nkt = 4 * qc + 4
                            for kt in range(nkt):
                                ins = e.matmul(
                                    psO[:, qs * 512: qs * 512 + 129],
                                    expT[:, kt * 512 + qs * 128: kt * 512 + (qs + 1) * 128],
                                    vaug[:, g * 2064 + kt * 129: g * 2064 + (kt + 1) * 129],
                                    start=(kt == 0), stop=(kt == nkt - 1))
                                if kt == nkt - 1:
                                    ins.then_inc(m3o, 1)
                        e.wait_ge(s3a, 4 * (idx3 + 1))
                        if idx3 >= 1:
                            e.wait_ge(v3q, idx3)
                        for qs in range(4):
                            ins = e.transpose(
                                psT[:, qs * 128:(qs + 1) * 128],
                                anat[:, qs * 128:(qs + 1) * 128],
                                id_s[:])
                            if qs == 3:
                                ins.then_inc(m3tt, 1)
                for u in range(24, 40):
                    m = u - 24
                    e.wait_ge(ldg3, 16 * (u + 1))
                    if u == 24:
                        e.wait_ge(v3q, 64)
                    for lcp in range(2):
                        widx0 = 4 * m + 2 * lcp
                        if widx0 >= 2:
                            e.wait_ge(e3wo, widx0)
                        for k in range(NK):
                            for lc in (2 * lcp, 2 * lcp + 1):
                                ins = e.matmul(
                                    psP[:, (lc % 2) * 512:(lc % 2) * 512 + 512],
                                    wqb[u % 2][:, k * 128:(k + 1) * 128],
                                    qT[:, k * L + lc * 512: k * L + (lc + 1) * 512],
                                    start=(k == 0), stop=(k == NK - 1))
                                if k == NK - 1:
                                    ins.then_inc(m3wo, 1)
                for k in range(NK):
                    e.wait_ge(e3wo, 4 * NK)
                    e.wait_ge(sq3s, k + 1)
                    for lc in range(4):
                        ins1 = e.matmul(psP[0:1, lc * 512:(lc + 1) * 512], onesc[:],
                                        xt[:, k * L + lc * 512: k * L + (lc + 1) * 512],
                                        start=(k == 0), stop=(k == NK - 1))
                        ins2 = e.matmul(psO[0:1, lc * 512:(lc + 1) * 512], onesc[:],
                                        sq3[:, lc * 512:(lc + 1) * 512],
                                        start=(k == 0), stop=(k == NK - 1))
                        if lc == 3:
                            ins2.then_inc(ln3k, 1)
                        if k == NK - 1 and lc == 3:
                            ins1.then_inc(m3ln, 1)
                e.wait_ge(rst3, 1)
                for lc in range(4):
                    e.matmul(psP[:, lc * 512:(lc + 1) * 512], onesr[:],
                             meanr3[:, lc * 512:(lc + 1) * 512], start=True, stop=True)
                    ins = e.matmul(psO[:, lc * 512:(lc + 1) * 512], onesr[:],
                                   varr3[:, lc * 512:(lc + 1) * 512], start=True, stop=True)
                    if lc == 3:
                        ins.then_inc(bc3, 1)

            @blk.scalar
            def _(e):
                sc2 = 0
                for h in range(NH):
                    for qc in range(NQC):
                        idx3 = h * NQC + qc
                        if idx3 >= 1:
                            e.wait_ge(m3o, 4 * idx3)
                        for kt in range(4 * qc + 4):
                            e.wait_ge(m3s, sc2 + 1)
                            e.activation(out=expT[:, kt * 512:(kt + 1) * 512],
                                         in_=psP[:, (sc2 % 3) * 512:(sc2 % 3) * 512 + 512],
                                         func=AF.Exp,
                                         scale=cvals[:, 3:4]).then_inc(e3x, 1)
                            sc2 += 1
                        if idx3 >= 1:
                            e.wait_ge(m3tt, idx3)
                        for qs in range(4):
                            oc = idx3 * 4 + qs
                            e.wait_ge(v3r, oc + 1)
                            e.activation(out=anat[:, qs * 128:(qs + 1) * 128],
                                         in_=psO[:, qs * 512: qs * 512 + 128],
                                         func=AF.Copy,
                                         scale=rcol[:, qs:qs + 1]).then_inc(s3a, 1)
                for k in range(NK):
                    e.wait_ge(e3wo, 4 * NK)
                    if k >= 1:
                        e.wait_ge(ln3k, k)
                    e.activation(out=sq3[:], in_=xt[:, k * L:(k + 1) * L],
                                 func=AF.Square).then_inc(sq3s, 1)
                e.wait_ge(st3, 1)
                e.activation(out=varr3[:], in_=varr3[:], func=AF.Ln, bias=cvals[0:1, 0:1])
                e.activation(out=varr3[:], in_=varr3[:], func=AF.Exp,
                             scale=cvals[0:1, 1:2]).then_inc(rst3, 1)

            @blk.vector
            def _(e):
                e.memset(vaug[:], 1.0)
                for u in range(NU):
                    is_v = (u >= 16 and (u - 16) % 2 == 1)
                    if is_v:
                        g = (u - 16) // 2
                        if g >= 1:
                            e.wait_ge(m3t, 4 * g)
                        for lc in range(4):
                            idx = 4 * u + lc
                            e.wait_ge(m3p, idx + 1)
                            e.tensor_scalar(vtmp[:, lc * 512:(lc + 1) * 512],
                                            psP[:, (lc % 2) * 512:(lc % 2) * 512 + 512],
                                            bkv_s[:, (u - 16):(u - 15)], None,
                                            ALU.add).then_inc(e3p, 1)
                        for grp in range(4):
                            idx2 = 4 * g + grp
                            e.wait_ge(m3t, idx2 + 1)
                            e.tensor_copy(
                                vaug[:, g * 2064 + grp * 4 * 129: g * 2064 + (grp * 4 + 4) * 129]
                                .rearrange("p (t c) -> p t c", t=4)[:, :, 0:128],
                                psT[:, 0:512].rearrange("p (t c) -> p t c", t=4),
                            ).then_inc(v3a, 1)
                        continue
                    for lc in range(4):
                        idx = 4 * u + lc
                        e.wait_ge(m3p, idx + 1)
                        if u < 16:
                            dst = qT[:, u * L + lc * 512: u * L + (lc + 1) * 512]
                            bcol = bq_s[:, u:u + 1]
                        else:
                            g = (u - 16) // 2
                            dst = kT[:, g * L + lc * 512: g * L + (lc + 1) * 512]
                            bcol = bkv_s[:, (u - 16):(u - 15)]
                        e.scalar_tensor_tensor(
                            dst, psP[:, (lc % 2) * 512:(lc % 2) * 512 + 512], bcol,
                            pe_s[:, lc * 512:(lc + 1) * 512],
                            ALU.add, ALU.add).then_inc(e3p, 1)
                sc2 = 0
                pc2 = 0
                for h in range(NH):
                    for qc in range(NQC):
                        idx3 = h * NQC + qc
                        for kt in range(4 * qc + 4):
                            partial = (kt >= 4 * qc)
                            if partial:
                                e.wait_ge(e3x, sc2 + 1)
                                off = 384 - (kt - 4 * qc) * 128
                                e.tensor_tensor(
                                    expT[:, kt * 512:(kt + 1) * 512],
                                    expT[:, kt * 512:(kt + 1) * 512],
                                    tri_s[:, off: off + 512],
                                    ALU.mult).then_inc(v3m, 1)
                                pc2 += 1
                            sc2 += 1
                        for qs in range(4):
                            oc = idx3 * 4 + qs
                            e.wait_ge(m3o, oc + 1)
                            e.reciprocal(rcol[:, qs:qs + 1],
                                         psO[:, qs * 512 + 128: qs * 512 + 129]).then_inc(v3r, 1)
                        e.wait_ge(m3tt, idx3 + 1)
                        e.tensor_copy(qT[:, h * L + qc * 512: h * L + (qc + 1) * 512],
                                      psT[:, 0:512]).then_inc(v3q, 1)
                for m in range(16):
                    for lc in range(4):
                        widx = 4 * m + lc
                        e.wait_ge(m3wo, widx + 1)
                        xsl = xt[:, m * L + lc * 512: m * L + (lc + 1) * 512]
                        e.tensor_tensor(xsl, psP[:, (lc % 2) * 512:(lc % 2) * 512 + 512],
                                        xsl, ALU.add).then_inc(e3wo, 1)
                e.wait_ge(m3ln, 1)
                e.wait_ge(ln3k, 16)
                e.tensor_scalar(meanr3[:], psP[0:1, :], 1.0 / DM, None, ALU.mult)
                e.tensor_scalar(varr3[:], psO[0:1, :], 1.0 / DM, None, ALU.mult)
                e.tensor_tensor(tmpr3[:], meanr3[:], meanr3[:], ALU.mult)
                e.tensor_tensor(varr3[:], varr3[:], tmpr3[:], ALU.subtract).then_inc(st3, 1)
                e.wait_ge(bc3, 1)
                for k in range(NK):
                    xsl = xt[:, k * L:(k + 1) * L]
                    e.tensor_tensor(sq3[:], xsl, psP[:, :], ALU.subtract)
                    e.tensor_tensor(xsl, sq3[:], psO[:, :], ALU.mult).then_inc(app3, 1)
        _barrier(nc, "p3bar")


def prep_p3(inputs):
    t, d = L, 128
    pos = np.arange(t, dtype=np.float32)[:, None]
    i = np.arange(0, d, 2, dtype=np.float32)
    inv_freq = np.exp(-(np.log(10000.0) * i / d))
    ang = pos * inv_freq
    pe = np.zeros((t, d), np.float32)
    pe[:, 0::2] = np.sin(ang)
    pe[:, 1::2] = np.cos(ang)
    peT = np.ascontiguousarray(pe.T).astype(BF)
    tri = np.zeros((128, 896), BF)
    for kk in range(128):
        tri[kk, 384 + kk:] = 1.0
    return {
        "wq3": np.ascontiguousarray(
            inputs["L1_attn_wq"].reshape(NK, 128, 16, 128).transpose(2, 1, 0, 3).reshape(16 * 128, NK * 128)),
        "wkv3": np.ascontiguousarray(
            inputs["L1_attn_wkv"].reshape(NK, 128, 8, 128).transpose(2, 1, 0, 3).reshape(8 * 128, NK * 128)),
        "wo3": np.ascontiguousarray(
            inputs["L1_attn_wo"].reshape(NK, 128, 16, 128).transpose(2, 1, 0, 3).reshape(16 * 128, NK * 128)),
        "bq3": np.ascontiguousarray(inputs["L1_attn_bq"].reshape(16, 128).T),
        "bkv3": np.ascontiguousarray(inputs["L1_attn_bkv"].reshape(8, 128).T),
        "pe3": peT,
        "tri3": tri,
        "id3": np.eye(128, dtype=np.float32).astype(BF),
    }


NJOBS = 136
JPR = NJOBS // 8


def emit_p4a(nc, cx):
    jpk_d = nc.dram_tensor("jpk", [JPR * 128, 256], I32, kind="ExternalInput").ap()
    jsc_d = nc.dram_tensor("jsc", [128, JPR * 16], F16, kind="ExternalInput").ap()
    id4_d = nc.dram_tensor("id4", [128, 128], BF16, kind="ExternalInput").ap()
    scr_my = nc.dram_tensor("scr_my", [JPR * 128, 2048], BF16, kind="Internal")
    scr_all = nc.dram_tensor("scr_all", [NJOBS * 128, 2048], BF16, kind="Internal")
    cx.scr_all = scr_all.ap()
    with ExitStack() as stk:
        def sb(name, shape, dtype):
            return stk.enter_context(nc.sbuf_tensor(name, shape, dtype))

        def sem(name):
            return stk.enter_context(nc.semaphore(name))

        pk = [sb(f"pk{i}", [128, 512], U16) for i in range(2)]
        nib = sb("nib", [128, 2048], U16)
        tt1 = sb("tt1", [128, 2048], U16)
        tt2 = sb("tt2", [128, 2048], U16)
        bits = sb("bits", [128, 2048], U16)
        m01 = sb("m01", [128, 2048], U16)
        spc = sb("spc", [128, 2048], U16)
        vbits = sb("vbits", [128, 2048], U16)
        wbf = sb("wbf", [128, 2048], BF16)
        wout = [sb(f"wout{i}", [128, 2048], BF16) for i in range(2)]
        jsc_s = sb("jsc_s", [128, JPR * 16], F32)
        jsch = sb("jsch", [128, JPR * 16], F16)
        id4_s = sb("id4_s", [128, 128], BF16)
        psX = stk.enter_context(nc.psum_tensor("psX", [128, 2048], F32))
        s4l = sem("p4sl")
        v4x = sem("p4vx")
        v4d = sem("p4vd")
        m4t = sem("p4mt")
        v4e = sem("p4ve")
        w4o = sem("p4wo")
        c4l = sem("p4cl")
        vsc = sem("p4vsc")

        with nc.Block() as blk:
            @blk.sync
            def _(e):
                e.dma_start(out=jsch[:], in_=jsc_d).then_inc(c4l, 16)
                e.dma_start(out=id4_s[:], in_=id4_d).then_inc(c4l, 16)
                for j in range(JPR):
                    if j >= 2:
                        e.wait_ge(v4x, j - 1)
                    e.dma_start(out=pk[j % 2][:].bitcast(I32),
                                in_=jpk_d[j * 128:(j + 1) * 128, :]).then_inc(s4l, 16)

            @blk.scalar
            def _(e):
                for j in range(JPR):
                    e.wait_ge(v4e, 2 * j + 2)
                    e.dma_start(out=scr_my.ap()[j * 128:(j + 1) * 128, :],
                                in_=wout[j % 2][:]).then_inc(w4o, 16)
                e.wait_ge(w4o, 16 * JPR)

            @blk.vector
            def _(e):
                e.wait_ge(c4l, 32)
                e.tensor_copy(jsc_s[:], jsch[:]).then_inc(vsc, 1)
                for j in range(JPR):
                    nd = 2048
                    e.wait_ge(s4l, 16 * (j + 1))
                    for jj in range(4):
                        ins = e.tensor_scalar(nib[:, jj:nd:4], pk[j % 2][:],
                                              4 * jj, 15, ALU.logical_shift_right,
                                              ALU.bitwise_and)
                        if jj == 3:
                            ins.then_inc(v4x, 1)
                    e.tensor_scalar(tt1[:], nib[:], 8, 12,
                                    ALU.bitwise_and, ALU.logical_shift_left)
                    e.tensor_scalar(tt2[:], nib[:], 7, 6,
                                    ALU.bitwise_and, ALU.logical_shift_left)
                    e.tensor_tensor(bits[:], tt1[:], tt2[:], ALU.add)
                    e.tensor_scalar(bits[:], bits[:], 16128, None, ALU.add)
                    e.tensor_scalar(m01[:], nib[:], 6, None, ALU.bitwise_and)
                    e.tensor_scalar(m01[:], m01[:], 0, None, ALU.is_equal)
                    e.tensor_scalar(spc[:], nib[:], 1, None, ALU.bitwise_and)
                    e.tensor_scalar(spc[:], spc[:], 0x3F00, None, ALU.mult)
                    e.tensor_tensor(spc[:], spc[:], tt1[:], ALU.add)
                    if j >= 1:
                        e.wait_ge(m4t, 2 * j)
                    e.select(vbits[:], m01[:], spc[:], bits[:])
                    for gidx in range(16):
                        scol = jsc_s[:, j * 16 + gidx: j * 16 + gidx + 1]
                        ins = e.tensor_scalar(wbf[:, gidx * 128:(gidx + 1) * 128],
                                              vbits[:, gidx * 128:(gidx + 1) * 128].bitcast(BF16),
                                              scol, None, ALU.mult)
                        if gidx == 15:
                            ins.then_inc(v4d, 1)
                    for grp in range(2):
                        e.wait_ge(m4t, 2 * j + grp + 1)
                        if j >= 2:
                            e.wait_ge(w4o, 16 * (j - 1))
                        e.tensor_copy(
                            wout[j % 2][:, grp * 1024: grp * 1024 + 1024],
                            psX[:, grp * 512: grp * 512 + 512].bitcast(BF16),
                        ).then_inc(v4e, 1)

            @blk.tensor
            def _(e):
                e.wait_ge(c4l, 32)
                for j in range(JPR):
                    e.wait_ge(v4d, j + 1)
                    for grp in range(2):
                        if j >= 1:
                            e.wait_ge(v4e, 2 * j - 1 + grp)
                        for i in range(8):
                            gidx = grp * 8 + i
                            ins = e.transpose(
                                psX[:, grp * 512: grp * 512 + 512].bitcast(BF16)[:, i * 128:(i + 1) * 128],
                                wbf[:, gidx * 128:(gidx + 1) * 128],
                                id4_s[:])
                            if i == 7:
                                ins.then_inc(m4t, 1)

            @blk.gpsimd
            def _(e):
                e.wait_ge(w4o, 16 * JPR)
                e.collective_compute(
                    "AllGather",
                    ALU.bypass,
                    replica_groups=[list(range(8))],
                    ins=[scr_my.ap().opt()],
                    outs=[scr_all.ap().opt()],
                ).then_inc(cx.ccS, 1)
        _barrier(nc, "p4abar")


def prep_p4a(inputs):
    gup = inputs["L1_ffn_gate_up_packed"]
    dnp = inputs["L1_ffn_down_packed"]
    sgu = inputs["L1_ffn_gate_up_scales"]
    sdn = inputs["L1_ffn_down_scales"]
    jobs_pk = []
    jobs_sc = []
    for t in range(88):
        jobs_pk.append(gup[t * 128:(t + 1) * 128, :])
        jobs_sc.append(np.ascontiguousarray(sgu[:, t * 128:(t + 1) * 128]).T)
    for t in range(16):
        for c in range(3):
            blkp = np.zeros((128, 256), np.int32)
            w32 = min(256, 704 - c * 256)
            blkp[:, 0:w32] = dnp[t * 128:(t + 1) * 128, c * 256: c * 256 + w32]
            jobs_pk.append(blkp)
            blks = np.zeros((128, 16), np.float16)
            ngc = min(16, 44 - c * 16)
            blks[:, 0:ngc] = sdn[c * 16: c * 16 + ngc, t * 128:(t + 1) * 128].T
            jobs_sc.append(blks)
    per_core_pk = []
    per_core_sc = []
    for r in range(8):
        per_core_pk.append(np.ascontiguousarray(
            np.concatenate(jobs_pk[r * JPR:(r + 1) * JPR], axis=0)))
        per_core_sc.append(np.ascontiguousarray(
            np.concatenate(jobs_sc[r * JPR:(r + 1) * JPR], axis=1).astype(np.float16)))
    return {
        "jpk": per_core_pk,
        "jsc": per_core_sc,
        "id4": np.eye(128, dtype=np.float32).astype(BF),
    }


LB4 = 512
NB4 = L // LB4


def emit_p4b(nc, cx):
    scr_all = cx.scr_all
    xt = cx.xt
    onesc = cx.onesc
    onesr = cx.onesr
    cvals = cx.cvals
    with ExitStack() as stk:
        def sb(name, shape, dtype):
            return stk.enter_context(nc.sbuf_tensor(name, shape, dtype))

        def sem(name):
            return stk.enter_context(nc.semaphore(name))

        hact = sb("hact4", [128, NT * LB4], BF16)
        yblk = sb("yblk", [128, NK * LB4], BF16)
        g4 = [sb(f"g4{i}", [128, NK * 128], BF16) for i in range(2)]
        u4 = [sb(f"u4{i}", [128, NK * 128], BF16) for i in range(2)]
        dn4 = [sb(f"dn4{i}", [128, NT * 128], BF16) for i in range(2)]
        gact = sb("gact4", [128, LB4], BF16)
        uact = sb("uact4", [128, LB4], BF16)
        sq4 = sb("sq4", [128, LB4], BF16)
        tmp4 = sb("tmp4", [128, LB4], BF16)
        rmsrow = sb("rmsrow", [1, LB4], F32)
        meanr4 = sb("meanr4", [1, LB4], F32)
        varr4 = sb("varr4", [1, LB4], F32)
        tmpr4 = sb("tmpr4", [1, LB4], F32)
        psA4 = stk.enter_context(nc.psum_tensor("psA4", [128, 2048], F32))
        psB4 = stk.enter_context(nc.psum_tensor("psB4", [128, 2048], F32))
        l4g = sem("p4lg")
        m4 = sem("p4m")
        s4 = sem("p4s")
        e4 = sem("p4e")
        l4d = sem("p4ld2")
        m4d = sem("p4md")
        ev4d = sem("p4evd")
        sqy4s = sem("p4sqy")
        mr4k = sem("p4mrk")
        vr4 = sem("p4vr")
        rst4r = sem("p4rstr")
        bcr4 = sem("p4bcr")
        appr4 = sem("p4appr")
        sq4s = sem("p4sqs")
        ln4k = sem("p4lnk")
        m4ln = sem("p4mln")
        st4 = sem("p4st")
        rst4 = sem("p4rst")
        bc4 = sem("p4bc")
        app4 = sem("p4app")

        with nc.Block() as blk:
            @blk.sync
            def _(e):
                e.wait_ge(cx.ccS, 1)
                for B in range(NB4):
                    for t in range(NT):
                        g = B * NT + t
                        if g >= 2:
                            e.wait_ge(m4, 2 * (g - 2) + 2)
                        e.dma_start(out=g4[g % 2][:],
                                    in_=scr_all[t * 128:(t + 1) * 128, :]).then_inc(l4g, 16)
                        e.dma_start(out=u4[g % 2][:],
                                    in_=scr_all[(t + 44) * 128:(t + 45) * 128, :]).then_inc(l4g, 16)
                    for k in range(NK):
                        kg = B * NK + k
                        if kg >= 2:
                            e.wait_ge(m4d, kg - 1)
                        for cc in range(3):
                            jj = 88 + 3 * k + cc
                            wcols = 1536 if cc == 2 else 2048
                            e.dma_start(
                                out=dn4[k % 2][:, cc * 2048: cc * 2048 + wcols],
                                in_=scr_all[jj * 128:(jj + 1) * 128, 0:wcols]).then_inc(l4d, 16)

            @blk.tensor
            def _(e):
                for B in range(NB4):
                    for t in range(NT):
                        g = B * NT + t
                        e.wait_ge(l4g, 32 * (g + 1))
                        if g >= 2:
                            e.wait_ge(s4, 2 * (g - 2) + 1)
                        if t < 2 and B >= 1:
                            e.wait_ge(app4, B * NK)
                        for k in range(NK):
                            ins = e.matmul(
                                psA4[:, (g % 2) * 512:(g % 2) * 512 + 512],
                                g4[g % 2][:, k * 128:(k + 1) * 128],
                                xt[:, k * L + B * LB4: k * L + (B + 1) * LB4],
                                start=(k == 0), stop=(k == NK - 1))
                            if k == NK - 1:
                                ins.then_inc(m4, 1)
                        if g >= 2:
                            e.wait_ge(s4, 2 * (g - 2) + 2)
                        for k in range(NK):
                            ins = e.matmul(
                                psA4[:, 1024 + (g % 2) * 512: 1024 + (g % 2) * 512 + 512],
                                u4[g % 2][:, k * 128:(k + 1) * 128],
                                xt[:, k * L + B * LB4: k * L + (B + 1) * LB4],
                                start=(k == 0), stop=(k == NK - 1))
                            if k == NK - 1:
                                ins.then_inc(m4, 1)
                    for k in range(NK):
                        kg = B * NK + k
                        e.wait_ge(l4d, 48 * (kg + 1))
                        if k == 0:
                            e.wait_ge(e4, (B + 1) * NT)
                        if kg >= 2:
                            e.wait_ge(ev4d, kg - 1)
                        for kk in range(NT):
                            ins = e.matmul(
                                psB4[:, (k % 2) * 512:(k % 2) * 512 + 512],
                                dn4[k % 2][:, kk * 128:(kk + 1) * 128],
                                hact[:, kk * LB4:(kk + 1) * LB4],
                                start=(kk == 0), stop=(kk == NT - 1))
                            if kk == NT - 1:
                                ins.then_inc(m4d, 1)
                    for k in range(NK):
                        e.wait_ge(sqy4s, B * NK + k + 1)
                        ins = e.matmul(psB4[0:1, 1024:1536], onesc[:], sq4[:],
                                       start=(k == 0), stop=(k == NK - 1))
                        ins.then_inc(mr4k, 1)
                    e.wait_ge(rst4r, B + 1)
                    e.matmul(psB4[:, 1024:1536], onesr[:], rmsrow[:],
                             start=True, stop=True).then_inc(bcr4, 1)
                    for k in range(NK):
                        e.wait_ge(appr4, B * NK + k + 1)
                        e.wait_ge(sq4s, B * NK + k + 1)
                        ins1 = e.matmul(psB4[0:1, 1536:2048], onesc[:],
                                        xt[:, k * L + B * LB4: k * L + (B + 1) * LB4],
                                        start=(k == 0), stop=(k == NK - 1))
                        ins2 = e.matmul(psA4[0:1, 0:512], onesc[:], sq4[:],
                                        start=(k == 0), stop=(k == NK - 1))
                        ins2.then_inc(ln4k, 1)
                        if k == NK - 1:
                            ins1.then_inc(m4ln, 1)
                    e.wait_ge(rst4, B + 1)
                    e.matmul(psA4[:, 0:512], onesr[:], meanr4[:],
                             start=True, stop=True)
                    e.matmul(psA4[:, 512:1024], onesr[:], varr4[:],
                             start=True, stop=True).then_inc(bc4, 1)

            @blk.scalar
            def _(e):
                for B in range(NB4):
                    for t in range(NT):
                        g = B * NT + t
                        e.wait_ge(m4, 2 * g + 1)
                        if g >= 1:
                            e.wait_ge(e4, g)
                        e.activation(out=gact[:], in_=psA4[:, (g % 2) * 512:(g % 2) * 512 + 512],
                                     func=AF.Silu).then_inc(s4, 1)
                        e.wait_ge(m4, 2 * g + 2)
                        e.activation(out=uact[:], in_=psA4[:, 1024 + (g % 2) * 512: 1024 + (g % 2) * 512 + 512],
                                     func=AF.Copy).then_inc(s4, 1)
                    for k in range(NK):
                        e.wait_ge(ev4d, B * NK + k + 1)
                        if B * NK + k >= 1:
                            e.wait_ge(mr4k, B * NK + k)
                        e.activation(out=sq4[:], in_=yblk[:, k * LB4:(k + 1) * LB4],
                                     func=AF.Square).then_inc(sqy4s, 1)
                    e.wait_ge(vr4, B + 1)
                    e.activation(out=rmsrow[:], in_=rmsrow[:], func=AF.Ln, bias=cvals[0:1, 2:3])
                    e.activation(out=rmsrow[:], in_=rmsrow[:], func=AF.Exp,
                                 scale=cvals[0:1, 1:2]).then_inc(rst4r, 1)
                    for k in range(NK):
                        e.wait_ge(appr4, B * NK + k + 1)
                        if B * NK + k >= 1:
                            e.wait_ge(ln4k, B * NK + k)
                        e.activation(out=sq4[:], in_=xt[:, k * L + B * LB4: k * L + (B + 1) * LB4],
                                     func=AF.Square).then_inc(sq4s, 1)
                    e.wait_ge(st4, B + 1)
                    e.activation(out=varr4[:], in_=varr4[:], func=AF.Ln, bias=cvals[0:1, 0:1])
                    e.activation(out=varr4[:], in_=varr4[:], func=AF.Exp,
                                 scale=cvals[0:1, 1:2]).then_inc(rst4, 1)

            @blk.vector
            def _(e):
                for B in range(NB4):
                    for t in range(NT):
                        g = B * NT + t
                        e.wait_ge(s4, 2 * g + 2)
                        e.tensor_tensor(hact[:, t * LB4:(t + 1) * LB4], gact[:], uact[:],
                                        ALU.mult).then_inc(e4, 1)
                    for k in range(NK):
                        kg = B * NK + k
                        e.wait_ge(m4d, kg + 1)
                        e.tensor_copy(yblk[:, k * LB4:(k + 1) * LB4],
                                      psB4[:, (k % 2) * 512:(k % 2) * 512 + 512]).then_inc(ev4d, 1)
                    e.wait_ge(mr4k, (B + 1) * NK)
                    e.tensor_scalar(rmsrow[:], psB4[0:1, 1024:1536], 1.0 / DM, None,
                                    ALU.mult).then_inc(vr4, 1)
                    e.wait_ge(bcr4, B + 1)
                    for k in range(NK):
                        xsl = xt[:, k * L + B * LB4: k * L + (B + 1) * LB4]
                        e.tensor_tensor(tmp4[:], yblk[:, k * LB4:(k + 1) * LB4],
                                        psB4[:, 1024:1536], ALU.mult)
                        e.tensor_tensor(xsl, xsl, tmp4[:], ALU.add).then_inc(appr4, 1)
                    e.wait_ge(m4ln, B + 1)
                    e.wait_ge(ln4k, (B + 1) * NK)
                    e.tensor_scalar(meanr4[:], psB4[0:1, 1536:2048], 1.0 / DM, None, ALU.mult)
                    e.tensor_scalar(varr4[:], psA4[0:1, 0:512], 1.0 / DM, None, ALU.mult)
                    e.tensor_tensor(tmpr4[:], meanr4[:], meanr4[:], ALU.mult)
                    e.tensor_tensor(varr4[:], varr4[:], tmpr4[:], ALU.subtract).then_inc(st4, 1)
                    e.wait_ge(bc4, B + 1)
                    for k in range(NK):
                        xsl = xt[:, k * L + B * LB4: k * L + (B + 1) * LB4]
                        e.tensor_tensor(sq4[:], xsl, psA4[:, 0:512], ALU.subtract)
                        e.tensor_tensor(xsl, sq4[:], psA4[:, 512:1024], ALU.mult).then_inc(app4, 1)
        _barrier(nc, "p4bbar")


def prep_p4(inputs):
    sgu = inputs["L1_ffn_gate_up_scales"]
    sdn = inputs["L1_ffn_down_scales"]
    sgu_h = np.ascontiguousarray(
        sgu.T.reshape(88, 128, 16).transpose(1, 0, 2).reshape(128, 88 * 16))
    sdn_h = np.ascontiguousarray(
        sdn.T.reshape(16, 128, 44).transpose(1, 0, 2).reshape(128, 16 * 44))
    return {
        "gup": np.ascontiguousarray(inputs["L1_ffn_gate_up_packed"]),
        "dnp": np.ascontiguousarray(inputs["L1_ffn_down_packed"]),
        "sgu": sgu_h,
        "sdn": sdn_h,
        "id4": np.eye(128, dtype=np.float32).astype(BF),
    }


def prep_p4b(inputs):
    return {}


PHASE_EMIT = {"p1": emit_p1, "p2": emit_p2, "p3": emit_p3,
              "p4a": emit_p4a, "p4b": emit_p4b}
PHASE_PREP = {"p1": prep_p1, "p2": prep_p2, "p3": prep_p3,
              "p4a": prep_p4a, "p4b": prep_p4b}

PLAN = [("p4a", "p1", "p2", "p3", "p4b")]


def build_launch(phases):
    nc = bass.Bass("TRN2", target_bir_lowering=False, debug=False)
    nc._stk = ExitStack()
    xt_in = nc.dram_tensor("xt_in", [128, NK * L], F32, kind="ExternalInput").ap()
    onesc_d = nc.dram_tensor("onesc", [128, 1], BF16, kind="ExternalInput").ap()
    onesr_d = nc.dram_tensor("onesr", [1, 128], F32, kind="ExternalInput").ap()
    xt_out = nc.dram_tensor("xt_out", [128, NK * L], F32, kind="ExternalOutput").ap()
    with nc._stk:
        cx = Ctx()
        cx.xt = nc._stk.enter_context(nc.sbuf_tensor("xt", [128, NK * L], BF16))
        cx.onesc = nc._stk.enter_context(nc.sbuf_tensor("onesc_sb", [128, 1], BF16))
        cx.onesr = nc._stk.enter_context(nc.sbuf_tensor("onesr_sb", [1, 128], F32))
        cx.cvals = nc._stk.enter_context(nc.sbuf_tensor("cvals", [128, 4], F32))
        cx.ccS = nc._stk.enter_context(nc.semaphore("ccS"))
        hld = nc._stk.enter_context(nc.semaphore("hld"))
        hldg = nc._stk.enter_context(nc.semaphore("hldg"))
        cx.hldg = hldg
        with nc.Block() as blk:
            @blk.sync
            def _(e):
                e.dma_start(out=cx.onesc[:], in_=onesc_d).then_inc(hld, 16)
                e.dma_start(out=cx.onesr[:], in_=onesr_d).then_inc(hld, 16)
                e.wait_ge(hld, 32)

            @blk.gpsimd
            def _(e):
                for j in range(4):
                    e.dma_start(
                        out=cx.xt[:, j * 4 * L:(j + 1) * 4 * L],
                        in_=xt_in[:, j * 4 * L:(j + 1) * 4 * L],
                    ).then_inc(hldg, 16)
                e.wait_ge(hld, 32)

            @blk.vector
            def _(e):
                e.memset(cx.cvals[:, 0:1], 1e-5)
                e.memset(cx.cvals[:, 1:2], -0.5)
                e.memset(cx.cvals[:, 2:3], 1e-6)
                e.memset(cx.cvals[:, 3:4], 0.08838834764831845)
        _barrier(nc, "hbar")
        for ph in phases:
            PHASE_EMIT[ph](nc, cx)
        outS = nc._stk.enter_context(nc.semaphore("outS"))
        with nc.Block() as blk:
            @blk.gpsimd
            def _(e):
                for k in range(NK):
                    e.dma_start(out=xt_out[:, k * L:(k + 1) * L],
                                in_=cx.xt[:, k * L:(k + 1) * L]).then_inc(outS, 16)
                e.wait_ge(outS, 16 * NK)
    return nc


_BUILD_CACHE = {}


def kernel(**inputs):
    x = inputs["L0_attn_x"]
    out = np.empty((8, 2048, 2048), np.float32)
    def tile_xt(a):
        return np.ascontiguousarray(a.T.reshape(NK, 128, L).transpose(1, 0, 2).reshape(128, NK * L))

    def untile_xt(a):
        return a.reshape(128, NK, L).transpose(1, 0, 2).reshape(DM, L)

    cur = [tile_xt(x[b]) for b in range(8)]
    consts = {"onesc": np.ones((128, 1), BF), "onesr": np.ones((1, 128), np.float32)}
    for phases in PLAN:
        if phases not in _BUILD_CACHE:
            _BUILD_CACHE[phases] = build_launch(phases)
        nc = _BUILD_CACHE[phases]
        common = dict(consts)
        for ph in phases:
            common.update(PHASE_PREP[ph](inputs))
        in_maps = []
        for b in range(8):
            m = {}
            for kk, vv in common.items():
                m[kk] = vv[b] if isinstance(vv, list) else vv
            m["xt_in"] = cur[b]
            in_maps.append(m)
        res = run_bass_kernel_spmd(nc, in_maps, core_ids=list(range(8)))
        cur = [res.results[b]["xt_out"] for b in range(8)]
    for b in range(8):
        out[b] = untile_xt(cur[b]).T
    return out


# version 10
# speedup vs baseline: 1.1212x; 1.0041x over previous
import numpy as np
import ml_dtypes
from contextlib import ExitStack
import concourse.bass as bass
import concourse.mybir as mybir
from concourse.bass_utils import run_bass_kernel_spmd

F32 = mybir.dt.float32
BF16 = mybir.dt.bfloat16
F16 = mybir.dt.float16
I32 = mybir.dt.int32
U8 = mybir.dt.uint8
U16 = mybir.dt.uint16
AF = mybir.ActivationFunctionType
ALU = mybir.AluOpType
BF = ml_dtypes.bfloat16

L = 2048
DM = 2048
NK = 16
NS = 8
NLH = 2
LHALF = L // NLH
HID = 5632
NT = HID // 128
LB = 1024
NB = L // LB


class Ctx:
    pass


def _barrier(nc, name):
    ph = nc._stk.enter_context(nc.semaphore(name))
    with nc.Block() as blk:
        def mk(e):
            e.drain().then_inc(ph, 1)
            e.wait_ge(ph, 5)
        blk.sync(mk)
        blk.scalar(mk)
        blk.vector(mk)
        blk.tensor(mk)
        blk.gpsimd(mk)
    return ph


def emit_p1(nc, cx):
    wx = nc.dram_tensor("wx", [128, NK * 272], F32, kind="ExternalInput").ap()
    wdt = nc.dram_tensor("wdt", [128, 2 * DM], F32, kind="ExternalInput").ap()
    bdt_d = nc.dram_tensor("bdt", [128, NK], F32, kind="ExternalInput").ap()
    alog_d = nc.dram_tensor("alog", [128, NK * NS], F32, kind="ExternalInput").ap()
    dpar_d = nc.dram_tensor("dpar", [128, NK], F32, kind="ExternalInput").ap()
    sel_d = nc.dram_tensor("sel", [16, 2048], BF16, kind="ExternalInput").ap()
    xt = cx.xt
    onesc = cx.onesc
    onesr = cx.onesr
    cvals = cx.cvals
    with ExitStack() as stk:
        def sb(name, shape, dtype):
            return stk.enter_context(nc.sbuf_tensor(name, shape, dtype))

        def sem(name):
            return stk.enter_context(nc.semaphore(name))

        wx_s = sb("wx_s", [128, NK * 272], BF16)
        wdt_s = sb("wdt_s", [128, 2 * DM], BF16)
        bdt_s = sb("bdt_s", [128, NK], F32)
        alog_s = sb("alog_s", [128, NK * NS], F32)
        negA = sb("negA", [128, NK * NS], F32)
        dpar_s = sb("dpar_s", [128, NK], F32)
        dp1 = sb("dp1", [128, NK], F32)
        sel_s = sb("sel_s", [16, 2048], BF16)
        dtr = sb("dtr", [128, 2 * L], BF16)
        bc16 = sb("bc16", [16, L], BF16)
        bcb = sb("bcb", [128, NS * LHALF], BF16)
        ccb = sb("ccb", [128, NS * LHALF], BF16)
        zz = sb("zz", [128, LHALF], F32)
        sp1 = sb("sp1", [128, LHALF], F32)
        sp2 = sb("sp2", [128, LHALF], F32)
        dlt = sb("dlt", [128, LHALF], F32)
        uu = sb("uu", [128, LHALF], BF16)
        dbu = sb("dbu", [128, LHALF], BF16)
        hh = sb("hh", [128, LHALF], BF16)
        tmp = sb("tmp", [128, LHALF], BF16)
        yy = sb("yy", [128, LHALF], BF16)
        da0 = sb("da0", [128, LHALF], BF16)
        da1 = sb("da1", [128, LHALF], BF16)
        state = sb("state", [128, 128], F32)
        sq = sb("sq", [128, L], BF16)
        meanr = sb("meanr", [1, L], F32)
        varr = sb("varr", [1, L], F32)
        tmpr = sb("tmpr", [1, L], F32)
        psA = stk.enter_context(nc.psum_tensor("psA", [128, 2048], F32))
        psB = stk.enter_context(nc.psum_tensor("psB", [128, 2048], F32))
        ld = sem("p1ld")
        ldg = sem("p1ldg")
        mmA = sem("p1mmA")
        evA = sem("p1evA")
        evBC = sem("p1evBC")
        zk = sem("p1zk")
        sp = sem("p1sp")
        da = sem("p1da")
        hc = sem("p1hc")
        dl = sem("p1dl")
        res = sem("p1res")
        sqs = sem("p1sqs")
        lnk = sem("p1lnk")
        st_ = sem("p1st")
        rst = sem("p1rst")
        app = sem("p1app")
        da_bufs = [da0, da1]

        with nc.Block() as blk:
            @blk.sync
            def _(e):
                e.dma_start(out=bdt_s[:], in_=bdt_d).then_inc(ld, 16)
                e.dma_start(out=alog_s[:], in_=alog_d).then_inc(ld, 16)
                e.dma_start(out=dpar_s[:], in_=dpar_d).then_inc(ld, 16)
                e.dma_start(out=sel_s[:], in_=sel_d).then_inc(ld, 16)

            @blk.gpsimd
            def _(e):
                e.dma_start(out=wx_s[:], in_=wx).then_inc(ldg, 16)
                e.dma_start(out=wdt_s[:], in_=wdt).then_inc(ldg, 16)

            @blk.tensor
            def _(e):
                e.wait_ge(ld, 64)
                e.wait_ge(ldg, 32)
                e.wait_ge(cx.hldg, 64)
                for c in range(3):
                    mwid = 128 if c < 2 else 16
                    if c >= 1:
                        e.wait_ge(evA, 4 * c)
                    for k in range(NK):
                        for lc in range(4):
                            ins = e.matmul(
                                psA[0:mwid, lc * 512:(lc + 1) * 512],
                                wx_s[:, k * 272 + c * 128: k * 272 + c * 128 + mwid],
                                xt[:, k * L + lc * 512: k * L + (lc + 1) * 512],
                                start=(k == 0), stop=(k == NK - 1),
                            )
                            if k == NK - 1:
                                ins.then_inc(mmA, 1)
                for lh in range(NLH):
                    for n in range(16):
                        idx = lh * 16 + n
                        if idx - 1 >= 1:
                            e.wait_ge(evBC, idx - 1)
                        if idx == 0:
                            e.wait_ge(evA, 12)
                        if n == 0 and lh >= 1:
                            e.wait_ge(zk, lh * NK)
                        ps = psA if (n % 2 == 0) else psB
                        for lc in range(2):
                            ins = e.matmul(
                                ps[:, lc * 512:(lc + 1) * 512],
                                sel_s[:, n * 128:(n + 1) * 128],
                                bc16[:, lh * LHALF + lc * 512: lh * LHALF + (lc + 1) * 512],
                                start=True, stop=True,
                            )
                            if lc == 1:
                                ins.then_inc(mmA, 1)
                    for k in range(NK):
                        gidx = lh * NK + k
                        if gidx == 0:
                            e.wait_ge(evBC, 16)
                        if gidx >= 1:
                            e.wait_ge(zk, gidx)
                        if gidx == NK:
                            e.wait_ge(evBC, 32)
                        for r in range(2):
                            for lc in range(2):
                                ins = e.matmul(
                                    psA[:, lc * 512:(lc + 1) * 512],
                                    wdt_s[:, r * DM + k * 128: r * DM + (k + 1) * 128],
                                    dtr[:, r * L + lh * LHALF + lc * 512: r * L + lh * LHALF + (lc + 1) * 512],
                                    start=(r == 0), stop=(r == 1),
                                )
                                if lc == 1 and r == 1:
                                    ins.then_inc(mmA, 1)
                e.wait_ge(zk, 32)
                for k in range(NK):
                    e.wait_ge(res, NK + k + 1)
                    e.wait_ge(sqs, k + 1)
                    for lc in range(4):
                        ins1 = e.matmul(psA[0:1, lc * 512:(lc + 1) * 512], onesc[:],
                                        xt[:, k * L + lc * 512: k * L + (lc + 1) * 512],
                                        start=(k == 0), stop=(k == NK - 1))
                        ins2 = e.matmul(psB[0:1, lc * 512:(lc + 1) * 512], onesc[:],
                                        sq[:, lc * 512:(lc + 1) * 512],
                                        start=(k == 0), stop=(k == NK - 1))
                        if lc == 3:
                            ins2.then_inc(lnk, 1)
                        if k == NK - 1 and lc == 3:
                            ins1.then_inc(mmA, 1)
                e.wait_ge(rst, 1)
                for lc in range(4):
                    e.matmul(psA[:, lc * 512:(lc + 1) * 512], onesr[:],
                             meanr[:, lc * 512:(lc + 1) * 512], start=True, stop=True)
                    ins = e.matmul(psB[:, lc * 512:(lc + 1) * 512], onesr[:],
                                   varr[:, lc * 512:(lc + 1) * 512], start=True, stop=True)
                    if lc == 3:
                        ins.then_inc(mmA, 1)

            @blk.vector
            def _(e):
                e.wait_ge(cx.hldg, 64)
                for c in range(3):
                    for lc in range(4):
                        i = c * 4 + lc
                        e.wait_ge(mmA, i + 1)
                        if c < 2:
                            e.tensor_copy(dtr[:, c * L + lc * 512: c * L + (lc + 1) * 512],
                                          psA[:, lc * 512:(lc + 1) * 512]).then_inc(evA, 1)
                        else:
                            e.tensor_copy(bc16[:, lc * 512:(lc + 1) * 512],
                                          psA[0:16, lc * 512:(lc + 1) * 512]).then_inc(evA, 1)
                for lh in range(NLH):
                    for n in range(16):
                        e.wait_ge(mmA, 12 + lh * 32 + n + 1)
                        ps = psA if (n % 2 == 0) else psB
                        dstt = bcb if n < 8 else ccb
                        nn = n % 8
                        e.tensor_copy(dstt[:, nn * LHALF:(nn + 1) * LHALF],
                                      ps[:, 0:1024]).then_inc(evBC, 1)
                    for k in range(NK):
                        gidx = lh * NK + k
                        e.wait_ge(mmA, 12 + lh * 32 + 16 + k + 1)
                        e.tensor_scalar(zz[:], psA[:, 0:1024], bdt_s[:, k:k + 1], None,
                                        ALU.add).then_inc(zk, 1)
                        e.wait_ge(sp, gidx + 1)
                        e.scalar_tensor_tensor(dlt[:], zz[:], 0.0, sp1[:],
                                               ALU.max, ALU.add).then_inc(dl, 1)
                        e.tensor_tensor(uu[:], dlt[:],
                                        xt[:, k * L + lh * LHALF: k * L + lh * LHALF + LHALF],
                                        ALU.mult)
                        for n in range(NS):
                            g = gidx * NS + n
                            e.tensor_tensor(dbu[:], uu[:],
                                            bcb[:, n * LHALF:(n + 1) * LHALF], ALU.mult)
                            e.wait_ge(da, g + 1)
                            init = 0.0 if lh == 0 else state[:, (k * NS + n):(k * NS + n + 1)]
                            e.tensor_tensor_scan(hh[:], da_bufs[n % 2][:], dbu[:], init,
                                                 ALU.mult, ALU.add).then_inc(hc, 1)
                            if n == 0:
                                e.tensor_tensor(yy[:], hh[:],
                                                ccb[:, n * LHALF:(n + 1) * LHALF], ALU.mult)
                            else:
                                e.tensor_tensor(tmp[:], hh[:],
                                                ccb[:, n * LHALF:(n + 1) * LHALF], ALU.mult)
                                e.tensor_tensor(yy[:], yy[:], tmp[:], ALU.add)
                            if lh == 0:
                                e.tensor_copy(state[:, (k * NS + n):(k * NS + n + 1)],
                                              hh[:, LHALF - 1:LHALF])
                        xsl = xt[:, k * L + lh * LHALF: k * L + lh * LHALF + LHALF]
                        e.scalar_tensor_tensor(xsl, xsl, dp1[:, k:k + 1], yy[:],
                                               ALU.mult, ALU.add).then_inc(res, 1)
                e.wait_ge(mmA, 77)
                e.wait_ge(lnk, 16)
                e.tensor_scalar(meanr[:], psA[0:1, :], 1.0 / DM, None, ALU.mult)
                e.tensor_scalar(varr[:], psB[0:1, :], 1.0 / DM, None, ALU.mult)
                e.tensor_tensor(tmpr[:], meanr[:], meanr[:], ALU.mult)
                e.tensor_tensor(varr[:], varr[:], tmpr[:], ALU.subtract).then_inc(st_, 1)
                e.wait_ge(mmA, 78)
                for k in range(NK):
                    xsl = xt[:, k * L:(k + 1) * L]
                    e.tensor_tensor(sq[:], xsl, psA[:, :], ALU.subtract)
                    e.tensor_tensor(xsl, sq[:], psB[:, :], ALU.mult).then_inc(app, 1)

            @blk.scalar
            def _(e):
                e.wait_ge(ld, 64)
                e.wait_ge(ldg, 32)
                e.wait_ge(cx.hldg, 64)
                e.activation(out=negA[:], in_=alog_s[:], func=AF.Exp)
                e.mul(negA[:], negA[:], -1.0)
                e.add(dp1[:], dpar_s[:], 1.0)
                for lh in range(NLH):
                    for k in range(NK):
                        gidx = lh * NK + k
                        e.wait_ge(zk, gidx + 1)
                        e.activation(out=sp2[:], in_=zz[:], func=AF.Abs)
                        e.activation(out=sp1[:], in_=sp2[:], func=AF.Exp, scale=-1.0)
                        e.activation(out=sp1[:], in_=sp1[:], func=AF.Ln,
                                     bias=1.0).then_inc(sp, 1)
                        e.wait_ge(dl, gidx + 1)
                        for n in range(NS):
                            g = gidx * NS + n
                            if g - 1 >= 1:
                                e.wait_ge(hc, g - 1)
                            e.activation(out=da_bufs[n % 2][:], in_=dlt[:], func=AF.Exp,
                                         scale=negA[:, (k * NS + n):(k * NS + n + 1)]
                                         ).then_inc(da, 1)
                for k in range(NK):
                    e.wait_ge(res, NK + k + 1)
                    if k >= 1:
                        e.wait_ge(lnk, k)
                    e.activation(out=sq[:], in_=xt[:, k * L:(k + 1) * L],
                                 func=AF.Square).then_inc(sqs, 1)
                e.wait_ge(st_, 1)
                e.activation(out=varr[:], in_=varr[:], func=AF.Ln, bias=cvals[0:1, 0:1])
                e.activation(out=varr[:], in_=varr[:], func=AF.Exp,
                             scale=cvals[0:1, 1:2]).then_inc(rst, 1)
        _barrier(nc, "p1bar")


def prep_p1(inputs):
    sel = np.zeros((16, 2048), BF)
    for n in range(16):
        sel[n, n * 128:(n + 1) * 128] = 1.0
    return {
        "wx": np.ascontiguousarray(
            inputs["L0_attn_W_xproj"].T.reshape(NK, 128, 272).transpose(1, 0, 2).reshape(128, NK * 272)),
        "wdt": np.ascontiguousarray(
            inputs["L0_attn_W_dt"].T.reshape(2, 128, DM).transpose(1, 0, 2).reshape(128, 2 * DM)),
        "bdt": np.ascontiguousarray(inputs["L0_attn_b_dt"].reshape(NK, 128).T),
        "alog": np.ascontiguousarray(
            inputs["L0_attn_A_log"].reshape(NK, 128, NS).transpose(1, 0, 2).reshape(128, NK * NS)),
        "dpar": np.ascontiguousarray(inputs["L0_attn_D_param"].reshape(NK, 128).T),
        "sel": sel,
    }


def emit_p2(nc, cx):
    wgT = nc.dram_tensor("wgT", [NT * 128, NK * 128], I32, kind="ExternalInput").ap()
    wuT = nc.dram_tensor("wuT", [NT * 128, NK * 128], I32, kind="ExternalInput").ap()
    wdT = nc.dram_tensor("wdT", [NK * 128, NT * 128], I32, kind="ExternalInput").ap()
    sg_d = nc.dram_tensor("sg", [128, NT], F32, kind="ExternalInput").ap()
    su_d = nc.dram_tensor("su", [128, NT], F32, kind="ExternalInput").ap()
    sd_d = nc.dram_tensor("sd", [128, NK], F32, kind="ExternalInput").ap()
    xt = cx.xt
    onesc = cx.onesc
    onesr = cx.onesr
    cvals = cx.cvals
    with ExitStack() as stk:
        def sb(name, shape, dtype):
            return stk.enter_context(nc.sbuf_tensor(name, shape, dtype))

        def sem(name):
            return stk.enter_context(nc.semaphore(name))

        hact = sb("hact", [128, NT * LB], BF16)
        wg_b = [sb(f"wg{i}", [128, NK * 128], BF16) for i in range(2)]
        wu_b = [sb(f"wu{i}", [128, NK * 128], BF16) for i in range(2)]
        wd_b = [sb(f"wd{i}", [128, NT * 128], BF16) for i in range(2)]
        gact = sb("gact", [128, LB], BF16)
        uact = sb("uact", [128, LB], BF16)
        sg_s = sb("sg_s", [128, NT], F32)
        su_s = sb("su_s", [128, NT], F32)
        sd_s = sb("sd_s", [128, NK], F32)
        sq2 = sb("sq2", [128, LB], BF16)
        at2 = sb("at2", [128, LB], BF16)
        meanr2 = hact[0:1, 0:2048].bitcast(F32)
        varr2 = hact[0:1, 2048:4096].bitcast(F32)
        tmpr2 = hact[0:1, 4096:6144].bitcast(F32)
        psG0 = stk.enter_context(nc.psum_tensor("psG0", [128, LB], F32))
        psG1 = stk.enter_context(nc.psum_tensor("psG1", [128, LB], F32))
        psU = stk.enter_context(nc.psum_tensor("psU", [128, LB], F32))
        psD = stk.enter_context(nc.psum_tensor("psD", [128, LB], F32))
        psG = [psG0, psG1]
        psUp = [psU, psD]
        ld2 = sem("p2ld")
        wr2 = sem("p2wr")
        m2 = sem("p2m")
        s2 = sem("p2s")
        e2 = sem("p2e")
        ldd = sem("p2ldd")
        wrd = sem("p2wrd")
        m2d = sem("p2md")
        ev2d = sem("p2evd")
        sq2s = sem("p2sqs")
        lnk2 = sem("p2lnk")
        m2ln = sem("p2mln")
        st2 = sem("p2st")
        rst2 = sem("p2rst")
        bc2 = sem("p2bc")
        app2 = sem("p2app")
        ld0 = sem("p2ld0")

        with nc.Block() as blk:
            @blk.sync
            def _(e):
                e.dma_start(out=sg_s[:], in_=sg_d).then_inc(ld0, 16)
                e.dma_start(out=su_s[:], in_=su_d).then_inc(ld0, 16)
                e.dma_start(out=sd_s[:], in_=sd_d).then_inc(ld0, 16)

            @blk.gpsimd
            def _(e):
                for B in range(NB):
                    for t in range(NT):
                        g = B * NT + t
                        if g >= 2:
                            e.wait_ge(m2, 2 * (g - 2) + 2)
                        e.dma_start(out=wg_b[g % 2][:],
                                    in_=wgT[t * 128:(t + 1) * 128, :]).then_inc(ld2, 16)
                        e.dma_start(out=wu_b[g % 2][:],
                                    in_=wuT[t * 128:(t + 1) * 128, :]).then_inc(ld2, 16)
                    for k in range(NK):
                        kg = B * NK + k
                        if kg >= 2:
                            e.wait_ge(m2d, kg - 1)
                        e.dma_start(out=wd_b[k % 2][:],
                                    in_=wdT[k * 128:(k + 1) * 128, :]).then_inc(ldd, 16)

            @blk.tensor
            def _(e):
                for B in range(NB):
                    for t in range(NT):
                        g = B * NT + t
                        e.wait_ge(wr2, 2 * g + 1)
                        if g >= 2:
                            e.wait_ge(s2, 2 * (g - 2) + 1)
                        if t < 2 and B >= 1:
                            e.wait_ge(app2, B * NK)
                        for k in range(NK):
                            for lc in range(2):
                                ins = e.matmul(
                                    psG[g % 2][:, lc * 512:(lc + 1) * 512],
                                    wg_b[g % 2][:, k * 128:(k + 1) * 128],
                                    xt[:, k * L + B * LB + lc * 512: k * L + B * LB + (lc + 1) * 512],
                                    start=(k == 0), stop=(k == NK - 1))
                                if lc == 1 and k == NK - 1:
                                    ins.then_inc(m2, 1)
                        e.wait_ge(wr2, 2 * g + 2)
                        if g >= 2:
                            e.wait_ge(s2, 2 * (g - 2) + 2)
                        for k in range(NK):
                            for lc in range(2):
                                ins = e.matmul(
                                    psUp[g % 2][:, lc * 512:(lc + 1) * 512],
                                    wu_b[g % 2][:, k * 128:(k + 1) * 128],
                                    xt[:, k * L + B * LB + lc * 512: k * L + B * LB + (lc + 1) * 512],
                                    start=(k == 0), stop=(k == NK - 1))
                                if lc == 1 and k == NK - 1:
                                    ins.then_inc(m2, 1)
                    for k in range(NK):
                        kg = B * NK + k
                        e.wait_ge(wrd, kg + 1)
                        if k == 0:
                            e.wait_ge(e2, (B + 1) * NT)
                        e.wait_ge(ev2d, kg)
                        for kk in range(NT):
                            for lc in range(2):
                                ins = e.matmul(
                                    psD[:, lc * 512:(lc + 1) * 512],
                                    wd_b[k % 2][:, kk * 128:(kk + 1) * 128],
                                    hact[:, kk * LB + lc * 512: kk * LB + (lc + 1) * 512],
                                    start=(kk == 0), stop=(kk == NT - 1))
                                if lc == 1 and kk == NT - 1:
                                    ins.then_inc(m2d, 1)
                    if B >= 1:
                        e.wait_ge(app2, B * NK)
                    for k in range(NK):
                        e.wait_ge(ev2d, B * NK + k + 1)
                        e.wait_ge(sq2s, B * NK + k + 1)
                        for lc in range(2):
                            ins1 = e.matmul(psG0[0:1, lc * 512:(lc + 1) * 512], onesc[:],
                                            xt[:, k * L + B * LB + lc * 512: k * L + B * LB + (lc + 1) * 512],
                                            start=(k == 0), stop=(k == NK - 1))
                            ins2 = e.matmul(psG1[0:1, lc * 512:(lc + 1) * 512], onesc[:],
                                            sq2[:, lc * 512:(lc + 1) * 512],
                                            start=(k == 0), stop=(k == NK - 1))
                            if lc == 1:
                                ins2.then_inc(lnk2, 1)
                            if k == NK - 1 and lc == 1:
                                ins1.then_inc(m2ln, 1)
                    e.wait_ge(rst2, B + 1)
                    for lc in range(2):
                        e.matmul(psG0[:, lc * 512:(lc + 1) * 512], onesr[:],
                                 meanr2[:, lc * 512:(lc + 1) * 512], start=True, stop=True)
                        ins = e.matmul(psG1[:, lc * 512:(lc + 1) * 512], onesr[:],
                                       varr2[:, lc * 512:(lc + 1) * 512], start=True, stop=True)
                        if lc == 1:
                            ins.then_inc(bc2, 1)

            @blk.scalar
            def _(e):
                e.wait_ge(ld0, 48)
                for B in range(NB):
                    for t in range(NT):
                        g = B * NT + t
                        e.wait_ge(m2, 2 * g + 1)
                        if g >= 1:
                            e.wait_ge(e2, g)
                        e.activation(out=gact[:], in_=psG[g % 2][:, :], func=AF.Silu,
                                     scale=sg_s[:, t:t + 1]).then_inc(s2, 1)
                        e.wait_ge(m2, 2 * g + 2)
                        e.activation(out=uact[:], in_=psUp[g % 2][:, :], func=AF.Copy,
                                     scale=su_s[:, t:t + 1]).then_inc(s2, 1)
                    for k in range(NK):
                        e.wait_ge(ev2d, B * NK + k + 1)
                        if B * NK + k >= 1:
                            e.wait_ge(lnk2, B * NK + k)
                        e.activation(out=sq2[:],
                                     in_=xt[:, k * L + B * LB: k * L + B * LB + LB],
                                     func=AF.Square).then_inc(sq2s, 1)
                    e.wait_ge(st2, B + 1)
                    e.activation(out=varr2[:], in_=varr2[:], func=AF.Ln, bias=cvals[0:1, 0:1])
                    e.activation(out=varr2[:], in_=varr2[:], func=AF.Exp,
                                 scale=cvals[0:1, 1:2]).then_inc(rst2, 1)

            @blk.vector
            def _(e):
                def prep_gu(gg):
                    e.wait_ge(ld2, 32 * (gg + 1))
                    e.tensor_scalar(wg_b[gg % 2][:], wg_b[gg % 2][:], -128.0, None,
                                    ALU.add).then_inc(wr2, 1)
                    e.tensor_scalar(wu_b[gg % 2][:], wu_b[gg % 2][:], -128.0, None,
                                    ALU.add).then_inc(wr2, 1)
                for B in range(NB):
                    prep_gu(B * NT)
                    prep_gu(B * NT + 1)
                    for t in range(NT):
                        g = B * NT + t
                        if t + 2 < NT:
                            prep_gu(g + 2)
                        e.wait_ge(s2, 2 * g + 2)
                        e.tensor_tensor(hact[:, t * LB:(t + 1) * LB], gact[:], uact[:],
                                        ALU.mult).then_inc(e2, 1)
                    for k in range(NK):
                        kg = B * NK + k
                        e.wait_ge(ldd, 16 * (kg + 1))
                        e.tensor_scalar(wd_b[k % 2][:], wd_b[k % 2][:], -128.0, None,
                                        ALU.add).then_inc(wrd, 1)
                        e.wait_ge(m2d, kg + 1)
                        xsl = xt[:, k * L + B * LB: k * L + B * LB + LB]
                        e.scalar_tensor_tensor(xsl, psD[:, :], sd_s[:, k:k + 1], xsl,
                                               ALU.mult, ALU.add).then_inc(ev2d, 1)
                    e.wait_ge(m2ln, B + 1)
                    e.wait_ge(lnk2, (B + 1) * NK)
                    e.tensor_scalar(meanr2[:], psG0[0:1, :], 1.0 / DM, None, ALU.mult)
                    e.tensor_scalar(varr2[:], psG1[0:1, :], 1.0 / DM, None, ALU.mult)
                    e.tensor_tensor(tmpr2[:], meanr2[:], meanr2[:], ALU.mult)
                    e.tensor_tensor(varr2[:], varr2[:], tmpr2[:], ALU.subtract).then_inc(st2, 1)
                    e.wait_ge(bc2, B + 1)
                    for k in range(NK):
                        xsl = xt[:, k * L + B * LB: k * L + B * LB + LB]
                        e.tensor_tensor(at2[:], xsl, psG0[:, :], ALU.subtract)
                        e.tensor_tensor(xsl, at2[:], psG1[:, :], ALU.mult).then_inc(app2, 1)
        _barrier(nc, "p2bar")


def prep_p2(inputs):
    return {
        "wgT": np.ascontiguousarray(
            inputs["L0_ffn_wq_gate"].reshape(NT, 128, NK, 128).transpose(0, 3, 2, 1).reshape(NT * 128, NK * 128)),
        "wuT": np.ascontiguousarray(
            inputs["L0_ffn_wq_up"].reshape(NT, 128, NK, 128).transpose(0, 3, 2, 1).reshape(NT * 128, NK * 128)),
        "wdT": np.ascontiguousarray(
            inputs["L0_ffn_wq_down"].reshape(NK, 128, NT, 128).transpose(0, 3, 2, 1).reshape(NK * 128, NT * 128)),
        "sg": np.ascontiguousarray(inputs["L0_ffn_s_gate"].reshape(NT, 128).T),
        "su": np.ascontiguousarray(inputs["L0_ffn_s_up"].reshape(NT, 128).T),
        "sd": np.ascontiguousarray(inputs["L0_ffn_s_down"].reshape(NK, 128).T),
    }


def emit_p3(nc, cx):
    wq3 = nc.dram_tensor("wq3", [16 * 128, NK * 128], F32, kind="ExternalInput").ap()
    wkv3 = nc.dram_tensor("wkv3", [8 * 128, NK * 128], F32, kind="ExternalInput").ap()
    wo3 = nc.dram_tensor("wo3", [16 * 128, NK * 128], F32, kind="ExternalInput").ap()
    bq_d = nc.dram_tensor("bq3", [128, 16], F32, kind="ExternalInput").ap()
    bkv_d = nc.dram_tensor("bkv3", [128, 8], F32, kind="ExternalInput").ap()
    pe_d = nc.dram_tensor("pe3", [128, L], BF16, kind="ExternalInput").ap()
    tri_d = nc.dram_tensor("tri3", [128, 896], BF16, kind="ExternalInput").ap()
    id_d = nc.dram_tensor("id3", [128, 128], BF16, kind="ExternalInput").ap()
    xt = cx.xt
    onesc = cx.onesc
    onesr = cx.onesr
    cvals = cx.cvals
    NH = 16
    NQC = 4
    with ExitStack() as stk:
        def sb(name, shape, dtype):
            return stk.enter_context(nc.sbuf_tensor(name, shape, dtype))

        def sem(name):
            return stk.enter_context(nc.semaphore(name))

        qT = sb("qT", [128, NH * L], BF16)
        kT = sb("kT", [128, 4 * L], BF16)
        vaug = sb("vaug", [128, 4 * 16 * 129], BF16)
        expT = sb("expT", [128, 16 * 512], BF16)
        wqb = [sb(f"wqb{i}", [128, NK * 128], BF16) for i in range(2)]
        vtmp = sb("vtmp", [128, L], BF16)
        anat = sb("anat", [128, 512], BF16)
        rcol = sb("rcol", [128, 4], F32)
        bq_s = sb("bq_s", [128, 16], F32)
        bkv_s = sb("bkv_s", [128, 8], F32)
        pe_s = sb("pe_s", [128, L], BF16)
        tri_s = sb("tri_s", [128, 896], BF16)
        id_s = sb("id_s", [128, 128], BF16)
        psP = stk.enter_context(nc.psum_tensor("psP", [128, 2048], F32))
        psO = stk.enter_context(nc.psum_tensor("psO", [128, 2048], F32))
        psT = psP[:, 1536:1792].bitcast(BF16)
        meanr3 = expT[0:1, 0:4096].bitcast(F32)
        varr3 = expT[0:1, 4096:8192].bitcast(F32)
        tmpr3 = qT[0:1, 0:4096].bitcast(F32)
        sq3 = kT[:, 0:L]
        ld3 = sem("p3ld")
        ldg3 = sem("p3ldg")
        m3p = sem("p3mp")
        e3p = sem("p3ep")
        m3t = sem("p3mt")
        v3a = sem("p3va")
        m3s = sem("p3ms")
        e3x = sem("p3ex")
        v3m = sem("p3vm")
        m3o = sem("p3mo")
        v3r = sem("p3vr")
        s3a = sem("p3sa")
        m3tt = sem("p3mtt")
        v3q = sem("p3vq")
        m3wo = sem("p3mwo")
        e3wo = sem("p3ewo")
        sq3s = sem("p3sqs")
        ln3k = sem("p3lnk")
        m3ln = sem("p3mln")
        st3 = sem("p3st")
        rst3 = sem("p3rst")
        bc3 = sem("p3bc")
        app3 = sem("p3app")

        NU = 24
        sched = []
        pair_first_sc = {}
        pair_first_pc = {}
        sc = 0
        pc = 0
        for h in range(NH):
            for qc in range(NQC):
                pair_first_sc[(h, qc)] = sc
                pair_first_pc[(h, qc)] = pc
                for kt in range(4 * qc + 4):
                    partial = (kt >= 4 * qc)
                    sched.append((h, qc, kt, partial))
                    sc += 1
                    if partial:
                        pc += 1
        NSC = sc
        NPC = pc

        with nc.Block() as blk:
            @blk.sync
            def _(e):
                e.dma_start(out=bq_s[:], in_=bq_d).then_inc(ld3, 16)
                e.dma_start(out=bkv_s[:], in_=bkv_d).then_inc(ld3, 16)
                e.dma_start(out=pe_s[:], in_=pe_d).then_inc(ld3, 16)
                e.dma_start(out=tri_s[:], in_=tri_d).then_inc(ld3, 16)
                e.dma_start(out=id_s[:], in_=id_d).then_inc(ld3, 16)

            @blk.gpsimd
            def _(e):
                for u in range(40):
                    if 2 <= u < 26:
                        e.wait_ge(m3p, 4 * (u - 1))
                    elif u >= 26:
                        e.wait_ge(m3wo, 4 * (u - 24) - 4)
                    if u < 16:
                        src = wq3[u * 128:(u + 1) * 128, :]
                    elif u < 24:
                        src = wkv3[(u - 16) * 128:(u - 15) * 128, :]
                    else:
                        src = wo3[(u - 24) * 128:(u - 23) * 128, :]
                    e.dma_start(out=wqb[u % 2][:], in_=src).then_inc(ldg3, 16)

            @blk.tensor
            def _(e):
                for u in range(NU):
                    e.wait_ge(ldg3, 16 * (u + 1))
                    if u == 0:
                        e.wait_ge(ld3, 80)
                    for lcp in range(2):
                        idx0 = 4 * u + 2 * lcp
                        if idx0 >= 2:
                            e.wait_ge(e3p, idx0)
                        for k in range(NK):
                            for lc in (2 * lcp, 2 * lcp + 1):
                                ins = e.matmul(
                                    psP[:, (lc % 2) * 512:(lc % 2) * 512 + 512],
                                    wqb[u % 2][:, k * 128:(k + 1) * 128],
                                    xt[:, k * L + lc * 512: k * L + (lc + 1) * 512],
                                    start=(k == 0), stop=(k == NK - 1))
                                if k == NK - 1:
                                    ins.then_inc(m3p, 1)
                    if u >= 16 and (u - 16) % 2 == 1:
                        g = (u - 16) // 2
                        e.wait_ge(e3p, 4 * (u + 1))
                        for grp in range(4):
                            idx2 = 4 * g + grp
                            if idx2 >= 1:
                                e.wait_ge(v3a, idx2)
                            for j in range(4):
                                lt = grp * 4 + j
                                ins = e.transpose(
                                    psT[:, j * 128:(j + 1) * 128],
                                    vtmp[:, lt * 128:(lt + 1) * 128],
                                    id_s[:])
                                if j == 3:
                                    ins.then_inc(m3t, 1)
                sc2 = 0
                for h in range(NH):
                    g = h // 4
                    for qc in range(NQC):
                        idx3 = h * NQC + qc
                        for kt in range(4 * qc + 4):
                            partial = (kt >= 4 * qc)
                            if sc2 == 0:
                                e.wait_ge(e3p, 4 * NU)
                                e.wait_ge(m3t, 16)
                            if sc2 >= 3:
                                e.wait_ge(e3x, sc2 - 2)
                            ins = e.matmul(
                                psP[:, (sc2 % 3) * 512:(sc2 % 3) * 512 + 512],
                                kT[:, g * L + kt * 128: g * L + (kt + 1) * 128],
                                qT[:, h * L + qc * 512: h * L + (qc + 1) * 512],
                                start=True, stop=True).then_inc(m3s, 1)
                            sc2 += 1
                        e.wait_ge(e3x, sc2)
                        npc = pair_first_pc[(h, qc)] + 4
                        e.wait_ge(v3m, npc)
                        if idx3 == 0:
                            e.wait_ge(v3a, 16)
                        for qs in range(4):
                            oc = idx3 * 4 + qs
                            if oc - 3 >= 1:
                                e.wait_ge(s3a, oc - 3)
                            nkt = 4 * qc + 4
                            for kt in range(nkt):
                                ins = e.matmul(
                                    psO[:, qs * 512: qs * 512 + 129],
                                    expT[:, kt * 512 + qs * 128: kt * 512 + (qs + 1) * 128],
                                    vaug[:, g * 2064 + kt * 129: g * 2064 + (kt + 1) * 129],
                                    start=(kt == 0), stop=(kt == nkt - 1))
                                if kt == nkt - 1:
                                    ins.then_inc(m3o, 1)
                        e.wait_ge(s3a, 4 * (idx3 + 1))
                        if idx3 >= 1:
                            e.wait_ge(v3q, idx3)
                        for qs in range(4):
                            ins = e.transpose(
                                psT[:, qs * 128:(qs + 1) * 128],
                                anat[:, qs * 128:(qs + 1) * 128],
                                id_s[:])
                            if qs == 3:
                                ins.then_inc(m3tt, 1)
                for u in range(24, 40):
                    m = u - 24
                    e.wait_ge(ldg3, 16 * (u + 1))
                    if u == 24:
                        e.wait_ge(v3q, 64)
                    for lcp in range(2):
                        widx0 = 4 * m + 2 * lcp
                        if widx0 >= 2:
                            e.wait_ge(e3wo, widx0)
                        for k in range(NK):
                            for lc in (2 * lcp, 2 * lcp + 1):
                                ins = e.matmul(
                                    psP[:, (lc % 2) * 512:(lc % 2) * 512 + 512],
                                    wqb[u % 2][:, k * 128:(k + 1) * 128],
                                    qT[:, k * L + lc * 512: k * L + (lc + 1) * 512],
                                    start=(k == 0), stop=(k == NK - 1))
                                if k == NK - 1:
                                    ins.then_inc(m3wo, 1)
                for k in range(NK):
                    e.wait_ge(e3wo, 4 * NK)
                    e.wait_ge(sq3s, k + 1)
                    for lc in range(4):
                        ins1 = e.matmul(psP[0:1, lc * 512:(lc + 1) * 512], onesc[:],
                                        xt[:, k * L + lc * 512: k * L + (lc + 1) * 512],
                                        start=(k == 0), stop=(k == NK - 1))
                        ins2 = e.matmul(psO[0:1, lc * 512:(lc + 1) * 512], onesc[:],
                                        sq3[:, lc * 512:(lc + 1) * 512],
                                        start=(k == 0), stop=(k == NK - 1))
                        if lc == 3:
                            ins2.then_inc(ln3k, 1)
                        if k == NK - 1 and lc == 3:
                            ins1.then_inc(m3ln, 1)
                e.wait_ge(rst3, 1)
                for lc in range(4):
                    e.matmul(psP[:, lc * 512:(lc + 1) * 512], onesr[:],
                             meanr3[:, lc * 512:(lc + 1) * 512], start=True, stop=True)
                    ins = e.matmul(psO[:, lc * 512:(lc + 1) * 512], onesr[:],
                                   varr3[:, lc * 512:(lc + 1) * 512], start=True, stop=True)
                    if lc == 3:
                        ins.then_inc(bc3, 1)

            @blk.scalar
            def _(e):
                sc2 = 0
                for h in range(NH):
                    for qc in range(NQC):
                        idx3 = h * NQC + qc
                        if idx3 >= 1:
                            e.wait_ge(m3o, 4 * idx3)
                        for kt in range(4 * qc + 4):
                            e.wait_ge(m3s, sc2 + 1)
                            e.activation(out=expT[:, kt * 512:(kt + 1) * 512],
                                         in_=psP[:, (sc2 % 3) * 512:(sc2 % 3) * 512 + 512],
                                         func=AF.Exp,
                                         scale=cvals[:, 3:4]).then_inc(e3x, 1)
                            sc2 += 1
                        if idx3 >= 1:
                            e.wait_ge(m3tt, idx3)
                        for qs in range(4):
                            oc = idx3 * 4 + qs
                            e.wait_ge(v3r, oc + 1)
                            e.activation(out=anat[:, qs * 128:(qs + 1) * 128],
                                         in_=psO[:, qs * 512: qs * 512 + 128],
                                         func=AF.Copy,
                                         scale=rcol[:, qs:qs + 1]).then_inc(s3a, 1)
                for k in range(NK):
                    e.wait_ge(e3wo, 4 * NK)
                    if k >= 1:
                        e.wait_ge(ln3k, k)
                    e.activation(out=sq3[:], in_=xt[:, k * L:(k + 1) * L],
                                 func=AF.Square).then_inc(sq3s, 1)
                e.wait_ge(st3, 1)
                e.activation(out=varr3[:], in_=varr3[:], func=AF.Ln, bias=cvals[0:1, 0:1])
                e.activation(out=varr3[:], in_=varr3[:], func=AF.Exp,
                             scale=cvals[0:1, 1:2]).then_inc(rst3, 1)

            @blk.vector
            def _(e):
                e.memset(vaug[:], 1.0)
                for u in range(NU):
                    is_v = (u >= 16 and (u - 16) % 2 == 1)
                    if is_v:
                        g = (u - 16) // 2
                        if g >= 1:
                            e.wait_ge(m3t, 4 * g)
                        for lc in range(4):
                            idx = 4 * u + lc
                            e.wait_ge(m3p, idx + 1)
                            e.tensor_scalar(vtmp[:, lc * 512:(lc + 1) * 512],
                                            psP[:, (lc % 2) * 512:(lc % 2) * 512 + 512],
                                            bkv_s[:, (u - 16):(u - 15)], None,
                                            ALU.add).then_inc(e3p, 1)
                        for grp in range(4):
                            idx2 = 4 * g + grp
                            e.wait_ge(m3t, idx2 + 1)
                            e.tensor_copy(
                                vaug[:, g * 2064 + grp * 4 * 129: g * 2064 + (grp * 4 + 4) * 129]
                                .rearrange("p (t c) -> p t c", t=4)[:, :, 0:128],
                                psT[:, 0:512].rearrange("p (t c) -> p t c", t=4),
                            ).then_inc(v3a, 1)
                        continue
                    for lc in range(4):
                        idx = 4 * u + lc
                        e.wait_ge(m3p, idx + 1)
                        if u < 16:
                            dst = qT[:, u * L + lc * 512: u * L + (lc + 1) * 512]
                            bcol = bq_s[:, u:u + 1]
                        else:
                            g = (u - 16) // 2
                            dst = kT[:, g * L + lc * 512: g * L + (lc + 1) * 512]
                            bcol = bkv_s[:, (u - 16):(u - 15)]
                        e.scalar_tensor_tensor(
                            dst, psP[:, (lc % 2) * 512:(lc % 2) * 512 + 512], bcol,
                            pe_s[:, lc * 512:(lc + 1) * 512],
                            ALU.add, ALU.add).then_inc(e3p, 1)
                sc2 = 0
                pc2 = 0
                for h in range(NH):
                    for qc in range(NQC):
                        idx3 = h * NQC + qc
                        for kt in range(4 * qc + 4):
                            partial = (kt >= 4 * qc)
                            if partial:
                                e.wait_ge(e3x, sc2 + 1)
                                off = 384 - (kt - 4 * qc) * 128
                                e.tensor_tensor(
                                    expT[:, kt * 512:(kt + 1) * 512],
                                    expT[:, kt * 512:(kt + 1) * 512],
                                    tri_s[:, off: off + 512],
                                    ALU.mult).then_inc(v3m, 1)
                                pc2 += 1
                            sc2 += 1
                        for qs in range(4):
                            oc = idx3 * 4 + qs
                            e.wait_ge(m3o, oc + 1)
                            e.reciprocal(rcol[:, qs:qs + 1],
                                         psO[:, qs * 512 + 128: qs * 512 + 129]).then_inc(v3r, 1)
                        e.wait_ge(m3tt, idx3 + 1)
                        e.tensor_copy(qT[:, h * L + qc * 512: h * L + (qc + 1) * 512],
                                      psT[:, 0:512]).then_inc(v3q, 1)
                for m in range(16):
                    for lc in range(4):
                        widx = 4 * m + lc
                        e.wait_ge(m3wo, widx + 1)
                        xsl = xt[:, m * L + lc * 512: m * L + (lc + 1) * 512]
                        e.tensor_tensor(xsl, psP[:, (lc % 2) * 512:(lc % 2) * 512 + 512],
                                        xsl, ALU.add).then_inc(e3wo, 1)
                e.wait_ge(m3ln, 1)
                e.wait_ge(ln3k, 16)
                e.tensor_scalar(meanr3[:], psP[0:1, :], 1.0 / DM, None, ALU.mult)
                e.tensor_scalar(varr3[:], psO[0:1, :], 1.0 / DM, None, ALU.mult)
                e.tensor_tensor(tmpr3[:], meanr3[:], meanr3[:], ALU.mult)
                e.tensor_tensor(varr3[:], varr3[:], tmpr3[:], ALU.subtract).then_inc(st3, 1)
                e.wait_ge(bc3, 1)
                for k in range(NK):
                    xsl = xt[:, k * L:(k + 1) * L]
                    e.tensor_tensor(sq3[:], xsl, psP[:, :], ALU.subtract)
                    e.tensor_tensor(xsl, sq3[:], psO[:, :], ALU.mult).then_inc(app3, 1)
        _barrier(nc, "p3bar")


def prep_p3(inputs):
    t, d = L, 128
    pos = np.arange(t, dtype=np.float32)[:, None]
    i = np.arange(0, d, 2, dtype=np.float32)
    inv_freq = np.exp(-(np.log(10000.0) * i / d))
    ang = pos * inv_freq
    pe = np.zeros((t, d), np.float32)
    pe[:, 0::2] = np.sin(ang)
    pe[:, 1::2] = np.cos(ang)
    peT = np.ascontiguousarray(pe.T).astype(BF)
    tri = np.zeros((128, 896), BF)
    for kk in range(128):
        tri[kk, 384 + kk:] = 1.0
    return {
        "wq3": np.ascontiguousarray(
            inputs["L1_attn_wq"].reshape(NK, 128, 16, 128).transpose(2, 1, 0, 3).reshape(16 * 128, NK * 128)),
        "wkv3": np.ascontiguousarray(
            inputs["L1_attn_wkv"].reshape(NK, 128, 8, 128).transpose(2, 1, 0, 3).reshape(8 * 128, NK * 128)),
        "wo3": np.ascontiguousarray(
            inputs["L1_attn_wo"].reshape(NK, 128, 16, 128).transpose(2, 1, 0, 3).reshape(16 * 128, NK * 128)),
        "bq3": np.ascontiguousarray(inputs["L1_attn_bq"].reshape(16, 128).T),
        "bkv3": np.ascontiguousarray(inputs["L1_attn_bkv"].reshape(8, 128).T),
        "pe3": peT,
        "tri3": tri,
        "id3": np.eye(128, dtype=np.float32).astype(BF),
    }


NJOBS = 136
JPR = NJOBS // 8


def emit_p4a(nc, cx):
    jpk_d = nc.dram_tensor("jpk", [JPR * 128, 256], I32, kind="ExternalInput").ap()
    jsc_d = nc.dram_tensor("jsc", [128, JPR * 16], F16, kind="ExternalInput").ap()
    id4_d = nc.dram_tensor("id4", [128, 128], BF16, kind="ExternalInput").ap()
    scr_my = nc.dram_tensor("scr_my", [JPR * 128, 2048], BF16, kind="Internal")
    scr_all = nc.dram_tensor("scr_all", [NJOBS * 128, 2048], BF16, kind="Internal")
    cx.scr_all = scr_all.ap()
    with ExitStack() as stk:
        def sb(name, shape, dtype):
            return stk.enter_context(nc.sbuf_tensor(name, shape, dtype))

        def sem(name):
            return stk.enter_context(nc.semaphore(name))

        pk = [sb(f"pk{i}", [128, 512], U16) for i in range(2)]
        nib = sb("nib", [128, 2048], U16)
        tt1 = sb("tt1", [128, 2048], U16)
        tt2 = sb("tt2", [128, 2048], U16)
        bits = sb("bits", [128, 2048], U16)
        m01 = sb("m01", [128, 2048], U16)
        spc = sb("spc", [128, 2048], U16)
        vbits = sb("vbits", [128, 2048], U16)
        wbf = sb("wbf", [128, 2048], BF16)
        wout = [sb(f"wout{i}", [128, 2048], BF16) for i in range(2)]
        jsc_s = sb("jsc_s", [128, JPR * 16], F32)
        jsch = sb("jsch", [128, JPR * 16], F16)
        id4_s = sb("id4_s", [128, 128], BF16)
        psX = stk.enter_context(nc.psum_tensor("psX", [128, 2048], F32))
        s4l = sem("p4sl")
        v4x = sem("p4vx")
        v4d = sem("p4vd")
        m4t = sem("p4mt")
        v4e = sem("p4ve")
        w4o = sem("p4wo")
        c4l = sem("p4cl")
        vsc = sem("p4vsc")

        with nc.Block() as blk:
            @blk.sync
            def _(e):
                e.dma_start(out=jsch[:], in_=jsc_d).then_inc(c4l, 16)
                e.dma_start(out=id4_s[:], in_=id4_d).then_inc(c4l, 16)
                for j in range(JPR):
                    if j >= 2:
                        e.wait_ge(v4x, j - 1)
                    e.dma_start(out=pk[j % 2][:].bitcast(I32),
                                in_=jpk_d[j * 128:(j + 1) * 128, :]).then_inc(s4l, 16)

            @blk.scalar
            def _(e):
                for j in range(JPR):
                    e.wait_ge(v4e, 2 * j + 2)
                    e.dma_start(out=scr_my.ap()[j * 128:(j + 1) * 128, :],
                                in_=wout[j % 2][:]).then_inc(w4o, 16)
                e.wait_ge(w4o, 16 * JPR)

            @blk.vector
            def _(e):
                e.wait_ge(c4l, 32)
                e.tensor_copy(jsc_s[:], jsch[:]).then_inc(vsc, 1)
                for j in range(JPR):
                    nd = 2048
                    e.wait_ge(s4l, 16 * (j + 1))
                    for jj in range(4):
                        ins = e.tensor_scalar(nib[:, jj:nd:4], pk[j % 2][:],
                                              4 * jj, 15, ALU.logical_shift_right,
                                              ALU.bitwise_and)
                        if jj == 3:
                            ins.then_inc(v4x, 1)
                    e.tensor_scalar(tt1[:], nib[:], 8, 12,
                                    ALU.bitwise_and, ALU.logical_shift_left)
                    e.tensor_scalar(tt2[:], nib[:], 7, 6,
                                    ALU.bitwise_and, ALU.logical_shift_left)
                    e.tensor_tensor(bits[:], tt1[:], tt2[:], ALU.add)
                    e.tensor_scalar(bits[:], bits[:], 16128, None, ALU.add)
                    e.tensor_scalar(m01[:], nib[:], 6, None, ALU.bitwise_and)
                    e.tensor_scalar(m01[:], m01[:], 0, None, ALU.is_equal)
                    e.tensor_scalar(spc[:], nib[:], 1, None, ALU.bitwise_and)
                    e.tensor_scalar(spc[:], spc[:], 0x3F00, None, ALU.mult)
                    e.tensor_tensor(spc[:], spc[:], tt1[:], ALU.add)
                    if j >= 1:
                        e.wait_ge(m4t, 2 * j)
                    e.select(vbits[:], m01[:], spc[:], bits[:])
                    for gidx in range(16):
                        scol = jsc_s[:, j * 16 + gidx: j * 16 + gidx + 1]
                        ins = e.tensor_scalar(wbf[:, gidx * 128:(gidx + 1) * 128],
                                              vbits[:, gidx * 128:(gidx + 1) * 128].bitcast(BF16),
                                              scol, None, ALU.mult)
                        if gidx == 15:
                            ins.then_inc(v4d, 1)
                    for grp in range(2):
                        e.wait_ge(m4t, 2 * j + grp + 1)
                        if j >= 2:
                            e.wait_ge(w4o, 16 * (j - 1))
                        e.tensor_copy(
                            wout[j % 2][:, grp * 1024: grp * 1024 + 1024],
                            psX[:, grp * 512: grp * 512 + 512].bitcast(BF16),
                        ).then_inc(v4e, 1)

            @blk.tensor
            def _(e):
                e.wait_ge(c4l, 32)
                for j in range(JPR):
                    e.wait_ge(v4d, j + 1)
                    for grp in range(2):
                        if j >= 1:
                            e.wait_ge(v4e, 2 * j - 1 + grp)
                        for i in range(8):
                            gidx = grp * 8 + i
                            ins = e.transpose(
                                psX[:, grp * 512: grp * 512 + 512].bitcast(BF16)[:, i * 128:(i + 1) * 128],
                                wbf[:, gidx * 128:(gidx + 1) * 128],
                                id4_s[:])
                            if i == 7:
                                ins.then_inc(m4t, 1)

            @blk.gpsimd
            def _(e):
                e.wait_ge(w4o, 16 * JPR)
                e.collective_compute(
                    "AllGather",
                    ALU.bypass,
                    replica_groups=[list(range(8))],
                    ins=[scr_my.ap().opt()],
                    outs=[scr_all.ap().opt()],
                ).then_inc(cx.ccS, 1)
        _barrier(nc, "p4abar")


def prep_p4a(inputs):
    gup = inputs["L1_ffn_gate_up_packed"]
    dnp = inputs["L1_ffn_down_packed"]
    sgu = inputs["L1_ffn_gate_up_scales"]
    sdn = inputs["L1_ffn_down_scales"]
    jobs_pk = []
    jobs_sc = []
    for t in range(88):
        jobs_pk.append(gup[t * 128:(t + 1) * 128, :])
        jobs_sc.append(np.ascontiguousarray(sgu[:, t * 128:(t + 1) * 128]).T)
    for t in range(16):
        for c in range(3):
            blkp = np.zeros((128, 256), np.int32)
            w32 = min(256, 704 - c * 256)
            blkp[:, 0:w32] = dnp[t * 128:(t + 1) * 128, c * 256: c * 256 + w32]
            jobs_pk.append(blkp)
            blks = np.zeros((128, 16), np.float16)
            ngc = min(16, 44 - c * 16)
            blks[:, 0:ngc] = sdn[c * 16: c * 16 + ngc, t * 128:(t + 1) * 128].T
            jobs_sc.append(blks)
    per_core_pk = []
    per_core_sc = []
    for r in range(8):
        per_core_pk.append(np.ascontiguousarray(
            np.concatenate(jobs_pk[r * JPR:(r + 1) * JPR], axis=0)))
        per_core_sc.append(np.ascontiguousarray(
            np.concatenate(jobs_sc[r * JPR:(r + 1) * JPR], axis=1).astype(np.float16)))
    return {
        "jpk": per_core_pk,
        "jsc": per_core_sc,
        "id4": np.eye(128, dtype=np.float32).astype(BF),
    }


LB4 = 512
NB4 = L // LB4


def emit_p4b(nc, cx):
    scr_all = cx.scr_all
    xt = cx.xt
    onesc = cx.onesc
    onesr = cx.onesr
    cvals = cx.cvals
    with ExitStack() as stk:
        def sb(name, shape, dtype):
            return stk.enter_context(nc.sbuf_tensor(name, shape, dtype))

        def sem(name):
            return stk.enter_context(nc.semaphore(name))

        hact = sb("hact4", [128, NT * LB4], BF16)
        yblk = sb("yblk", [128, NK * LB4], BF16)
        g4 = [sb(f"g4{i}", [128, NK * 128], BF16) for i in range(2)]
        u4 = [sb(f"u4{i}", [128, NK * 128], BF16) for i in range(2)]
        dn4 = [sb(f"dn4{i}", [128, NT * 128], BF16) for i in range(2)]
        gact = sb("gact4", [128, LB4], BF16)
        uact = sb("uact4", [128, LB4], BF16)
        sq4 = sb("sq4", [128, LB4], BF16)
        tmp4 = sb("tmp4", [128, LB4], BF16)
        rmsrow = sb("rmsrow", [1, LB4], F32)
        meanr4 = sb("meanr4", [1, LB4], F32)
        varr4 = sb("varr4", [1, LB4], F32)
        tmpr4 = sb("tmpr4", [1, LB4], F32)
        psA4 = stk.enter_context(nc.psum_tensor("psA4", [128, 2048], F32))
        psB4 = stk.enter_context(nc.psum_tensor("psB4", [128, 2048], F32))
        l4g = sem("p4lg")
        m4 = sem("p4m")
        s4 = sem("p4s")
        e4 = sem("p4e")
        l4d = sem("p4ld2")
        m4d = sem("p4md")
        ev4d = sem("p4evd")
        sqy4s = sem("p4sqy")
        mr4k = sem("p4mrk")
        vr4 = sem("p4vr")
        rst4r = sem("p4rstr")
        bcr4 = sem("p4bcr")
        appr4 = sem("p4appr")
        sq4s = sem("p4sqs")
        ln4k = sem("p4lnk")
        m4ln = sem("p4mln")
        st4 = sem("p4st")
        rst4 = sem("p4rst")
        bc4 = sem("p4bc")
        app4 = sem("p4app")

        with nc.Block() as blk:
            @blk.sync
            def _(e):
                e.wait_ge(cx.ccS, 1)
                for B in range(NB4):
                    for t in range(NT):
                        g = B * NT + t
                        if g >= 2:
                            e.wait_ge(m4, 2 * (g - 2) + 2)
                        e.dma_start(out=g4[g % 2][:],
                                    in_=scr_all[t * 128:(t + 1) * 128, :]).then_inc(l4g, 16)
                        e.dma_start(out=u4[g % 2][:],
                                    in_=scr_all[(t + 44) * 128:(t + 45) * 128, :]).then_inc(l4g, 16)
                    for k in range(NK):
                        kg = B * NK + k
                        if kg >= 2:
                            e.wait_ge(m4d, kg - 1)
                        for cc in range(3):
                            jj = 88 + 3 * k + cc
                            wcols = 1536 if cc == 2 else 2048
                            e.dma_start(
                                out=dn4[k % 2][:, cc * 2048: cc * 2048 + wcols],
                                in_=scr_all[jj * 128:(jj + 1) * 128, 0:wcols]).then_inc(l4d, 16)

            @blk.tensor
            def _(e):
                for B in range(NB4):
                    for t in range(NT):
                        g = B * NT + t
                        e.wait_ge(l4g, 32 * (g + 1))
                        if g >= 2:
                            e.wait_ge(s4, 2 * (g - 2) + 1)
                        if t < 2 and B >= 1:
                            e.wait_ge(app4, B * NK)
                        for k in range(NK):
                            ins = e.matmul(
                                psA4[:, (g % 2) * 512:(g % 2) * 512 + 512],
                                g4[g % 2][:, k * 128:(k + 1) * 128],
                                xt[:, k * L + B * LB4: k * L + (B + 1) * LB4],
                                start=(k == 0), stop=(k == NK - 1))
                            if k == NK - 1:
                                ins.then_inc(m4, 1)
                        if g >= 2:
                            e.wait_ge(s4, 2 * (g - 2) + 2)
                        for k in range(NK):
                            ins = e.matmul(
                                psA4[:, 1024 + (g % 2) * 512: 1024 + (g % 2) * 512 + 512],
                                u4[g % 2][:, k * 128:(k + 1) * 128],
                                xt[:, k * L + B * LB4: k * L + (B + 1) * LB4],
                                start=(k == 0), stop=(k == NK - 1))
                            if k == NK - 1:
                                ins.then_inc(m4, 1)
                    for k in range(NK):
                        kg = B * NK + k
                        e.wait_ge(l4d, 48 * (kg + 1))
                        if k == 0:
                            e.wait_ge(e4, (B + 1) * NT)
                        if kg >= 2:
                            e.wait_ge(ev4d, kg - 1)
                        for kk in range(NT):
                            ins = e.matmul(
                                psB4[:, (k % 2) * 512:(k % 2) * 512 + 512],
                                dn4[k % 2][:, kk * 128:(kk + 1) * 128],
                                hact[:, kk * LB4:(kk + 1) * LB4],
                                start=(kk == 0), stop=(kk == NT - 1))
                            if kk == NT - 1:
                                ins.then_inc(m4d, 1)
                    for k in range(NK):
                        e.wait_ge(sqy4s, B * NK + k + 1)
                        ins = e.matmul(psB4[0:1, 1024:1536], onesc[:], sq4[:],
                                       start=(k == 0), stop=(k == NK - 1))
                        ins.then_inc(mr4k, 1)
                    e.wait_ge(rst4r, B + 1)
                    e.matmul(psB4[:, 1024:1536], onesr[:], rmsrow[:],
                             start=True, stop=True).then_inc(bcr4, 1)
                    for k in range(NK):
                        e.wait_ge(appr4, B * NK + k + 1)
                        e.wait_ge(sq4s, B * NK + k + 1)
                        ins1 = e.matmul(psB4[0:1, 1536:2048], onesc[:],
                                        xt[:, k * L + B * LB4: k * L + (B + 1) * LB4],
                                        start=(k == 0), stop=(k == NK - 1))
                        ins2 = e.matmul(psA4[0:1, 0:512], onesc[:], sq4[:],
                                        start=(k == 0), stop=(k == NK - 1))
                        ins2.then_inc(ln4k, 1)
                        if k == NK - 1:
                            ins1.then_inc(m4ln, 1)
                    e.wait_ge(rst4, B + 1)
                    e.matmul(psA4[:, 0:512], onesr[:], meanr4[:],
                             start=True, stop=True)
                    e.matmul(psA4[:, 512:1024], onesr[:], varr4[:],
                             start=True, stop=True).then_inc(bc4, 1)

            @blk.scalar
            def _(e):
                for B in range(NB4):
                    for t in range(NT):
                        g = B * NT + t
                        e.wait_ge(m4, 2 * g + 1)
                        if g >= 1:
                            e.wait_ge(e4, g)
                        e.activation(out=gact[:], in_=psA4[:, (g % 2) * 512:(g % 2) * 512 + 512],
                                     func=AF.Silu).then_inc(s4, 1)
                        e.wait_ge(m4, 2 * g + 2)
                        e.activation(out=uact[:], in_=psA4[:, 1024 + (g % 2) * 512: 1024 + (g % 2) * 512 + 512],
                                     func=AF.Copy).then_inc(s4, 1)
                    for k in range(NK):
                        e.wait_ge(ev4d, B * NK + k + 1)
                        if B * NK + k >= 1:
                            e.wait_ge(mr4k, B * NK + k)
                        e.activation(out=sq4[:], in_=yblk[:, k * LB4:(k + 1) * LB4],
                                     func=AF.Square).then_inc(sqy4s, 1)
                    e.wait_ge(vr4, B + 1)
                    e.activation(out=rmsrow[:], in_=rmsrow[:], func=AF.Ln, bias=cvals[0:1, 2:3])
                    e.activation(out=rmsrow[:], in_=rmsrow[:], func=AF.Exp,
                                 scale=cvals[0:1, 1:2]).then_inc(rst4r, 1)
                    for k in range(NK):
                        e.wait_ge(appr4, B * NK + k + 1)
                        if B * NK + k >= 1:
                            e.wait_ge(ln4k, B * NK + k)
                        e.activation(out=sq4[:], in_=xt[:, k * L + B * LB4: k * L + (B + 1) * LB4],
                                     func=AF.Square).then_inc(sq4s, 1)
                    e.wait_ge(st4, B + 1)
                    e.activation(out=varr4[:], in_=varr4[:], func=AF.Ln, bias=cvals[0:1, 0:1])
                    e.activation(out=varr4[:], in_=varr4[:], func=AF.Exp,
                                 scale=cvals[0:1, 1:2]).then_inc(rst4, 1)

            @blk.vector
            def _(e):
                for B in range(NB4):
                    for t in range(NT):
                        g = B * NT + t
                        e.wait_ge(s4, 2 * g + 2)
                        e.tensor_tensor(hact[:, t * LB4:(t + 1) * LB4], gact[:], uact[:],
                                        ALU.mult).then_inc(e4, 1)
                    for k in range(NK):
                        kg = B * NK + k
                        e.wait_ge(m4d, kg + 1)
                        e.tensor_copy(yblk[:, k * LB4:(k + 1) * LB4],
                                      psB4[:, (k % 2) * 512:(k % 2) * 512 + 512]).then_inc(ev4d, 1)
                    e.wait_ge(mr4k, (B + 1) * NK)
                    e.tensor_scalar(rmsrow[:], psB4[0:1, 1024:1536], 1.0 / DM, None,
                                    ALU.mult).then_inc(vr4, 1)
                    e.wait_ge(bcr4, B + 1)
                    for k in range(NK):
                        xsl = xt[:, k * L + B * LB4: k * L + (B + 1) * LB4]
                        e.tensor_tensor(tmp4[:], yblk[:, k * LB4:(k + 1) * LB4],
                                        psB4[:, 1024:1536], ALU.mult)
                        e.tensor_tensor(xsl, xsl, tmp4[:], ALU.add).then_inc(appr4, 1)
                    e.wait_ge(m4ln, B + 1)
                    e.wait_ge(ln4k, (B + 1) * NK)
                    e.tensor_scalar(meanr4[:], psB4[0:1, 1536:2048], 1.0 / DM, None, ALU.mult)
                    e.tensor_scalar(varr4[:], psA4[0:1, 0:512], 1.0 / DM, None, ALU.mult)
                    e.tensor_tensor(tmpr4[:], meanr4[:], meanr4[:], ALU.mult)
                    e.tensor_tensor(varr4[:], varr4[:], tmpr4[:], ALU.subtract).then_inc(st4, 1)
                    e.wait_ge(bc4, B + 1)
                    for k in range(NK):
                        xsl = xt[:, k * L + B * LB4: k * L + (B + 1) * LB4]
                        e.tensor_tensor(sq4[:], xsl, psA4[:, 0:512], ALU.subtract)
                        e.tensor_tensor(xsl, sq4[:], psA4[:, 512:1024], ALU.mult).then_inc(app4, 1)

            @blk.gpsimd
            def _(e):
                for B in range(NB4):
                    for k in range(NK):
                        e.wait_ge(app4, B * NK + k + 1)
                        e.dma_start(
                            out=cx.xt_out[:, k * L + B * LB4: k * L + (B + 1) * LB4],
                            in_=xt[:, k * L + B * LB4: k * L + (B + 1) * LB4],
                        ).then_inc(cx.outS, 16)
        _barrier(nc, "p4bbar")


def prep_p4(inputs):
    sgu = inputs["L1_ffn_gate_up_scales"]
    sdn = inputs["L1_ffn_down_scales"]
    sgu_h = np.ascontiguousarray(
        sgu.T.reshape(88, 128, 16).transpose(1, 0, 2).reshape(128, 88 * 16))
    sdn_h = np.ascontiguousarray(
        sdn.T.reshape(16, 128, 44).transpose(1, 0, 2).reshape(128, 16 * 44))
    return {
        "gup": np.ascontiguousarray(inputs["L1_ffn_gate_up_packed"]),
        "dnp": np.ascontiguousarray(inputs["L1_ffn_down_packed"]),
        "sgu": sgu_h,
        "sdn": sdn_h,
        "id4": np.eye(128, dtype=np.float32).astype(BF),
    }


def prep_p4b(inputs):
    return {}


PHASE_EMIT = {"p1": emit_p1, "p2": emit_p2, "p3": emit_p3,
              "p4a": emit_p4a, "p4b": emit_p4b}
PHASE_PREP = {"p1": prep_p1, "p2": prep_p2, "p3": prep_p3,
              "p4a": prep_p4a, "p4b": prep_p4b}

PLAN = [("p4a", "p1", "p2", "p3", "p4b")]


def build_launch(phases):
    nc = bass.Bass("TRN2", target_bir_lowering=False, debug=False)
    nc._stk = ExitStack()
    xt_in = nc.dram_tensor("xt_in", [128, NK * L], F32, kind="ExternalInput").ap()
    onesc_d = nc.dram_tensor("onesc", [128, 1], BF16, kind="ExternalInput").ap()
    onesr_d = nc.dram_tensor("onesr", [1, 128], F32, kind="ExternalInput").ap()
    xt_out = nc.dram_tensor("xt_out", [128, NK * L], F32, kind="ExternalOutput").ap()
    with nc._stk:
        cx = Ctx()
        cx.xt = nc._stk.enter_context(nc.sbuf_tensor("xt", [128, NK * L], BF16))
        cx.onesc = nc._stk.enter_context(nc.sbuf_tensor("onesc_sb", [128, 1], BF16))
        cx.onesr = nc._stk.enter_context(nc.sbuf_tensor("onesr_sb", [1, 128], F32))
        cx.cvals = nc._stk.enter_context(nc.sbuf_tensor("cvals", [128, 4], F32))
        cx.ccS = nc._stk.enter_context(nc.semaphore("ccS"))
        cx.xt_out = xt_out
        cx.outS = nc._stk.enter_context(nc.semaphore("outSp"))
        cx.streamed_out = False
        hld = nc._stk.enter_context(nc.semaphore("hld"))
        hldg = nc._stk.enter_context(nc.semaphore("hldg"))
        cx.hldg = hldg
        with nc.Block() as blk:
            @blk.sync
            def _(e):
                e.dma_start(out=cx.onesc[:], in_=onesc_d).then_inc(hld, 16)
                e.dma_start(out=cx.onesr[:], in_=onesr_d).then_inc(hld, 16)
                e.wait_ge(hld, 32)

            @blk.gpsimd
            def _(e):
                for j in range(4):
                    e.dma_start(
                        out=cx.xt[:, j * 4 * L:(j + 1) * 4 * L],
                        in_=xt_in[:, j * 4 * L:(j + 1) * 4 * L],
                    ).then_inc(hldg, 16)
                e.wait_ge(hld, 32)

            @blk.vector
            def _(e):
                e.memset(cx.cvals[:, 0:1], 1e-5)
                e.memset(cx.cvals[:, 1:2], -0.5)
                e.memset(cx.cvals[:, 2:3], 1e-6)
                e.memset(cx.cvals[:, 3:4], 0.08838834764831845)
        _barrier(nc, "hbar")
        for ph in phases:
            PHASE_EMIT[ph](nc, cx)
            if ph == "p4b":
                cx.streamed_out = True
        with nc.Block() as blk:
            @blk.gpsimd
            def _(e):
                if cx.streamed_out:
                    e.wait_ge(cx.outS, 16 * NB4 * NK)
                else:
                    for k in range(NK):
                        e.dma_start(out=xt_out[:, k * L:(k + 1) * L],
                                    in_=cx.xt[:, k * L:(k + 1) * L]).then_inc(cx.outS, 16)
                    e.wait_ge(cx.outS, 16 * NK)
    return nc


_BUILD_CACHE = {}


def kernel(**inputs):
    x = inputs["L0_attn_x"]
    out = np.empty((8, 2048, 2048), np.float32)
    def tile_xt(a):
        return np.ascontiguousarray(a.T.reshape(NK, 128, L).transpose(1, 0, 2).reshape(128, NK * L))

    def untile_xt(a):
        return a.reshape(128, NK, L).transpose(1, 0, 2).reshape(DM, L)

    cur = [tile_xt(x[b]) for b in range(8)]
    consts = {"onesc": np.ones((128, 1), BF), "onesr": np.ones((1, 128), np.float32)}
    for phases in PLAN:
        if phases not in _BUILD_CACHE:
            _BUILD_CACHE[phases] = build_launch(phases)
        nc = _BUILD_CACHE[phases]
        common = dict(consts)
        for ph in phases:
            common.update(PHASE_PREP[ph](inputs))
        in_maps = []
        for b in range(8):
            m = {}
            for kk, vv in common.items():
                m[kk] = vv[b] if isinstance(vv, list) else vv
            m["xt_in"] = cur[b]
            in_maps.append(m)
        res = run_bass_kernel_spmd(nc, in_maps, core_ids=list(range(8)))
        cur = [res.results[b]["xt_out"] for b in range(8)]
    for b in range(8):
        out[b] = untile_xt(cur[b]).T
    return out
